# version 04
# baseline (speedup 1.0000x reference)
import numpy as np
import concourse.bass as bass
import concourse.mybir as mybir
from concourse.bass_utils import run_bass_kernel_spmd
from contextlib import ExitStack

F32 = mybir.dt.float32
BF16 = mybir.dt.bfloat16
AF = mybir.ActivationFunctionType
ALU = mybir.AluOpType
AX = mybir.AxisListType

D = 1024
NH = 16
DKH = 64
DVH = 128
DQK = 1024
DVAL = 2048
FF = 2752
E0, H0 = 8, 2048
E1, RH = 16, 1024
SH = 4096
NCORE = 8
EPS_LN = 1e-5
EPS_RMS = 1e-6
EPS_L2 = 1e-6
EPS_DEN = 1e-6
SCALE_DK = DKH ** -0.5
DCH = 8
VCH = 16
FFCH = 22
FF_LAST = FF - 21 * 128
SHCH = SH // 128
RHCH = RH // 128


class Cfg:
    def __init__(self, tpc=2048):
        assert tpc % 512 == 0
        self.tpc = tpc
        self.nblk = tpc // 512
        self.nch = tpc // 128
        self.ntile = tpc // 128


class Sem:
    __slots__ = ("h", "n")

    def __init__(self, h):
        self.h = h
        self.n = 0


class B:
    ENGS = ("sync", "tensor", "vector", "scalar", "gpsimd")

    def __init__(self, nc, cfg):
        self.nc = nc
        self.cfg = cfg
        self.ops = {e: [] for e in self.ENGS}
        self.ctx = ExitStack()

    def sem(self, name):
        return Sem(self.ctx.enter_context(self.nc.semaphore(name)))

    def sb(self, name, shape, dtype):
        return self.nc.alloc_sbuf_tensor(name, shape, dtype)

    def ps(self, name, shape, dtype=F32):
        return self.nc.alloc_psum_tensor(name, shape, dtype)

    def E(self, eng, fn):
        self.ops[eng].append(fn)

    def I(self, eng, fn, sem, k=1):
        sem.n += k

        def wrapped(e, fn=fn, h=sem.h, k=k):
            fn(e).then_inc(h, k)
        self.ops[eng].append(wrapped)

    def W(self, eng, sem, n=None):
        n = sem.n if n is None else n
        self.ops[eng].append(lambda e, h=sem.h, n=n: e.wait_ge(h, n))

    def D(self, eng):
        self.ops[eng].append(lambda e: e.drain())

    def F(self, eng, sem, k=1):
        sem.n += k
        self.ops[eng].append(lambda e, h=sem.h, k=k: e.drain().then_inc(h, k))

    def finalize(self, block):
        for name in self.ENGS:
            fns = self.ops[name]

            def runner(e, fns=fns):
                for f in fns:
                    f(e)
            getattr(block, name)(runner)


class WQ:

    def __init__(self, b, name, nrot=6):
        self.b = b
        self.sems = [b.sem(f"{name}{i}") for i in range(nrot)]
        self.i = 0

    def load(self, dst_ap, src_ap, pre_wait=None):
        s = self.sems[self.i % len(self.sems)]
        self.i += 1
        bb = self.b
        if pre_wait is not None:
            bb.W("gpsimd", pre_wait[0], pre_wait[1])
        bb.W("gpsimd", s, s.n)
        s.n += 16

        def fn(g, dst=dst_ap, src=src_ap, h=s.h):
            g.dma_start(out=dst, in_=src).then_inc(h, 16)
        bb.E("gpsimd", fn)
        return (s, s.n)


class WS:

    def __init__(self, K, views):
        self.K = K
        self.bufs = views
        self.free = [None] * len(views)
        self.i = 0

    def get(self, src_ap):
        j = self.i % len(self.bufs)
        self.i += 1
        tok = self.K.wq.load(self.bufs[j], src_ap, pre_wait=self.free[j])
        return j, self.bufs[j], tok

    def release(self, j):
        K = self.K
        K.b.F("tensor", K.s_t)
        self.free[j] = (K.s_t, K.s_t.n)

    def barrier(self, tok):
        self.free = [tok] * len(self.bufs)


def build(cfg: Cfg, debug=()):
    nc = bass.Bass("TRN2", target_bir_lowering=False)
    tpc = cfg.tpc
    dr = {}

    def din(name, shape, dtype=F32):
        dr[name] = nc.dram_tensor(name, shape, dtype, kind="ExternalInput")
        return dr[name]

    din("x", [tpc, D])
    din("xp", [4, D])
    din("cmask", [2, NCORE])
    din("w_norm", [128, DCH])
    din("w_pnw", [128, DCH])
    din("w_q", [128, DCH * DCH * 128])
    din("w_k", [128, DCH * DCH * 128])
    din("w_v", [128, VCH * DCH * 128])
    din("w_ba", [128, DCH * 128])
    din("w_conv", [128, 128])
    din("w_Alog", [NH, 1])
    din("w_dtb", [NH, 1])
    din("w_g", [128, VCH * DCH * 128])
    din("w_onw", [128, 1])
    din("w_o", [128, DCH * VCH * 128])
    din("w_gate", [128, FFCH * DCH * 128])
    din("w_up", [128, FFCH * DCH * 128])
    din("w_down", [128, DCH * FFCH * 128])
    din("w_rt0", [128, DCH * 128])
    din("w_m1", [E0, 128, (H0 // 128) * DCH * 128])
    din("w_m2", [E0, 128, (H0 // 128) * DCH * 128])
    din("w_m3", [E0, 128, DCH * (H0 // 128) * 128])
    din("w_wq", [128, (NH // 2) * DKH])
    din("w_wk", [128, (NH // 2) * DKH])
    din("w_wv", [128, (NH // 2) * DKH])
    din("w_wo", [128, DCH * DCH * 128])
    din("w_rt1", [128, DCH * 128])
    din("w_eb", [1, E1])
    din("w_s1", [128, SHCH * DCH * 128])
    din("w_s2", [128, SHCH * DCH * 128])
    din("w_s3", [128, DCH * SHCH * 128])
    din("w_r1", [E1, 128, RHCH * DCH * 128])
    din("w_r2", [E1, 128, RHCH * DCH * 128])
    din("w_r3", [E1, 128, DCH * RHCH * 128])

    y = nc.dram_tensor("y", [tpc, D], F32, kind="ExternalOutput")
    oloc_d = nc.dram_tensor("oloc_d", [cfg.nch, 128, NH * 128], BF16)
    g_d = nc.dram_tensor("g_d", [cfg.nch, 128, 8 * 128], BF16)
    cc0_in = nc.dram_tensor("cc0_in", [NH * DKH, DKH + DVH], F32)
    cc0_out = nc.dram_tensor("cc0_out", [NCORE * NH * DKH, DKH + DVH], F32)
    cc1_in = nc.dram_tensor("cc1_in", [128, 8 * 64 + 8], F32)
    cc1_out = nc.dram_tensor("cc1_out", [NCORE * 128, 8 * 64 + 8], F32)

    dbg = set(debug)

    b = B(nc, cfg)
    with b.ctx:
        K = Kern(nc, b, cfg, dr, y, oloc_d, g_d, (cc0_in, cc0_out, cc1_in, cc1_out), dbg)
        K.run()
        with nc.Block() as block:
            b.finalize(block)
    return nc


class Kern:
    def __init__(self, nc, b, cfg, dr, y, oloc_d, g_d, ccs, dbg):
        self.nc, self.b, self.cfg, self.dr = nc, b, cfg, dr
        self.y, self.oloc_d, self.g_d, self.ccs, self.dbg = y, oloc_d, g_d, ccs, dbg
        tpc = cfg.tpc
        self.xres = b.sb("xres", [128, DCH * tpc], F32)
        self.arsl = b.sb("arsl", [128, tpc], F32)
        self.ar16 = b.sb("ar16", [128, tpc], F32)
        self.ARN = b.sb("ARN", [128, 45312], BF16)
        A = self.ARN
        self.hb = A[:, 0:4096]
        self.qb = A[:, 4096:8192]
        self.kb = A[:, 8192:12288]
        self.vb = A[:, 12288:20480]
        self.wp = A[:, 20480:28672]
        self.obc = A[:, 28672:30720]
        self.gst = A[:, 30720:32768]
        self.sel16 = A[0:16, 32768:33792]
        self.oneh16 = A[0:16, 33792:35840]
        self.PT = A[:, 35840:36608]
        self.Pti = A[:, 36608:36736]
        self.RHS = A[:, 36736:36928]
        self.Ybuf = A[:, 36928:37312]
        self.Mt_bf = A[:, 37312:37440]
        self.Mti_bf = A[:, 37440:37568]
        self.At_bf = A[:, 37568:37696]
        self.Wf2 = A[:, 37696:37824]
        self.Qt2 = A[:, 37824:37952]
        self.Kb2 = A[:, 37952:38080]
        self.Z2 = A[:, 38080:38208]
        self.Kbt = A[:, 38208:38272]
        self.AchT = A[:, 38272:38336]
        self.bexp = A[0:16, 38336:38464]
        self.eg_b = A[0:16, 38464:38592]
        self.edec = A[0:16, 38592:38720]
        self.gbf = A[0:16, 38720:38848]
        self.bbf = A[0:16, 38848:38976]
        self.Ar_bf = A[:, 38976:39488]
        self.h3fm = A[:, 39488:39520]
        self.xnb = A[:, 39552:43648]
        self.AtTb = A[:, 4096:4608]
        self.wkv = A[:, 20480:21504]
        self.onescb = A[:, 21504:21505]
        self.kvzb = A[:, 44544:45064]
        self.zbd = A[:, 45064:45080]
        self.ARNF = b.sb("ARNF", [128, 4096], F32)
        AF_ = self.ARNF
        self.xtm = [AF_[:, 0:1024], AF_[:, 1024:2048]]
        self.xp_t = self.xtm[0][0:4, :]
        self.sqb = AF_[:, 2048:2560]
        self.S_f = AF_[:, 0:1024]
        self.Ar_f = AF_[:, 1024:1536]
        self.Atmp = AF_[:, 1536:2048]
        self.AtT = self.Atmp
        self.AID = AF_[:, 2560:3072]
        self.Btmp = AF_[:, 3072:4096]
        self.kvz = AF_[:, 2048:2568]
        self.kvtmp = AF_[:, 3072:3592]
        wfc = [0]

        def vf(cols, parts=128, dtype=F32):
            a = wfc[0]
            wfc[0] += cols
            return self._WSF[0:parts, a:a + cols]
        self._WSF = b.sb("WSF", [128, 5248], F32)
        self.identf = vf(128)
        self.mask_s = vf(128)
        self.mask_i1 = vf(128)
        self.onesc = vf(1)
        self.onesr = vf(128, 1)
        self.onesall = vf(128)
        self.statall = vf(2 * cfg.ntile)
        self.cstt = vf(16)
        self._cst = {}
        self.gtm = vf(16)
        self.ngtm = vf(16)
        self.gfm = vf(128, 16)
        self.egfm = vf(128, 16)
        self.btm = vf(16)
        self.egCc = vf(1)
        self.Dsb = vf(128)
        self.tmpA = vf(128)
        self.tmpB = vf(128)
        self.o1sb = vf(128)
        self.mksel = vf(128, 2)
        self.nksel = vf(128, 2)
        self.mubc = vf(NCORE)
        self.nubc = vf(NCORE)
        self.otile = vf(128)
        self.onrow = vf(128, 1)
        self.cs16 = vf(1, 16)
        self.ebb = vf(16)
        self.ebrow = vf(16, 1)
        self.mx8 = vf(8)
        self.rtm = vf(32)
        self.rw1 = vf(16)
        self.nw = vf(DCH)
        self.pnw = vf(DCH)
        self.convw = vf(128)
        self.dtb = vf(1, 16)
        self.negal = vf(1, 16)
        self.onw = vf(1)
        self.cmk = vf(NCORE, 2)
        self.ones2 = vf(2)
        self.ones2sc = vf(2)
        self.ones2T = vf(128, 2)
        self.ones2Tsc = vf(128, 2)
        self.cext = vf(516)
        self.ctails = vf(32 * 3)
        self.sqtmp = vf(512)
        self.l2row = vf(512, 2)
        self.xp_s = vf(4, 4)
        self.rt_sb = vf(DCH * 128)
        assert wfc[0] <= 5248, wfc[0]
        self._WSB = b.sb("WSB", [128, 1280], BF16)
        self.ident = self._WSB[:, 0:128]
        self.S_bf = self._WSB[:, 128:1152]
        self.wq_sb = self.ARN[:, 43008:43520]
        self.pb = [b.ps(f"pb{i}", [128, 512]) for i in range(6)]
        self.pbh = [b.ps(f"pbh{i}", [128, 1024], BF16) for i in range(2)]
        self.pb = self.pb + self.pbh
        self.s_t = b.sem("s_t")
        self.s_v = b.sem("s_v")
        self.s_sc = b.sem("s_sc")
        self.s_g = b.sem("s_g")
        self.s_cc = b.sem("s_cc")
        self.bsem = [b.sem(f"bk{i}") for i in range(8)]
        self.wq = WQ(b, "wq", nrot=6)
        self.R_LNM, self.R_LNRS, self.R_RMS = 0, 32, 64

    def arn(self, a, w):
        return self.ARN[:, a:a + w]

    def cst(self, val, parts=128):
        if val not in self._cst:
            k = len(self._cst)
            assert k < 16
            self._cst[val] = k
            self.b.E("gpsimd", lambda g, k=k, v=float(val):
                     g.memset(self.cstt[:, k:k + 1], v))
        return self.cstt[0:parts, self._cst[val]:self._cst[val] + 1]

    def bwait(self, i, eng="tensor"):
        self.b.W(eng, self.bsem[i], self.bsem[i].n)

    def MMg(self, bank_reg, lhsT, rhs, start=True, stop=True, tp=None, waits=()):
        b = self.b
        for sem, n in waits:
            b.W("tensor", sem, n)
        b.I("tensor", lambda t, o=bank_reg, l=lhsT, r=rhs, st=start, sp=stop, tp=tp:
            t.matmul(o, l, r, start=st, stop=sp, tile_position=tp,
                     skip_group_check=True), self.s_t)
        return self.s_t.n

    def TRg(self, bank_reg, in_, ident, tp=None, waits=()):
        b = self.b
        for sem, n in waits:
            b.W("tensor", sem, n)
        b.I("tensor", lambda t, o=bank_reg, i=in_, d=ident, tp=tp:
            t.transpose(o, i, d, tile_position=tp), self.s_t)
        return self.s_t.n

    def xn_block(self, tb, dst=None, also_f32=None):
        b, cfg = self.b, self.cfg
        tpc = cfg.tpc
        c0 = 512 * tb
        dst = self.xnb if dst is None else dst
        b.W("tensor", self.s_v, self.V_SLIVERS)
        self.bwait(3)
        self.MMg(self.pb[3][:, 0:512], self.onesall[0:1, 0:128],
                 self.arsl[0:1, c0:c0 + 512])
        self.bwait(4)
        self.MMg(self.pb[4][:, 0:512], self.onesall[32:33, 0:128],
                 self.arsl[32:33, c0:c0 + 512])
        tn = self.s_t.n
        b.W("vector", self.s_t, tn)
        for dc in range(DCH):
            xr = self.xres[:, dc * tpc + c0:dc * tpc + c0 + 512]
            d_ap = dst[:, dc * 512:(dc + 1) * 512]
            b.D("vector")
            b.E("vector", lambda v, x=xr: v.tensor_add(
                self.sqtmp[:, :], x, self.pb[3][:, 0:512]))
            b.D("vector")
            last = dc == DCH - 1
            b.I("vector", lambda v, d=d_ap: v.tensor_tensor(
                d, self.sqtmp[:, :], self.pb[4][:, 0:512], ALU.mult),
                self.bsem[4] if last else self.s_v)
            if last:
                b.I("vector", lambda v: v.drain(), self.bsem[3])
        b.F("vector", self.s_v)
        return self.s_v.n

    def h_block(self, tb):
        b, cfg = self.b, self.cfg
        c0 = 512 * tb
        b.W("tensor", self.s_v, self.s_v.n)
        self.bwait(4)
        self.MMg(self.pb[4][:, 0:512], self.onesall[64:65, 0:128],
                 self.arsl[64:65, c0:c0 + 512])
        tn = self.s_t.n
        b.W("vector", self.s_t, tn)
        for dc in range(DCH):
            xn_ap = self.xnb[:, dc * 512:(dc + 1) * 512]
            h_ap = self.hb[:, dc * 512:(dc + 1) * 512]
            b.D("vector")
            b.E("vector", lambda v, x=xn_ap, dc=dc: v.tensor_scalar_mul(
                self.sqtmp[:, :], x, self.nw[:, dc:dc + 1]))
            b.D("vector")
            last = dc == DCH - 1
            b.I("vector", lambda v, h=h_ap: v.tensor_tensor(
                h, self.sqtmp[:, :], self.pb[4][:, 0:512], ALU.mult),
                self.bsem[4] if last else self.s_v)
        b.F("vector", self.s_v)
        return self.s_v.n

    def run(self):
        tpc = self.cfg.tpc
        self.ph0_consts()
        self.ph1_load_ln()
        self.ph2_small()
        self.ph2_router0()
        self.ph3_l0_sweep()
        self.dump("qb", self.qb, 4096, BF16)
        self.dump("kb", self.kb, 4096, BF16)
        self.dump("vb", self.vb, 8192, BF16)
        self.dump("ar16", self.ar16[:, :], tpc, F32)
        self.dump("S_f", self.S_f[:, :], 8 * 128, F32)
        self.dump("Ar_f", self.Ar_f[:, :], 8 * 64, F32)
        self.ph4_ag_compose()
        self.dump("Sin", self.S_f[:, :], 8 * 128, F32)
        self.ph5_l0_blocks()
        self.dump("xres1", self.xres[:, :], DCH * tpc, F32)
        self.ph5c_moe0()
        self.dump("xres2", self.xres[:, :], DCH * tpc, F32)
        self.dump("w0rt", self.ar16[:, :], tpc, F32)
        self.ph7_l1_pre()
        self.dump("arsl2", self.arsl[:, :], tpc, F32)
        self.dump("xn2", self.ARN[:, 0:16384], 16384, BF16)
        self.dump("kvz", self.kvz[:, :], 520, F32)
        self.ph8_ag2()
        self.dump("kvzc", self.kvz[:, :], 520, F32)
        self.ph9_l1_blocks()
        self.dump("w1rt", self.ar16[:, :], tpc, F32)
        self.ph_out()

    def ph0_consts(self):
        b = self.b

        def gp(f):
            b.E("gpsimd", f)
        gp(lambda g: g.memset(self.ident[:, :], 1.0))
        gp(lambda g: g.memset(self.identf[:, :], 1.0))
        gp(lambda g: g.memset(self.mask_s[:, :], 1.0))
        gp(lambda g: g.memset(self.mask_i1[:, :], 1.0))
        gp(lambda g: g.memset(self.onesc[:, :], 1.0))
        gp(lambda g: g.memset(self.onesr[:, :], 1.0))
        gp(lambda g: g.memset(self.onesall[:, :], 1.0))
        gp(lambda g: g.memset(self.sel16[:, :], 1.0))
        gp(lambda g: g.memset(self.oneh16[:, :], 1.0))
        gp(lambda g: g.memset(self.arsl[:, :], 0.0))
        gp(lambda g: g.memset(self.ar16[:, :], 0.0))
        gp(lambda g: g.memset(self.S_bf[:, :], 0.0))
        gp(lambda g: g.memset(self.mksel[:, :], 0.0))
        gp(lambda g: g.memset(self.nksel[:, :], 1.0))
        gp(lambda g: g.memset(self.ones2[:, :], 0.0))
        gp(lambda g: g.memset(self.ones2sc[:, :], 0.0))
        b.D("gpsimd")
        gp(lambda g: g.affine_select(out=self.ident[:, :], in_=self.ident[:, :],
                                     compare_op=ALU.is_equal, fill=0.0, base=0,
                                     pattern=[[-1, 128]], channel_multiplier=1))
        gp(lambda g: g.affine_select(out=self.identf[:, :], in_=self.identf[:, :],
                                     compare_op=ALU.is_equal, fill=0.0, base=0,
                                     pattern=[[-1, 128]], channel_multiplier=1))
        gp(lambda g: g.affine_select(out=self.mask_i1[:, :], in_=self.mask_i1[:, :],
                                     compare_op=ALU.is_ge, fill=0.0, base=0,
                                     pattern=[[1, 128]], channel_multiplier=-1))
        gp(lambda g: g.affine_select(out=self.mask_s[:, :], in_=self.mask_s[:, :],
                                     compare_op=ALU.is_ge, fill=0.0, base=-1,
                                     pattern=[[1, 128]], channel_multiplier=-1))
        gp(lambda g: g.affine_select(
            out=self.sel16[:, :].rearrange("h (m r) -> h m r", r=64),
            in_=self.sel16[:, :].rearrange("h (m r) -> h m r", r=64),
            compare_op=ALU.is_equal, fill=0.0, base=0,
            pattern=[[-1, 16], [0, 64]], channel_multiplier=1))
        gp(lambda g: g.affine_select(
            out=self.oneh16[:, :].rearrange("h (s t) -> h s t", t=128),
            in_=self.oneh16[:, :].rearrange("h (s t) -> h s t", t=128),
            compare_op=ALU.is_equal, fill=0.0, base=0,
            pattern=[[-1, 16], [0, 128]], channel_multiplier=1))
        gp(lambda g: g.affine_select(
            out=self.nksel[:, :], in_=self.nksel[:, :], compare_op=ALU.is_ge,
            fill=0.0, base=-1, pattern=[[0, 128]], channel_multiplier=1))
        b.D("gpsimd")
        gp(lambda g: g.memset(self.mksel[0:1, :], 1.0))
        gp(lambda g: g.memset(self.ones2[0:64, 0:1], 1.0))
        gp(lambda g: g.memset(self.ones2[64:128, 1:2], 1.0))
        gp(lambda g: g.memset(self.ones2sc[0:64, 0:1], SCALE_DK))
        gp(lambda g: g.memset(self.ones2sc[64:128, 1:2], SCALE_DK))
        for v in (EPS_LN, -EPS_LN, 1.0 + EPS_RMS, EPS_L2, EPS_DEN, 1.0 / D,
                  1.0 / DVH, EPS_RMS):
            self.cst(v)
        b.F("gpsimd", self.s_g)
        self.G_CONST = self.s_g.n

    def ph1_load_ln(self):
        b, cfg = self.b, self.cfg
        tpc, ntile = cfg.tpc, cfg.ntile
        x_d = self.dr["x"]
        s_ldb = [b.sem("p1l0"), b.sem("p1l1")]
        s_tr = b.sem("p1tr")
        buffree = [0, 0]
        for i in range(ntile):
            j = i % 2
            buf = self.xtm[j]
            if i >= 2:
                b.W("sync", s_tr, buffree[j])
            b.I("sync", lambda e, buf=buf, src=x_d[128 * i:128 * (i + 1), :]:
                e.dma_start(out=buf[:, :], in_=src), s_ldb[j], k=16)
            ldn = s_ldb[j].n
            m_ap = self.statall[:, 2 * i:2 * i + 1]
            r_ap = self.statall[:, 2 * i + 1:2 * i + 2]
            b.W("vector", s_ldb[j], ldn)
            b.E("vector", lambda v, buf=buf, m=m_ap: v.reduce_sum(m, buf[:, :], axis=AX.X))
            b.D("vector")
            b.E("vector", lambda v, m=m_ap: v.tensor_scalar_mul(m, m, -1.0 / D))
            b.F("vector", self.s_v)
            mrdy = self.s_v.n
            b.W("scalar", self.s_g, self.G_CONST)
            b.W("scalar", s_ldb[j], ldn)
            b.W("scalar", self.s_v, mrdy)
            b.I("scalar", lambda s, buf=buf, m=m_ap: s.activation(
                self.sqb[:, :], buf[:, 0:512], AF.Square, bias=m), self.s_sc)
            sqrdy = self.s_sc.n
            b.W("vector", self.s_sc, sqrdy)
            b.E("vector", lambda v, r=r_ap: v.reduce_sum(r, self.sqb[:, :], axis=AX.X))
            b.F("vector", self.s_v)
            b.W("scalar", self.s_v, self.s_v.n)
            b.I("scalar", lambda s, buf=buf, m=m_ap: s.activation(
                self.sqb[:, :], buf[:, 512:1024], AF.Square, bias=m), self.s_sc)
            b.W("vector", self.s_sc, self.s_sc.n)
            b.E("vector", lambda v: v.reduce_sum(self.gtm[:, 0:1], self.sqb[:, :],
                                                 axis=AX.X))
            b.D("vector")
            b.E("vector", lambda v, r=r_ap: v.tensor_add(r, r, self.gtm[:, 0:1]))
            b.F("vector", self.s_v)
            vrdy = self.s_v.n
            b.W("scalar", self.s_v, vrdy)
            b.I("scalar", lambda s, r=r_ap: s.activation(
                r, r, AF.Sqrt, scale=self.cst(1.0 / D), bias=self.cst(EPS_LN)),
                self.s_sc)
            srdy = self.s_sc.n
            b.W("vector", self.s_sc, srdy)
            b.E("vector", lambda v, r=r_ap: v.reciprocal(r, r))
            b.F("vector", self.s_v)
            xnrdy = self.s_v.n
            b.W("tensor", self.s_g, self.G_CONST)
            b.W("tensor", s_ldb[j], ldn)
            b.W("tensor", self.s_v, xnrdy)
            for dc in range(DCH):
                bank = dc % 2
                px = self.pb[bank][:, 0:128]
                xr_ap = self.xres[:, dc * tpc + 128 * i: dc * tpc + 128 * (i + 1)]
                src_x = buf[:, 128 * dc:128 * (dc + 1)]
                self.bwait(bank)
                b.I("tensor", lambda t, px=px, s_=src_x: t.transpose(
                    px, s_, self.identf[:, :]), s_tr)
                b.W("vector", s_tr, s_tr.n)
                b.I("vector", lambda v, d=xr_ap, p=px: v.tensor_copy(d, p),
                    self.bsem[bank])
            buffree[j] = s_tr.n
        b.W("tensor", self.s_v, self.s_v.n)
        for i in range(ntile):
            pm = self.pb[4][0:1, 0:128]
            pr = self.pb[5][0:1, 0:128]
            d1 = self.arsl[0:1, 128 * i:128 * (i + 1)]
            d2 = self.arsl[32:33, 128 * i:128 * (i + 1)]
            self.bwait(4)
            self.bwait(5)
            b.I("tensor", lambda t, i=i, pm=pm: t.transpose(
                pm, self.statall[:, 2 * i:2 * i + 1], self.identf[:, :]), s_tr)
            b.I("tensor", lambda t, i=i, pr=pr: t.transpose(
                pr, self.statall[:, 2 * i + 1:2 * i + 2], self.identf[:, :]), s_tr)
            b.W("vector", s_tr, s_tr.n)
            b.I("vector", lambda v, d=d1, p=pm: v.tensor_copy(d, p), self.bsem[4])
            b.I("vector", lambda v, d=d2, p=pr: v.tensor_copy(d, p), self.bsem[5])
        rms_ap = self.arsl[64:65, 0:tpc]
        rs_ap = self.arsl[32:33, 0:tpc]
        b.D("vector")
        b.E("vector", lambda v: v.tensor_mul(rms_ap, rs_ap, rs_ap))
        b.F("vector", self.s_v)
        b.W("scalar", self.s_v, self.s_v.n)
        b.I("scalar", lambda s: s.activation(
            rms_ap, rms_ap, AF.Sqrt, scale=self.cst(-EPS_LN, 1),
            bias=self.cst(1.0 + EPS_RMS, 1)), self.s_sc)
        b.W("vector", self.s_sc, self.s_sc.n)
        b.E("vector", lambda v: v.reciprocal(rms_ap, rms_ap))
        b.F("vector", self.s_v)
        self.V_SLIVERS = self.s_v.n
        self.P1TR = (s_tr, s_tr.n)

    def ph2_small(self):
        b = self.b
        s_sl = b.sem("sml")
        b.W("sync", self.P1TR[0], self.P1TR[1])
        for dst, src in ((self.nw, self.dr["w_norm"]), (self.pnw, self.dr["w_pnw"]),
                         (self.convw, self.dr["w_conv"]), (self.dtb, self.dr["w_dtb"]),
                         (self.negal, self.dr["w_Alog"]), (self.onw, self.dr["w_onw"]),
                         (self.cmk, self.dr["cmask"]), (self.xp_t, self.dr["xp"]),
                         (self.rt_sb[:, 0:DCH * 128], self.dr["w_rt0"]),
                         (self.ebrow[:, :], self.dr["w_eb"])):
            b.W("sync", s_sl, s_sl.n)
            d_ap = dst[:, :] if hasattr(dst, "shape") and not hasattr(dst, "tensor") else dst
            b.I("sync", lambda e, d=d_ap, s=src: e.dma_start(out=d, in_=s[:, :]),
                s_sl, k=16)
        self.SML = s_sl.n
        b.W("scalar", s_sl, s_sl.n)
        b.I("scalar", lambda s: s.activation(self.negal[:, :], self.negal[:, :], AF.Exp),
            self.s_sc)
        b.W("vector", self.s_sc, self.s_sc.n)
        b.E("vector", lambda v: v.tensor_scalar_mul(self.negal[:, :], self.negal[:, :], -1.0))
        b.F("vector", self.s_v)
        self.V_NEGAL = self.s_v.n
        b.W("tensor", self.s_g, self.G_CONST)
        self.bwait(4)
        b.I("tensor", lambda t: t.transpose(self.pb[4][0:2, 0:128], self.ones2[:, :],
                                            self.identf[:, :]), self.s_t)
        b.W("vector", self.s_t, self.s_t.n)
        b.I("vector", lambda v: v.tensor_copy(self.ones2T[:, :], self.pb[4][0:2, 0:128]),
            self.bsem[4])
        self.bwait(5)
        b.I("tensor", lambda t: t.transpose(self.pb[5][0:2, 0:128], self.ones2sc[:, :],
                                            self.identf[:, :]), self.s_t)
        b.W("vector", self.s_t, self.s_t.n)
        b.I("vector", lambda v: v.tensor_copy(self.ones2Tsc[:, :], self.pb[5][0:2, 0:128]),
            self.bsem[5])
        st = self.xp_s
        b.W("vector", s_sl, s_sl.n)
        b.E("vector", lambda v: v.reduce_sum(st[0:4, 0:1], self.xp_t[0:4, :], axis=AX.X))
        b.D("vector")
        b.E("vector", lambda v: v.tensor_scalar_mul(st[0:4, 0:1], st[0:4, 0:1], -1.0 / D))
        b.F("vector", self.s_v)
        b.W("scalar", self.s_v, self.s_v.n)
        b.I("scalar", lambda s: s.activation(self.sqtmp[0:4, 0:512], self.xp_t[0:4, 0:512],
                                             AF.Square, bias=st[0:4, 0:1]), self.s_sc)
        b.D("scalar")
        b.I("scalar", lambda s: s.activation(self.cext[0:4, 0:512], self.xp_t[0:4, 512:1024],
                                             AF.Square, bias=st[0:4, 0:1]), self.s_sc)
        b.W("vector", self.s_sc, self.s_sc.n)
        b.E("vector", lambda v: v.reduce_sum(st[0:4, 1:2], self.sqtmp[0:4, 0:512], axis=AX.X))
        b.D("vector")
        b.E("vector", lambda v: v.reduce_sum(st[0:4, 2:3], self.cext[0:4, 0:512], axis=AX.X))
        b.D("vector")
        b.E("vector", lambda v: v.tensor_add(st[0:4, 1:2], st[0:4, 1:2], st[0:4, 2:3]))
        b.F("vector", self.s_v)
        b.W("scalar", self.s_v, self.s_v.n)
        b.I("scalar", lambda s: s.activation(st[0:4, 1:2], st[0:4, 1:2], AF.Sqrt,
                                             scale=self.cst(1.0 / D, 4),
                                             bias=self.cst(EPS_LN, 4)), self.s_sc)
        b.W("vector", self.s_sc, self.s_sc.n)
        b.E("vector", lambda v: v.reciprocal(st[0:4, 1:2], st[0:4, 1:2]))
        b.D("vector")
        b.E("vector", lambda v: v.tensor_mul(st[0:4, 2:3], st[0:4, 1:2], st[0:4, 1:2]))
        b.F("vector", self.s_v)
        b.W("scalar", self.s_v, self.s_v.n)
        b.I("scalar", lambda s: s.activation(st[0:4, 2:3], st[0:4, 2:3], AF.Sqrt,
                                             scale=self.cst(-EPS_LN, 4),
                                             bias=self.cst(1.0 + EPS_RMS, 4)), self.s_sc)
        b.W("vector", self.s_sc, self.s_sc.n)
        b.E("vector", lambda v: v.reciprocal(st[0:4, 2:3], st[0:4, 2:3]))
        b.D("vector")
        b.E("vector", lambda v: v.tensor_mul(st[0:4, 3:4], st[0:4, 1:2], st[0:4, 2:3]))
        b.D("vector")
        b.E("vector", lambda v: v.tensor_scalar(self.xp_t[0:4, :], self.xp_t[0:4, :],
                                                st[0:4, 0:1], st[0:4, 3:4],
                                                ALU.add, ALU.mult))
        b.F("vector", self.s_v)
        hx = self.s_v.n
        b.W("tensor", self.s_v, hx)
        for dc in range(DCH):
            self.bwait(5)
            b.I("tensor", lambda t, dc=dc: t.transpose(
                self.pb[5][0:128, 0:4], self.xp_t[0:4, 128 * dc:128 * (dc + 1)],
                self.identf[0:4, 0:4]), self.s_t)
            b.W("vector", self.s_t, self.s_t.n)
            b.I("vector", lambda v, dc=dc: v.tensor_scalar_mul(
                self.h3fm[:, 4 * dc:4 * (dc + 1)], self.pb[5][0:128, 0:4],
                self.nw[:, dc:dc + 1]), self.bsem[5])
        b.F("vector", self.s_v)
        self.V_H3 = self.s_v.n

    def ph2_router0(self):
        b, cfg = self.b, self.cfg
        tpc = cfg.tpc
        wrt = self.rt_sb[:, 0:DCH * 128]
        b.W("tensor", self.SML and self.s_v, self.V_SLIVERS)
        self.bwait(2)
        for dc in range(DCH):
            self.MMg(self.pb[2][0:8, 0:1], wrt[:, 128 * dc:128 * dc + 8],
                     self.onesc[:, :], start=(dc == 0), stop=(dc == DCH - 1),
                     waits=(((self.s_g, self.G_CONST),) if dc == 0 else ()))
        b.W("vector", self.s_t, self.s_t.n)
        b.I("vector", lambda v: v.tensor_copy(self.cs16[0:8, 0:1], self.pb[2][0:8, 0:1]),
            self.bsem[2])
        vcs = self.s_v.n + 0
        b.F("vector", self.s_v)
        vcs = self.s_v.n
        for tb in range(cfg.nblk):
            c0 = 512 * tb
            self.bwait(2)
            for dc in range(DCH):
                self.MMg(self.pb[2][0:8, 0:512], wrt[:, 128 * dc:128 * dc + 8],
                         self.xres[:, dc * tpc + c0: dc * tpc + c0 + 512],
                         start=(dc == 0), stop=(dc == DCH - 1))
            self.bwait(3)
            self.MMg(self.pb[3][0:8, 0:512], self.onesall[0:1, 0:8],
                     self.arsl[0:1, c0:c0 + 512])
            self.bwait(4)
            self.MMg(self.pb[4][0:8, 0:512], self.onesall[32:33, 0:8],
                     self.arsl[32:33, c0:c0 + 512])
            tn = self.s_t.n
            b.W("vector", self.s_t, tn)
            b.W("vector", self.s_v, vcs)
            b.E("vector", lambda v: v.tensor_copy(self.sqtmp[0:8, 0:512],
                                                  self.pb[2][0:8, 0:512]))
            b.I("vector", lambda v: v.drain(), self.bsem[2])
            b.E("vector", lambda v: v.scalar_tensor_tensor(
                self.sqtmp[0:8, 0:512], self.pb[3][0:8, 0:512], self.cs16[0:8, 0:1],
                self.sqtmp[0:8, 0:512], ALU.mult, ALU.add))
            b.I("vector", lambda v: v.drain(), self.bsem[3])
            g_ap = self.ar16[64:72, c0:c0 + 512]
            b.I("vector", lambda v, g=g_ap: v.tensor_tensor(
                g, self.sqtmp[0:8, 0:512], self.pb[4][0:8, 0:512], ALU.mult),
                self.bsem[4])
            b.F("vector", self.s_v)
        self.V_GATE0 = self.s_v.n
        b.F("tensor", self.s_t)
        self.T_RT0 = self.s_t.n

    def _ar_ident(self, t):
        b = self.b
        b.E("gpsimd", lambda g, t=t: g.memset(t, 1.0))
        b.D("gpsimd")
        for r0 in (0, 64):
            b.E("gpsimd", lambda g, t=t, r0=r0: g.affine_select(
                out=t[r0:r0 + 64, :].rearrange("p (j c) -> p j c", c=64),
                in_=t[r0:r0 + 64, :].rearrange("p (j c) -> p j c", c=64),
                compare_op=ALU.is_equal, fill=0.0, base=0,
                pattern=[[0, 8], [-1, 64]], channel_multiplier=1))

    def ph3_l0_sweep(self):
        b, cfg = self.b, self.cfg
        b.W("gpsimd", self.s_v, self.V_H3)
        b.E("gpsimd", lambda g: g.memset(self.S_f[:, :], 0.0))
        self._ar_ident(self.Ar_f[:, :])
        b.F("gpsimd", self.s_g)
        self.G_S0 = self.s_g.n
        b.W("vector", self.s_g, self.G_S0)
        b.E("vector", lambda v: v.tensor_copy(self.Ar_bf[:, :], self.Ar_f[:, :]))
        b.F("vector", self.s_v)
        b.W("tensor", self.s_g, self.G_S0)
        self.s_kv = b.sem("s_kv")
        self.s_wp = b.sem("s_wp")
        self.s_ba = b.sem("s_ba")
        self.socr = [b.sem("socr0"), b.sem("socr1")]
        self.sgdr = [b.sem("sgdr0"), b.sem("sgdr1")]
        self.soc = {}
        self.sgd = {}
        self.slot_free = [None] * 8
        self.slot_i = 0
        for tb in range(cfg.nblk):
            self.xn_block(tb)
            self.h_block(tb)
            self.l0_block_proj(tb)
            b.F("vector", self.s_kv)
            b.F("scalar", self.s_kv)
            KVN = self.s_kv.n
            BAN = self.s_ba.n
            for c in range(4):
                self.l0_delta_chunk(tb, c, KVN, BAN)

    def wslot(self, src_ap):
        j = self.slot_i % 8
        self.slot_i += 1
        dst = self.wp[:, j * 1024:(j + 1) * 1024]
        tok = self.wq.load(dst, src_ap, pre_wait=self.slot_free[j])
        return j, tok

    def wpc(self, j, dc):
        return self.wp[:, j * 1024 + dc * 128:j * 1024 + dc * 128 + 128]

    def l0_block_proj(self, tb):
        b, cfg = self.b, self.cfg
        s_kv, s_wp, s_ba = self.s_kv, self.s_wp, self.s_ba
        HBv = self.s_v.n
        projs = [("w_q", DCH, 0, self.qb, 1), ("w_k", DCH, 8, self.kb, 2),
                 ("w_v", VCH, 16, self.vb, 0)]
        pk = 0
        for wname, noc, po0, store, do_l2 in projs:
            img = self.dr[wname]
            for oc in range(noc):
                po = po0 + oc
                j, tok = self.wslot(img[:, oc * DCH * 128:(oc + 1) * DCH * 128])
                bank = pk % 2
                pk += 1
                b.W("tensor", tok[0], tok[1])
                b.W("tensor", self.s_v, HBv)
                self.bwait(bank)
                for dc in range(DCH):
                    self.MMg(self.pb[bank][:, 0:512], self.wpc(j, dc),
                             self.hb[:, dc * 512:(dc + 1) * 512],
                             start=(dc == 0), stop=(dc == DCH - 1))
                if tb == 0:
                    self.bwait(5)
                    b.W("tensor", self.s_v, self.V_H3)
                    for dc in range(DCH):
                        self.MMg(self.pb[5][:, 0:4], self.wpc(j, dc),
                                 self.h3fm[:, 4 * dc:4 * (dc + 1)],
                                 start=(dc == 0), stop=(dc == DCH - 1))
                    b.W("vector", self.s_t, self.s_t.n)
                    b.I("vector", lambda v, po=po: v.tensor_copy(
                        self.ctails[:, 3 * po:3 * po + 3], self.pb[5][:, 0:3]),
                        self.bsem[5])
                b.F("tensor", s_wp)
                self.slot_free[j] = (s_wp, s_wp.n)
                tmm = self.s_t.n
                b.W("vector", self.s_t, tmm)
                b.D("vector")
                b.E("vector", lambda v, po=po: v.tensor_copy(
                    self.cext[:, 0:3], self.ctails[:, 3 * po:3 * po + 3]))
                b.D("vector")
                b.I("vector", lambda v, bank=bank: v.tensor_copy(
                    self.cext[:, 3:515], self.pb[bank][:, 0:512]), self.bsem[bank])
                b.D("vector")
                b.E("vector", lambda v, po=po: v.tensor_copy(
                    self.ctails[:, 3 * po:3 * po + 3], self.cext[:, 512:515]))
                b.D("vector")
                b.W("vector", self.s_sc, self.s_sc.n)
                b.E("vector", lambda v, po=po: v.tensor_scalar_mul(
                    self.sqtmp[:, :], self.cext[:, 0:512], self.convw[:, 4 * po:4 * po + 1]))
                for i in range(1, 4):
                    b.D("vector")
                    b.E("vector", lambda v, po=po, i=i: v.scalar_tensor_tensor(
                        self.sqtmp[:, :], self.cext[:, i:i + 512],
                        self.convw[:, 4 * po + i:4 * po + i + 1],
                        self.sqtmp[:, :], ALU.mult, ALU.add))
                b.F("vector", self.s_v)
                cvr = self.s_v.n
                dst = store[:, oc * 512:(oc + 1) * 512]
                b.W("scalar", self.s_v, cvr)
                b.W("scalar", s_kv, s_kv.n)
                if do_l2:
                    b.I("scalar", lambda s: s.activation(
                        self.cext[:, 0:512], self.sqtmp[:, :], AF.Silu), self.s_sc)
                    b.D("scalar")
                    b.I("scalar", lambda s: s.activation(
                        self.sqtmp[:, :], self.cext[:, 0:512], AF.Square), self.s_sc)
                    sq2 = self.s_sc.n
                    self.bwait(4)
                    self.MMg(self.pb[4][0:2, 0:512], self.ones2[:, :], self.sqtmp[:, :],
                             waits=((self.s_sc, sq2),))
                    tl2 = self.s_t.n
                    b.W("scalar", self.s_t, tl2)
                    b.I("scalar", lambda s: s.activation(
                        self.l2row[:, :], self.pb[4][0:2, 0:512], AF.Sqrt,
                        bias=self.cst(EPS_L2, 2)), self.s_sc)
                    b.I("scalar", lambda s: s.drain(), self.bsem[4])
                    b.W("vector", self.s_sc, self.s_sc.n)
                    b.E("vector", lambda v: v.reciprocal(self.l2row[:, :], self.l2row[:, :]))
                    b.F("vector", self.s_v)
                    rcn = self.s_v.n
                    o2t = self.ones2Tsc if do_l2 == 1 else self.ones2T
                    self.bwait(5)
                    self.MMg(self.pb[5][:, 0:512], o2t[:, :], self.l2row[:, :],
                             waits=((self.s_v, rcn),))
                    tbc = self.s_t.n
                    b.W("vector", self.s_t, tbc)
                    b.I("vector", lambda v, dst=dst: v.tensor_tensor(
                        dst, self.cext[:, 0:512], self.pb[5][:, 0:512], ALU.mult),
                        self.bsem[5])
                    b.I("vector", lambda v: v.drain(), s_kv)
                else:
                    b.I("scalar", lambda s, dst=dst: s.activation(
                        dst, self.sqtmp[:, :], AF.Silu), self.s_sc)
                    b.I("scalar", lambda s: s.drain(), s_kv)
        img = self.dr["w_ba"]
        j, tok = self.wslot(img[:, :])
        b.W("tensor", tok[0], tok[1])
        b.W("tensor", self.s_v, HBv)
        self.bwait(2)
        self.bwait(3)
        for dc in range(DCH):
            self.MMg(self.pb[2][0:16, 0:512], self.wpc(j, dc)[:, 0:16],
                     self.hb[:, dc * 512:(dc + 1) * 512],
                     start=(dc == 0), stop=(dc == DCH - 1))
        for dc in range(DCH):
            self.MMg(self.pb[3][0:16, 0:512], self.wpc(j, dc)[:, 16:32],
                     self.hb[:, dc * 512:(dc + 1) * 512],
                     start=(dc == 0), stop=(dc == DCH - 1))
        b.F("tensor", s_wp)
        self.slot_free[j] = (s_wp, s_wp.n)
        tba = self.s_t.n
        c0 = 512 * tb
        beta_ap = self.ar16[0:16, c0:c0 + 512]
        g_ap = self.ar16[32:48, c0:c0 + 512]
        b.W("scalar", self.s_t, tba)
        b.W("scalar", self.s_v, self.V_NEGAL)
        b.I("scalar", lambda s, d=beta_ap: s.activation(
            d, self.pb[2][0:16, 0:512], AF.Sigmoid), self.s_sc)
        b.I("scalar", lambda s: s.drain(), self.bsem[2])
        b.W("scalar", self.s_v, self.s_v.n)
        b.I("scalar", lambda s: s.activation(
            self.sqtmp[0:16, 0:512], self.pb[3][0:16, 0:512], AF.Abs,
            bias=self.dtb[0:16, 0:1]), self.s_sc)
        b.D("scalar")
        b.I("scalar", lambda s: s.activation(
            self.sqtmp[0:16, 0:512], self.sqtmp[0:16, 0:512], AF.Exp, scale=-1.0),
            self.s_sc)
        b.D("scalar")
        b.I("scalar", lambda s: s.activation(
            self.sqtmp[0:16, 0:512], self.sqtmp[0:16, 0:512], AF.Ln, bias=1.0),
            self.s_sc)
        b.D("scalar")
        b.I("scalar", lambda s: s.activation(
            self.cext[0:16, 0:512], self.pb[3][0:16, 0:512], AF.Relu,
            bias=self.dtb[0:16, 0:1]), self.s_sc)
        b.I("scalar", lambda s: s.drain(), self.bsem[3])
        spn = self.s_sc.n
        b.W("vector", self.s_sc, spn)
        b.E("vector", lambda v: v.tensor_add(
            self.sqtmp[0:16, 0:512], self.sqtmp[0:16, 0:512], self.cext[0:16, 0:512]))
        b.D("vector")
        b.I("vector", lambda v, d=g_ap: v.tensor_scalar_mul(
            d, self.sqtmp[0:16, 0:512], self.negal[0:16, 0:1]), self.s_ba)
        b.F("vector", self.s_v)

    def l0_delta_chunk(self, tb, c, KVN, BAN):
        b, cfg = self.b, self.cfg
        c0 = 512 * tb
        gch = tb * 4 + c
        col = 128 * c
        bs = self.bsem
        s_t, s_v, s_sc = self.s_t, self.s_v, self.s_sc
        s_kv, s_ba = self.s_kv, self.s_ba

        g_ap = self.ar16[32:48, c0 + col:c0 + col + 128]
        beta_ap = self.ar16[0:16, c0 + col:c0 + col + 128]
        b.W("tensor", s_ba, BAN)
        self.bwait(4)
        t1 = self.TRg(self.pb[4][0:128, 0:16], g_ap, self.identf[32:48, 32:48])
        b.W("vector", s_t, t1)
        b.I("vector", lambda v: v.tensor_copy(self.gtm[:, :], self.pb[4][0:128, 0:16]),
            bs[4])
        b.F("vector", s_v)
        self.bwait(4)
        t2 = self.MMg(self.pb[4][0:128, 16:32], self.mask_i1[:, :], self.gtm[:, :],
                      waits=((s_v, s_v.n),))
        b.W("vector", s_t, t2)
        b.E("vector", lambda v: v.tensor_copy(self.gtm[:, :], self.pb[4][0:128, 16:32]))
        b.D("vector")
        b.I("vector", lambda v: v.tensor_scalar_mul(self.ngtm[:, :], self.gtm[:, :], -1.0),
            bs[4])
        b.F("vector", s_v)
        vgtm = s_v.n
        self.bwait(5)
        t3 = self.TRg(self.pb[5][0:16, 0:128], self.gtm[:, :], self.identf[:, :],
                      waits=((s_v, vgtm),))
        b.W("vector", s_t, t3)
        b.I("vector", lambda v: v.tensor_copy(self.gfm[:, :], self.pb[5][0:16, 0:128]),
            bs[5])
        b.F("vector", s_v)
        vgfm = s_v.n
        self.bwait(5)
        t4 = self.TRg(self.pb[5][0:128, 128:144], beta_ap, self.identf[0:16, 0:16])
        b.W("vector", s_t, t4)
        b.I("vector", lambda v: v.tensor_copy(self.btm[:, :], self.pb[5][0:128, 128:144]),
            bs[5])
        b.F("vector", s_v)
        b.W("scalar", s_v, vgfm)
        b.I("scalar", lambda s: s.activation(self.egfm[:, :], self.gfm[:, :], AF.Exp),
            s_sc)
        b.D("scalar")
        b.I("scalar", lambda s: s.activation(self.edec[:, :], self.gfm[:, :], AF.Exp,
                                             scale=-1.0, bias=self.gfm[:, 127:128]), s_sc)
        scg = s_sc.n
        b.W("vector", s_sc, scg)
        b.E("vector", lambda v, ba=beta_ap: v.tensor_tensor(
            self.bexp[:, :], ba, self.egfm[:, :], ALU.mult))
        b.D("vector")
        b.E("vector", lambda v: v.tensor_copy(self.eg_b[:, :], self.egfm[:, :]))
        b.D("vector")
        b.E("vector", lambda v, ba=beta_ap: v.tensor_copy(self.bbf[:, :], ba))
        b.F("vector", s_v)
        vpre = s_v.n

        for j in range(8):
            K2 = self.kb[:, j * 512 + col:j * 512 + col + 128]
            Q2 = self.qb[:, j * 512 + col:j * 512 + col + 128]
            sel_j = self.sel16[:, 128 * j:128 * (j + 1)]
            self.bwait(0)
            self.MMg(self.pb[0][:, 0:128], sel_j, self.bexp[:, :],
                     waits=((s_v, vpre),))
            self.bwait(1)
            self.MMg(self.pb[1][:, 0:128], sel_j, self.eg_b[:, :])
            self.bwait(2)
            bb3 = self.MMg(self.pb[2][:, 0:128], sel_j, self.edec[:, :])
            b.W("vector", s_t, bb3)
            b.W("vector", s_kv, KVN)
            b.I("vector", lambda v, K2=K2: v.tensor_tensor(
                self.Wf2[:, :], K2, self.pb[0][:, 0:128], ALU.mult), bs[0])
            b.D("vector")
            b.E("vector", lambda v, Q2=Q2: v.tensor_tensor(
                self.Qt2[:, :], Q2, self.pb[1][:, 0:128], ALU.mult))
            b.D("vector")
            b.I("vector", lambda v: v.tensor_copy(
                self.egCc[:, :], self.pb[1][:, 127:128]), bs[1])
            b.D("vector")
            b.I("vector", lambda v, K2=K2: v.tensor_tensor(
                self.Kb2[:, :], K2, self.pb[2][:, 0:128], ALU.mult), bs[2])
            b.F("vector", s_v)
            vW = s_v.n

            for hh in range(2):
                h = 2 * j + hh
                base = 64 * hh
                Kh = K2[base:base + 64, :]
                Qh = Q2[base:base + 64, :]
                Vh = self.vb[:, h * 512 + col:h * 512 + col + 128]
                idb = self.identf[base:base + 64, base:base + 64]
                idbb = self.ident[base:base + 64, base:base + 64]
                Sfh = self.S_f[base:base + 64, j * 128:(j + 1) * 128]
                Sbh = self.S_bf[base:base + 64, j * 128:(j + 1) * 128]
                Arfh = self.Ar_f[base:base + 64, j * 64:(j + 1) * 64]
                Arbh = self.Ar_bf[base:base + 64, j * 64:(j + 1) * 64]
                self.bwait(3)
                tPK = self.MMg(self.pb[3][:, 0:128], Kh, Kh, waits=((s_kv, KVN),))
                self.bwait(4)
                tKQ = self.MMg(self.pb[4][:, 0:128], Kh, Qh)
                self.bwait(5)
                tex = self.MMg(self.pb[5][0:1, 0:128], self.identf[0:16, h:h + 1],
                               self.gfm[:, :], waits=((s_v, vgfm),))
                b.W("vector", s_t, tex)
                b.I("vector", lambda v: v.tensor_copy(self.onrow[:, :],
                                                      self.pb[5][0:1, 0:128]), bs[5])
                b.F("vector", s_v)
                vex = s_v.n
                self.bwait(5)
                tgb = self.MMg(self.pb[5][:, 0:128], self.onesall[0:1, 0:128],
                               self.onrow[:, :], waits=((s_v, vex),))
                self.bwait(0)
                tbb = self.MMg(self.pb[0][:, 0:128],
                               self.oneh16[:, 128 * h:128 * (h + 1)], self.bbf[:, :])
                b.W("vector", s_t, tgb)
                b.I("vector", lambda v, h=h: v.tensor_scalar(
                    self.tmpA[:, :], self.pb[5][:, 0:128], self.ngtm[:, h:h + 1], 0.0,
                    ALU.add, ALU.min), bs[5])
                b.F("vector", s_v)
                b.W("scalar", s_v, s_v.n)
                b.I("scalar", lambda s: s.activation(self.Dsb[:, :], self.tmpA[:, :],
                                                     AF.Exp), s_sc)
                scD = s_sc.n
                b.W("vector", s_sc, scD)
                b.W("vector", s_t, tKQ)
                b.E("vector", lambda v: v.tensor_tensor(
                    self.tmpA[:, :], self.Dsb[:, :], self.pb[4][:, 0:128], ALU.mult))
                b.I("vector", lambda v: v.drain(), bs[4])
                b.E("vector", lambda v: v.tensor_tensor(
                    self.At_bf[:, :], self.tmpA[:, :], self.mask_i1[:, :], ALU.mult))
                b.D("vector")
                b.E("vector", lambda v: v.tensor_tensor(
                    self.tmpB[:, :], self.Dsb[:, :], self.pb[3][:, 0:128], ALU.mult))
                b.I("vector", lambda v: v.drain(), bs[3])
                b.W("vector", s_t, tbb)
                b.E("vector", lambda v: v.tensor_tensor(
                    self.tmpB[:, :], self.tmpB[:, :], self.pb[0][:, 0:128], ALU.mult))
                b.I("vector", lambda v: v.drain(), bs[0])
                b.E("vector", lambda v: v.tensor_tensor(
                    self.Mt_bf[:, :], self.tmpB[:, :], self.mask_s[:, :], ALU.mult))
                b.F("vector", s_v)
                vM = s_v.n
                self.bwait(6)
                tmt = self.TRg(self.pb[6][:, 0:128], self.Mt_bf[:, :], self.ident[:, :],
                               waits=((s_v, vM),))
                b.W("vector", s_t, tmt)
                b.I("vector", lambda v: v.tensor_copy(self.Mti_bf[:, :],
                                                      self.pb[6][:, 0:128]), bs[6])
                b.F("vector", s_v)
                vmti = s_v.n
                self.bwait(4)
                tws = self.MMg(self.pb[4][:, 0:128], self.Wf2[base:base + 64, :], Sbh,
                               waits=((s_v, vW),))
                self.bwait(7)
                tvt = self.TRg(self.pb[7][:, 0:128], Vh, self.ident[:, :])
                b.W("vector", s_t, tvt)
                b.E("vector", lambda v, h=h: v.tensor_scalar_mul(
                    self.tmpA[:, :], self.pb[7][:, 0:128], self.btm[:, h:h + 1]))
                b.I("vector", lambda v: v.drain(), bs[7])
                b.E("vector", lambda v: v.tensor_sub(
                    self.RHS[:, 0:128], self.tmpA[:, :], self.pb[4][:, 0:128]))
                b.I("vector", lambda v: v.drain(), bs[4])
                self.bwait(7)
                twt = self.TRg(self.pb[7][:, 128:192], self.Wf2[base:base + 64, :], idbb)
                b.W("vector", s_t, twt)
                b.I("vector", lambda v: v.tensor_copy(
                    self.RHS[:, 128:192], self.pb[7][:, 128:192]), bs[7])
                b.F("vector", s_v)
                vrhs = s_v.n
                curT, curti = self.Mt_bf[:, :], self.Mti_bf[:, :]
                for lvl in range(6):
                    pt = self.PT[:, 128 * lvl:128 * (lvl + 1)]
                    self.bwait(2)
                    tsq = self.MMg(self.pb[2][:, 0:128], curT, curti,
                                   waits=((s_v, s_v.n),))
                    b.W("vector", s_t, tsq)
                    b.I("vector", lambda v: v.tensor_copy(self.Pti[:, :],
                                                          self.pb[2][:, 0:128]), bs[2])
                    b.F("vector", s_v)
                    self.bwait(6)
                    tst = self.TRg(self.pb[6][:, 0:128], self.Pti[:, :], self.ident[:, :],
                                   waits=((s_v, s_v.n),))
                    b.W("vector", s_t, tst)
                    b.I("vector", lambda v, pt=pt: v.tensor_copy(pt, self.pb[6][:, 0:128]),
                        bs[6])
                    b.F("vector", s_v)
                    curT, curti = pt, self.Pti[:, :]
                vsq = s_v.n
                Y = self.RHS[:, :]
                b.W("tensor", s_v, vsq)
                for a in range(7):
                    lhs = self.Mt_bf[:, :] if a == 6 else self.PT[:, 128 * (5 - a):128 * (6 - a)]
                    Yn = self.Ybuf[:, 192 * (a % 2):192 * (a % 2) + 192]
                    self.bwait(5)
                    tap = self.MMg(self.pb[5][:, 0:192], lhs, Y,
                                   waits=((s_v, vrhs if a == 0 else s_v.n),))
                    b.W("vector", s_t, tap)
                    if a == 6:
                        b.I("vector", lambda v, Yn=Yn, Y=Y: v.tensor_sub(
                            Yn, Y, self.pb[5][:, 0:192]), bs[5])
                    else:
                        b.I("vector", lambda v, Yn=Yn, Y=Y: v.tensor_add(
                            Yn, Y, self.pb[5][:, 0:192]), bs[5])
                    b.F("vector", s_v)
                    Y = Yn
                U = Y
                vU = s_v.n
                self.bwait(0)
                to1 = self.MMg(self.pb[0][:, 0:128], Sbh, self.Qt2[base:base + 64, :])
                self.bwait(1)
                to2 = self.MMg(self.pb[1][:, 0:128], U[:, 0:128], self.At_bf[:, :],
                               waits=((s_v, vU),))
                b.W("vector", s_t, to2)
                b.E("vector", lambda v: v.tensor_copy(self.o1sb[:, :], self.pb[0][:, 0:128]))
                b.I("vector", lambda v: v.drain(), bs[0])
                ob_ap = self.obc[:, h * 128:(h + 1) * 128]
                if gch >= 1 and h == 0:
                    so, sn = self.soc[gch - 1]
                    b.W("vector", so, sn)
                b.I("vector", lambda v, d=ob_ap: v.tensor_add(
                    d, self.o1sb[:, :], self.pb[1][:, 0:128]), bs[1])
                b.F("vector", s_v)
                self.bwait(7)
                tkb = self.TRg(self.pb[7][:, 256:320], self.Kb2[base:base + 64, :], idbb)
                b.W("vector", s_t, tkb)
                b.I("vector", lambda v: v.tensor_copy(self.Kbt[:, :],
                                                      self.pb[7][:, 256:320]), bs[7])
                b.F("vector", s_v)
                vkbt = s_v.n
                self.bwait(2)
                tsu = self.MMg(self.pb[2][base:base + 64, 0:192], self.Kbt[:, :],
                               U[:, 0:192], tp=(0, base), waits=((s_v, vkbt),))
                b.W("vector", s_t, tsu)
                eg_ap = self.egCc[base:base + 64, 0:1]
                ps_ap = self.pb[2][base:base + 64, 0:128]
                b.E("vector", lambda v, Sfh=Sfh, eg=eg_ap, ps=ps_ap:
                    v.scalar_tensor_tensor(Sfh, Sfh, eg, ps, ALU.mult, ALU.add))
                b.D("vector")
                b.E("vector", lambda v, Sfh=Sfh, Sbh=Sbh: v.tensor_copy(Sbh, Sfh))
                b.D("vector")
                tB_ap = self.tmpB[base:base + 64, 0:64]
                pa_ap = self.pb[2][base:base + 64, 128:192]
                b.I("vector", lambda v, idb=idb, tB=tB_ap, eg=eg_ap, pa=pa_ap:
                    v.scalar_tensor_tensor(tB, idb, eg, pa, ALU.mult, ALU.subtract),
                    bs[2])
                b.D("vector")
                z_ap = self.Z2[base:base + 64, 0:64]
                b.E("vector", lambda v, z=z_ap, tB=tB_ap: v.tensor_copy(z, tB))
                b.F("vector", s_v)
                vach = s_v.n
                self.bwait(3)
                tz = self.MMg(self.pb[3][base:base + 64, 0:128], U[:, 128:192],
                              self.At_bf[:, :], tp=(0, base))
                b.W("vector", s_t, tz)
                tA_ap = self.tmpA[base:base + 64, 0:128]
                b.I("vector", lambda v, tA=tA_ap, base=base: v.tensor_sub(
                    tA, self.Qt2[base:base + 64, :],
                    self.pb[3][base:base + 64, 0:128]), bs[3])
                b.D("vector")
                zc_ap = self.Mti_bf[base:base + 64, 0:128]
                b.E("vector", lambda v, z=zc_ap, tA=tA_ap: v.tensor_copy(z, tA))
                b.F("vector", s_v)
                vzb = s_v.n
                gslot = gch % 2
                gb_ap = self.gst[base:base + 64, gslot * 1024 + j * 128:
                                 gslot * 1024 + j * 128 + 128]
                self.bwait(4)
                tg = self.MMg(self.pb[4][base:base + 64, 0:128], Arbh, zc_ap,
                              tp=(base, base), waits=((s_v, vzb),))
                b.W("vector", s_t, tg)
                if gch >= 2 and h == 0:
                    so, sn = self.sgd[gch - 2]
                    b.W("vector", so, sn)
                pg_ap = self.pb[4][base:base + 64, 0:128]
                b.I("vector", lambda v, d=gb_ap, p=pg_ap: v.tensor_copy(d, p), bs[4])
                self.bwait(5)
                ta1 = self.MMg(self.pb[5][base:base + 64, 0:64], z_ap, idbb,
                               tp=(base, base), waits=((s_v, vach),))
                b.W("vector", s_t, ta1)
                ac_ap = self.AchT[base:base + 64, 0:64]
                b.I("vector", lambda v, d=ac_ap, base=base: v.tensor_copy(
                    d, self.pb[5][base:base + 64, 0:64]), bs[5])
                b.F("vector", s_v)
                vat = s_v.n
                self.bwait(5)
                ta2 = self.MMg(self.pb[5][base:base + 64, 64:128], ac_ap, Arbh,
                               tp=(base, base), waits=((s_v, vat),))
                b.W("vector", s_t, ta2)
                b.E("vector", lambda v, Arfh=Arfh, base=base: v.tensor_copy(
                    Arfh, self.pb[5][base:base + 64, 64:128]))
                b.D("vector")
                b.I("vector", lambda v, Arbh=Arbh, Arfh=Arfh: v.tensor_copy(Arbh, Arfh),
                    bs[5])
                b.F("vector", s_v)
        b.F("vector", s_v)
        VEND = s_v.n
        soc = self.socr[gch % 2]
        b.W("sync", s_v, VEND)
        b.I("sync", lambda e, gch=gch: e.dma_start(
            out=self.oloc_d.ap()[gch], in_=self.obc[:, :]), soc, k=16)
        self.soc[gch] = (soc, soc.n)
        sgd = self.sgdr[gch % 2]
        gslot = gch % 2
        b.I("sync", lambda e, gch=gch, gslot=gslot: e.dma_start(
            out=self.g_d.ap()[gch], in_=self.gst[:, gslot * 1024:(gslot + 1) * 1024]),
            sgd, k=16)
        self.sgd[gch] = (sgd, sgd.n)

    def ph4_ag_compose(self):
        b, cfg = self.b, self.cfg
        cc0_in, cc0_out = self.ccs[0], self.ccs[1]
        s_ag = b.sem("s_ag")
        s_agl = b.sem("s_agl")
        b.F("vector", self.s_v)
        vfin = self.s_v.n
        b.W("sync", self.s_v, vfin)
        for j in range(8):
            a_dst = cc0_in.ap()[j * 128:(j + 1) * 128, 0:64]
            a_src = self.Ar_f[:, j * 64:(j + 1) * 64]
            b.I("sync", lambda e, a=a_dst, s_=a_src: e.dma_start(out=a, in_=s_),
                s_ag, k=16)
            s_dst = cc0_in.ap()[j * 128:(j + 1) * 128, 64:192]
            s_src = self.S_f[:, j * 128:(j + 1) * 128]
            b.I("sync", lambda e, a=s_dst, s_=s_src: e.dma_start(out=a, in_=s_),
                s_ag, k=16)
        b.W("gpsimd", s_ag, s_ag.n)
        b.I("gpsimd", lambda g: g.collective_compute(
            "AllGather", ALU.bypass, replica_groups=[list(range(NCORE))],
            ins=[cc0_in.ap().opt()], outs=[cc0_out.ap().opt()]), self.s_cc)
        CC0 = self.s_cc.n
        b.W("gpsimd", s_ag, s_ag.n)
        self._ar_ident(self.AID[:, :])
        b.F("gpsimd", self.s_g)
        AIDN = self.s_g.n
        b.W("vector", self.s_g, AIDN)
        b.W("tensor", self.s_g, self.G_CONST)
        b.W("tensor", self.s_v, self.V_NEGAL)
        self.bwait(0)
        b.I("tensor", lambda t: t.matmul(self.pb[0][:, 0:NCORE], self.mksel[:, :],
                                         self.cmk[:, :], start=True, stop=True), self.s_t)
        self.bwait(1)
        b.I("tensor", lambda t: t.matmul(self.pb[1][:, 0:NCORE], self.nksel[:, :],
                                         self.cmk[:, :], start=True, stop=True), self.s_t)
        b.W("vector", self.s_t, self.s_t.n)
        b.I("vector", lambda v: v.tensor_copy(self.mubc[:, :], self.pb[0][:, 0:NCORE]),
            self.bsem[0])
        b.I("vector", lambda v: v.tensor_copy(self.nubc[:, :], self.pb[1][:, 0:NCORE]),
            self.bsem[1])
        b.W("vector", s_ag, s_ag.n)
        b.E("vector", lambda v: v.memset(self.S_f[:, :], 0.0))
        b.D("vector")
        b.E("vector", lambda v: v.memset(self.S_bf[:, :], 0.0))
        b.F("vector", self.s_v)
        b.W("sync", self.s_cc, CC0)
        for l in range(NCORE):
            for j in range(8):
                r0 = l * NH * DKH + j * 128
                b.I("sync", lambda e, r0=r0, j=j: e.dma_start(
                    out=self.Atmp[:, j * 64:(j + 1) * 64],
                    in_=cc0_out.ap()[r0:r0 + 128, 0:64]), s_agl, k=16)
                b.I("sync", lambda e, r0=r0, j=j: e.dma_start(
                    out=self.Btmp[:, j * 128:(j + 1) * 128],
                    in_=cc0_out.ap()[r0:r0 + 128, 64:192]), s_agl, k=16)
            AGL = s_agl.n
            b.W("tensor", s_agl, AGL)
            for jj in range(8):
                for sub in range(2):
                    bb = 64 * sub
                    src = self.Atmp[bb:bb + 64, jj * 64:(jj + 1) * 64]
                    dst = self.AtT[bb:bb + 64, jj * 64:(jj + 1) * 64]
                    pbk = 2 + (jj % 2)
                    self.bwait(pbk)
                    reg = self.pb[pbk][bb:bb + 64, 0:64]
                    b.I("tensor", lambda t, r=reg, s_=src, bb=bb: t.matmul(
                        r, s_, self.identf[bb:bb + 64, bb:bb + 64],
                        start=True, stop=True, tile_position=(bb, bb),
                        skip_group_check=True), self.s_t)
                    b.W("vector", self.s_t, self.s_t.n)
                    b.I("vector", lambda v, d=dst, r=reg: v.tensor_copy(d, r),
                        self.bsem[pbk])
            b.F("vector", self.s_v)
            mu = self.mubc[:, l:l + 1]
            b.E("vector", lambda v: v.tensor_sub(self.AtT[:, :], self.AtT[:, :],
                                                 self.AID[:, :]))
            b.D("vector")
            b.E("vector", lambda v, mu=mu: v.scalar_tensor_tensor(
                self.AtT[:, :], self.AtT[:, :], mu, self.AID[:, :], ALU.mult, ALU.add))
            b.D("vector")
            b.E("vector", lambda v: v.tensor_copy(self.AtTb[:, :], self.AtT[:, :]))
            b.F("vector", self.s_v)
            vbl = self.s_v.n
            b.W("tensor", self.s_v, vbl)
            for jj in range(8):
                for sub in range(2):
                    bb = 64 * sub
                    lhs = self.AtTb[bb:bb + 64, jj * 64:(jj + 1) * 64]
                    rhs = self.S_bf[bb:bb + 64, jj * 128:(jj + 1) * 128]
                    pbk = 4 + (jj % 2)
                    self.bwait(pbk)
                    reg = self.pb[pbk][bb:bb + 64, 0:128]
                    b.I("tensor", lambda t, r=reg, l_=lhs, rh=rhs, bb=bb: t.matmul(
                        r, l_, rh, start=True, stop=True, tile_position=(bb, bb)),
                        self.s_t)
                    b.W("vector", self.s_t, self.s_t.n)
                    sfh = self.S_f[bb:bb + 64, jj * 128:(jj + 1) * 128]
                    bth = self.Btmp[bb:bb + 64, jj * 128:(jj + 1) * 128]
                    muh = self.mubc[bb:bb + 64, l:l + 1]
                    b.I("vector", lambda v, s_=sfh, bt=bth, mu=muh, r=reg:
                        v.scalar_tensor_tensor(s_, bt, mu, r, ALU.mult, ALU.add),
                        self.bsem[pbk])
            b.F("vector", self.s_v)
            b.E("vector", lambda v: v.tensor_copy(self.S_bf[:, :], self.S_f[:, :]))
            b.F("vector", self.s_v)
            b.F("tensor", self.s_t)
            b.W("sync", self.s_t, self.s_t.n)
            b.W("sync", self.s_v, self.s_v.n)
        self.V_SIN = self.s_v.n

    def ph5_l0_blocks(self):
        b, cfg = self.b, self.cfg
        s_t, s_v, s_sc = self.s_t, self.s_v, self.s_sc
        self.s_fin = b.sem("s_fin")
        self.s_or = b.sem("s_or")
        self.s_gr = b.sem("s_gr")
        for s in (self.socr[0], self.socr[1], self.sgdr[0], self.sgdr[1]):
            b.W("sync", s, s.n)
        for tb in range(cfg.nblk):
            self.xn_block(tb, dst=self.arn(4096 * tb, 4096))
        self.V_XNF = self.s_v.n
        wgg = WS(self, [self.arn(32768 + 1024 * i, 1024) for i in range(4)])
        wgu = WS(self, [self.arn(20480 + 1024 * i, 1024) for i in range(8)])
        wdn = WS(self, [self.arn(42496 + 1408 * i, 1408) for i in range(2)])
        wwo = WS(self, [self.arn(36864 + 2048 * i, 2048) for i in range(2)])
        wwo.barrier((self.s_v, self.V_XNF))
        wgg.barrier((self.s_v, self.V_XNF))
        wdn.barrier((self.s_v, self.V_XNF))
        self.ofree = None
        self.gfence = {}
        for tb in range(cfg.nblk):
            self.l0_block_fin(tb, wgg, wgu, wdn, wwo)
        b.F("tensor", s_t)
        self.PH5B_T = (s_t, s_t.n)

    def h5_block(self, tb):
        b, cfg = self.b, self.cfg
        c0 = 512 * tb
        b.W("tensor", self.s_v, self.V_XNF)
        self.bwait(4)
        self.MMg(self.pb[4][:, 0:512], self.onesall[64:65, 0:128],
                 self.arsl[64:65, c0:c0 + 512])
        tn = self.s_t.n
        b.W("vector", self.s_t, tn)
        for dc in range(DCH):
            xn_ap = self.arn(4096 * tb + 512 * dc, 512)
            h_ap = self.arn(16384 + 512 * dc, 512)
            b.D("vector")
            b.W("vector", self.s_sc, self.s_sc.n)
            b.E("vector", lambda v, x=xn_ap, dc=dc: v.tensor_scalar_mul(
                self.sqtmp[:, :], x, self.nw[:, dc:dc + 1]))
            b.D("vector")
            last = dc == DCH - 1
            b.I("vector", lambda v, h=h_ap: v.tensor_tensor(
                h, self.sqtmp[:, :], self.pb[4][:, 0:512], ALU.mult),
                self.bsem[4] if last else self.s_v)
        b.F("vector", self.s_v)
        return self.s_v.n

    def l0_block_fin(self, tb, wgg, wgu, wdn, wwo):
        b, cfg = self.b, self.cfg
        tpc = cfg.tpc
        c0 = 512 * tb
        bs = self.bsem
        s_t, s_v, s_sc, s_fin = self.s_t, self.s_v, self.s_sc, self.s_fin
        hb5 = lambda dc: self.arn(16384 + 512 * dc, 512)
        gos = lambda h, c: self.arn(20480 + 512 * h + 128 * c, 128)
        obc5 = self.arn(28672, 2048)
        x1s = lambda dc: self.arn(32768 + 512 * dc, 512)
        HB = self.h5_block(tb)
        img = self.dr["w_g"]
        for oc in range(NH):
            j, buf, tok = wgg.get(img[:, oc * DCH * 128:(oc + 1) * DCH * 128])
            bank = oc % 2
            b.W("tensor", tok[0], tok[1])
            b.W("tensor", s_v, HB)
            self.bwait(bank)
            for dc in range(DCH):
                b.I("tensor", lambda t, buf=buf, dc=dc, bank=bank: t.matmul(
                    self.pb[bank][:, 0:512], buf[:, dc * 128:(dc + 1) * 128],
                    hb5(dc), start=(dc == 0), stop=(dc == DCH - 1),
                    skip_group_check=True), s_t)
            wgg.release(j)
            tn = s_t.n
            dst = self.arn(20480 + 512 * oc, 512)
            b.W("scalar", s_t, tn)
            b.I("scalar", lambda s, d=dst, bank=bank: s.activation(
                d, self.pb[bank][:, 0:512], AF.Silu), s_sc)
            b.I("scalar", lambda s, bank=bank: s.drain(), bs[bank])
        b.F("scalar", s_sc)
        GS = s_sc.n
        b.W("vector", s_sc, GS)
        for c in range(4):
            gch = tb * 4 + c
            gslot = gch % 2
            gbuf = self.arn(30720 + 1024 * gslot, 1024)
            if self.ofree is not None:
                b.W("sync", self.ofree[0], self.ofree[1])
            b.I("sync", lambda e, gch=gch, d=obc5: e.dma_start(
                out=d, in_=self.oloc_d.ap()[gch]), self.s_or, k=16)
            if gch >= 2:
                fo = self.gfence[gch - 2]
                b.W("sync", fo[0], fo[1])
            b.I("sync", lambda e, gch=gch, d=gbuf: e.dma_start(
                out=d, in_=self.g_d.ap()[gch]), self.s_gr, k=16)
            ORN, GRN = self.s_or.n, self.s_gr.n
            for h in range(NH):
                jj, sub = h // 2, h % 2
                bb = 64 * sub
                Sin = self.S_bf[bb:bb + 64, jj * 128:(jj + 1) * 128]
                Gh = gbuf[bb:bb + 64, jj * 128:(jj + 1) * 128]
                ob_ap = obc5[:, h * 128:(h + 1) * 128]
                og_ap = gos(h, c)
                self.bwait(0)
                b.W("tensor", s_v, self.V_SIN)
                b.W("tensor", self.s_gr, GRN)
                self.MMg(self.pb[0][:, 0:128], Sin, Gh, tp=(bb, 0))
                tcor = s_t.n
                if h == 0:
                    b.W("vector", self.s_or, ORN)
                b.W("vector", s_t, tcor)
                b.W("vector", s_sc, s_sc.n)
                b.I("vector", lambda v, o=ob_ap: v.tensor_add(
                    self.otile[:, :], o, self.pb[0][:, 0:128]), bs[0])
                if h == NH - 1:
                    b.F("vector", s_v)
                    self.ofree = (s_v, s_v.n)
                else:
                    b.F("vector", s_v)
                vcor = s_v.n
                b.W("scalar", s_v, vcor)
                b.I("scalar", lambda s: s.activation(
                    self.sqtmp[:, 0:128], self.otile[:, :], AF.Square), s_sc)
                b.W("tensor", s_sc, s_sc.n)
                self.bwait(1)
                self.MMg(self.pb[1][0:1, 0:128], self.onesc[:, :], self.sqtmp[:, 0:128])
                tsum = s_t.n
                b.W("scalar", s_t, tsum)
                b.I("scalar", lambda s: s.activation(
                    self.onrow[:, :], self.pb[1][0:1, 0:128], AF.Sqrt,
                    scale=self.cst(1.0 / DVH, 1), bias=self.cst(EPS_RMS, 1)), s_sc)
                b.I("scalar", lambda s: s.drain(), bs[1])
                b.W("vector", s_sc, s_sc.n)
                b.E("vector", lambda v: v.reciprocal(self.onrow[:, :], self.onrow[:, :]))
                b.F("vector", s_v)
                vrn = s_v.n
                b.W("tensor", s_v, vrn)
                self.bwait(2)
                self.MMg(self.pb[2][:, 0:128], self.onesr[:, :], self.onrow[:, :])
                tbc = s_t.n
                b.W("vector", s_t, tbc)
                b.E("vector", lambda v: v.tensor_tensor(
                    self.otile[:, :], self.otile[:, :], self.pb[2][:, 0:128], ALU.mult))
                b.I("vector", lambda v: v.drain(), bs[2])
                b.E("vector", lambda v: v.tensor_scalar_mul(
                    self.otile[:, :], self.otile[:, :], self.onw[:, 0:1]))
                b.D("vector")
                b.I("vector", lambda v, d=og_ap: v.tensor_tensor(
                    d, self.otile[:, :], d, ALU.mult), s_fin)
                b.F("vector", s_v)
            b.F("tensor", s_t)
            self.gfence[gch] = (s_t, s_t.n)
        b.F("vector", s_fin)
        OGN = s_fin.n
        img = self.dr["w_o"]
        b.W("tensor", s_v, self.V_SLIVERS)
        self.bwait(3)
        self.MMg(self.pb[3][:, 0:512], self.onesall[0:1, 0:128],
                 self.arsl[0:1, c0:c0 + 512])
        self.bwait(5)
        self.MMg(self.pb[5][:, 0:512], self.onesall[32:33, 0:128],
                 self.arsl[32:33, c0:c0 + 512])
        self.bwait(2)
        for dc in range(DCH):
            j, buf, tok = wwo.get(img[:, dc * NH * 128:(dc + 1) * NH * 128])
            bank = dc % 2
            b.W("tensor", tok[0], tok[1])
            b.W("tensor", s_fin, OGN)
            self.bwait(bank)
            for vc in range(NH):
                b.I("tensor", lambda t, buf=buf, vc=vc, bank=bank: t.matmul(
                    self.pb[bank][:, 0:512], buf[:, vc * 128:(vc + 1) * 128],
                    self.arn(20480 + 512 * vc, 512),
                    start=(vc == 0), stop=(vc == NH - 1),
                    skip_group_check=True), s_t)
            wwo.release(j)
            tmix = s_t.n
            xr_ap = self.xres[:, dc * tpc + c0:dc * tpc + c0 + 512]
            b.W("vector", s_t, tmix)
            b.W("vector", s_sc, s_sc.n)
            b.E("vector", lambda v, x=xr_ap: v.tensor_add(
                self.sqtmp[:, :], x, self.pb[3][:, 0:512]))
            b.D("vector")
            b.E("vector", lambda v: v.tensor_tensor(
                self.sqtmp[:, :], self.sqtmp[:, :], self.pb[5][:, 0:512], ALU.mult))
            b.D("vector")
            b.I("vector", lambda v, bank=bank: v.tensor_add(
                self.cext[:, 0:512], self.sqtmp[:, :], self.pb[bank][:, 0:512]),
                bs[bank])
            b.D("vector")
            b.E("vector", lambda v, x=xr_ap: v.tensor_add(x, x, self.cext[:, 0:512]))
            b.D("vector")
            b.I("vector", lambda v, dc=dc: v.tensor_copy(x1s(dc), self.cext[:, 0:512]),
                s_fin)
            b.F("vector", s_v)
            vx1 = s_v.n
            b.W("scalar", s_v, vx1)
            b.I("scalar", lambda s: s.activation(
                self.sqb[:, 0:512], self.cext[:, 0:512], AF.Square), s_sc)
            b.W("tensor", s_sc, s_sc.n)
            b.I("tensor", lambda t, dc=dc: t.matmul(
                self.pb[2][0:1, 0:512], self.onesc[:, :], self.sqb[:, 0:512],
                start=(dc == 0), stop=(dc == DCH - 1), skip_group_check=True), s_t)
            if dc < DCH - 1:
                b.W("scalar", s_t, s_t.n)
        b.I("vector", lambda v: v.drain(), bs[3])
        b.I("vector", lambda v: v.drain(), bs[5])
        b.F("vector", s_v)
        tss = s_t.n
        b.W("scalar", s_t, tss)
        b.I("scalar", lambda s: s.activation(
            self.l2row[0:1, 0:512], self.pb[2][0:1, 0:512], AF.Sqrt,
            scale=self.cst(1.0 / D, 1), bias=self.cst(EPS_RMS, 1)), s_sc)
        b.I("scalar", lambda s: s.drain(), bs[2])
        b.W("vector", s_sc, s_sc.n)
        b.E("vector", lambda v: v.reciprocal(self.l2row[0:1, 0:512],
                                             self.l2row[0:1, 0:512]))
        b.F("vector", s_v)
        vrm = s_v.n
        b.W("tensor", s_v, vrm)
        self.bwait(3)
        self.MMg(self.pb[3][:, 0:512], self.onesall[0:1, 0:128],
                 self.l2row[0:1, 0:512])
        trb = s_t.n
        b.W("vector", s_t, trb)
        for dc in range(DCH):
            b.D("vector")
            b.E("vector", lambda v, dc=dc: v.tensor_tensor(
                self.sqtmp[:, :], x1s(dc), self.pb[3][:, 0:512], ALU.mult))
            b.D("vector")
            b.I("vector", lambda v, dc=dc: v.tensor_scalar_mul(
                x1s(dc), self.sqtmp[:, :], self.pnw[:, dc:dc + 1]),
                bs[3] if dc == DCH - 1 else s_v)
        b.F("vector", s_v)
        VH2 = s_v.n
        gimg, uimg, dimg = self.dr["w_gate"], self.dr["w_up"], self.dr["w_down"]
        b.F("tensor", s_t)
        wgu.barrier((s_t, s_t.n))
        wdn.barrier((s_t, s_t.n))
        for half in range(2):
            for fi in range(11):
                ffc = half * 11 + fi
                w = 128 if ffc < FFCH - 1 else FF_LAST
                jg, bg, tokg = wgu.get(gimg[:, ffc * DCH * 128:(ffc + 1) * DCH * 128])
                ju, bu, toku = wgu.get(uimg[:, ffc * DCH * 128:(ffc + 1) * DCH * 128])
                b.W("tensor", tokg[0], tokg[1])
                b.W("tensor", toku[0], toku[1])
                b.W("tensor", s_v, VH2)
                self.bwait(0)
                self.bwait(1)
                for dc in range(DCH):
                    b.I("tensor", lambda t, bg=bg, dc=dc, w=w: t.matmul(
                        self.pb[0][0:w, 0:512], bg[:, dc * 128:dc * 128 + w],
                        x1s(dc), start=(dc == 0), stop=(dc == DCH - 1),
                        skip_group_check=True), s_t)
                for dc in range(DCH):
                    b.I("tensor", lambda t, bu=bu, dc=dc, w=w: t.matmul(
                        self.pb[1][0:w, 0:512], bu[:, dc * 128:dc * 128 + w],
                        x1s(dc), start=(dc == 0), stop=(dc == DCH - 1),
                        skip_group_check=True), s_t)
                wgu.release(jg)
                wgu.release(ju)
                tmlp = s_t.n
                hid_ap = self.arn(36864 + 512 * fi, 512)[0:w, :]
                b.W("scalar", s_t, tmlp)
                b.W("scalar", s_v, s_v.n)
                b.I("scalar", lambda s, w=w: s.activation(
                    self.sqtmp[0:w, 0:512], self.pb[0][0:w, 0:512], AF.Silu), s_sc)
                b.I("scalar", lambda s: s.drain(), bs[0])
                scs = s_sc.n
                b.W("vector", s_sc, scs)
                b.I("vector", lambda v, hid=hid_ap, w=w: v.tensor_tensor(
                    hid, self.sqtmp[0:w, 0:512], self.pb[1][0:w, 0:512], ALU.mult),
                    bs[1])
                b.F("vector", s_v)
            VHID = s_v.n
            for oc2 in range(DCH):
                j, buf, tok = wdn.get(
                    dimg[:, oc2 * FFCH * 128 + half * 11 * 128:
                         oc2 * FFCH * 128 + half * 11 * 128 + 1408])
                bank = 4 + oc2 % 2
                b.W("tensor", tok[0], tok[1])
                b.W("tensor", s_v, VHID)
                self.bwait(bank)
                for fi in range(11):
                    ffc = half * 11 + fi
                    w = 128 if ffc < FFCH - 1 else FF_LAST
                    b.I("tensor", lambda t, buf=buf, fi=fi, w=w, bank=bank: t.matmul(
                        self.pb[bank][:, 0:512], buf[0:w, fi * 128:fi * 128 + 128],
                        self.arn(36864 + 512 * fi, 512)[0:w, :],
                        start=(fi == 0), stop=(fi == 10), skip_group_check=True), s_t)
                wdn.release(j)
                tdn = s_t.n
                xr_ap = self.xres[:, oc2 * tpc + c0:oc2 * tpc + c0 + 512]
                b.W("vector", s_t, tdn)
                b.D("vector")
                b.I("vector", lambda v, x=xr_ap, bank=bank: v.tensor_tensor(
                    x, x, self.pb[bank][:, 0:512], ALU.add), bs[bank])
            b.F("vector", s_v)
        b.F("tensor", s_t)
        wgg.barrier((s_t, s_t.n))
        wwo.barrier((s_t, s_t.n))

    def route_tiles(self, src_row, nexp, dst_row, use_bias, probs):
        b, cfg = self.b, self.cfg
        s_t, s_v, s_sc = self.s_t, self.s_v, self.s_sc
        bs = self.bsem
        idsl = self.identf[src_row:src_row + nexp, src_row:src_row + nexp]
        VW = self.s_v.n
        for ti in range(cfg.ntile):
            cols = 128 * ti
            self.bwait(2)
            t1 = self.TRg(self.pb[2][0:128, 0:nexp],
                          self.ar16[src_row:src_row + nexp, cols:cols + 128],
                          idsl, tp=(src_row, 0),
                          waits=((s_v, VW),) if ti == 0 else ())
            b.W("vector", s_t, t1)
            b.I("vector", lambda v, n=nexp: v.tensor_copy(
                self.rtm[:, 0:n], self.pb[2][0:128, 0:n]), bs[2])
            b.D("vector")
            if use_bias:
                b.E("vector", lambda v, n=nexp: v.tensor_add(
                    self.rw1[:, 0:n], self.rtm[:, 0:n], self.ebb[:, 0:n]))
                sel = self.rw1
            else:
                sel = self.rtm
            b.D("vector")
            b.E("vector", lambda v, sel=sel, n=nexp: v.max(
                self.mx8[:, :], sel[:, 0:n]))
            b.D("vector")
            b.E("vector", lambda v: v.tensor_sub(
                self.sqtmp[:, 0:1], self.mx8[:, 0:1], self.mx8[:, 1:2]))
            b.F("vector", s_v)
            vmx = s_v.n
            if probs:
                b.W("scalar", s_v, vmx)
                b.I("scalar", lambda s: s.activation(
                    self.sqtmp[:, 1:2], self.sqtmp[:, 0:1], AF.Sigmoid), s_sc)
                b.D("scalar")
                b.I("scalar", lambda s: s.activation(
                    self.sqtmp[:, 2:3], self.sqtmp[:, 0:1], AF.Sigmoid, scale=-1.0),
                    s_sc)
                b.W("vector", s_sc, s_sc.n)
                b.E("vector", lambda v, sel=sel, n=nexp: v.tensor_scalar(
                    self.rtm[:, 16:16 + n], sel[:, 0:n], self.mx8[:, 0:1],
                    self.sqtmp[:, 1:2], ALU.is_equal, ALU.mult))
                b.D("vector")
                b.E("vector", lambda v, sel=sel, n=nexp: v.tensor_scalar(
                    self.sqtmp[:, 4:4 + n], sel[:, 0:n], self.mx8[:, 1:2],
                    self.sqtmp[:, 2:3], ALU.is_equal, ALU.mult))
                b.D("vector")
                b.E("vector", lambda v, n=nexp: v.tensor_add(
                    self.rtm[:, 16:16 + n], self.rtm[:, 16:16 + n],
                    self.sqtmp[:, 4:4 + n]))
            else:
                b.E("vector", lambda v, sel=sel, n=nexp: v.tensor_scalar(
                    self.rtm[:, 16:16 + n], sel[:, 0:n], self.mx8[:, 0:1], 1.0,
                    ALU.is_equal, ALU.mult))
                b.D("vector")
                b.E("vector", lambda v, sel=sel, n=nexp: v.tensor_scalar(
                    self.sqtmp[:, 4:4 + n], sel[:, 0:n], self.mx8[:, 1:2], 1.0,
                    ALU.is_equal, ALU.mult))
                b.D("vector")
                b.E("vector", lambda v, n=nexp: v.tensor_add(
                    self.rtm[:, 16:16 + n], self.rtm[:, 16:16 + n],
                    self.sqtmp[:, 4:4 + n]))
                b.D("vector")
                b.E("vector", lambda v, n=nexp: v.tensor_mul(
                    self.rtm[:, 16:16 + n], self.rtm[:, 16:16 + n], self.rtm[:, 0:n]))
            b.F("vector", s_v)
            vw2 = s_v.n
            self.bwait(3)
            t2 = self.TRg(self.pb[3][0:nexp, 0:128], self.rtm[:, 16:16 + nexp],
                          self.identf[:, :], waits=((s_v, vw2),))
            b.W("vector", s_t, t2)
            b.I("vector", lambda v, cols=cols, n=nexp: v.tensor_copy(
                self.ar16[dst_row:dst_row + n, cols:cols + 128],
                self.pb[3][0:n, 0:128]), bs[3])
            b.F("vector", s_v)

    def webc_build(self, e, nexp, wrow):
        b, cfg = self.b, self.cfg
        s_t, s_v = self.s_t, self.s_v
        bs = self.bsem
        for tb in range(cfg.nblk):
            c0 = 512 * tb
            self.bwait(0)
            self.MMg(self.pb[0][0:1, 0:512], self.identf[0:nexp, e:e + 1],
                     self.ar16[wrow:wrow + nexp, c0:c0 + 512],
                     waits=((s_v, s_v.n),))
            b.W("vector", s_t, s_t.n)
            b.I("vector", lambda v: v.tensor_copy(self.l2row[0:1, 0:512],
                                                  self.pb[0][0:1, 0:512]), bs[0])
            b.F("vector", s_v)
            vv = s_v.n
            self.bwait(0)
            self.MMg(self.pb[0][:, 0:512], self.onesall[0:1, 0:128],
                     self.l2row[0:1, 0:512], waits=((s_v, vv),))
            b.W("vector", s_t, s_t.n)
            b.I("vector", lambda v, tb=tb: v.tensor_copy(
                self.webc_v(tb), self.pb[0][:, 0:512]), bs[0])
            b.F("vector", s_v)

    def webc_v(self, tb):
        return self.xtm[tb // 2][:, 512 * (tb % 2):512 * (tb % 2) + 512]

    def ph5c_moe0(self):
        b, cfg = self.b, self.cfg
        tpc = cfg.tpc
        s_t, s_v, s_sc = self.s_t, self.s_v, self.s_sc
        bs = self.bsem
        b.W("tensor", s_v, self.s_v.n)
        self.route_tiles(64, E0, 0, use_bias=False, probs=True)
        wsl = WS(self, [self.arn(32768 + 1024 * i, 1024) for i in range(12)])
        wsl.barrier(self.PH5B_T)
        m1i, m2i, m3i = self.dr["w_m1"], self.dr["w_m2"], self.dr["w_m3"]
        hid = lambda tb, oc8: self.arn(16384 + (8 * tb + oc8) * 512, 512)
        for e in range(E0):
            self.webc_build(e, E0, 0)
            VWE = s_v.n
            for half in range(2):
                for oc8 in range(8):
                    oc = 8 * half + oc8
                    bk1 = 2 * (oc8 % 2)
                    bk2 = 1 + 2 * (oc8 % 2)
                    j1, b1, tk1 = wsl.get(m1i.ap()[e][:, oc * 1024:(oc + 1) * 1024])
                    j2, b2, tk2 = wsl.get(m2i.ap()[e][:, oc * 1024:(oc + 1) * 1024])
                    b.W("tensor", tk1[0], tk1[1])
                    b.W("tensor", tk2[0], tk2[1])
                    for tb in range(cfg.nblk):
                        self.bwait(bk1)
                        self.bwait(bk2)
                        for dc in range(DCH):
                            b.I("tensor", lambda t, b1=b1, dc=dc, tb=tb, bk1=bk1:
                                t.matmul(self.pb[bk1][:, 0:512],
                                         b1[:, dc * 128:(dc + 1) * 128],
                                         self.arn(4096 * tb + 512 * dc, 512),
                                         start=(dc == 0), stop=(dc == DCH - 1),
                                         skip_group_check=True), s_t)
                        for dc in range(DCH):
                            b.I("tensor", lambda t, b2=b2, dc=dc, tb=tb, bk2=bk2:
                                t.matmul(self.pb[bk2][:, 0:512],
                                         b2[:, dc * 128:(dc + 1) * 128],
                                         self.arn(4096 * tb + 512 * dc, 512),
                                         start=(dc == 0), stop=(dc == DCH - 1),
                                         skip_group_check=True), s_t)
                        th = s_t.n
                        b.W("scalar", s_t, th)
                        b.W("scalar", s_v, s_v.n)
                        b.I("scalar", lambda s, bk1=bk1: s.activation(
                            self.sqtmp[:, :], self.pb[bk1][:, 0:512], AF.Silu), s_sc)
                        b.I("scalar", lambda s: s.drain(), bs[bk1])
                        b.W("vector", s_sc, s_sc.n)
                        b.I("vector", lambda v, d=hid(tb, oc8), bk2=bk2: v.tensor_tensor(
                            d, self.sqtmp[:, :], self.pb[bk2][:, 0:512], ALU.mult),
                            bs[bk2])
                        b.F("vector", s_v)
                    wsl.release(j1)
                    wsl.release(j2)
                VHID = s_v.n
                for oc2 in range(DCH):
                    j3, b3, tk3 = wsl.get(
                        m3i.ap()[e][:, oc2 * 2048 + half * 1024:
                                    oc2 * 2048 + half * 1024 + 1024])
                    b.W("tensor", tk3[0], tk3[1])
                    b.W("tensor", s_v, VHID)
                    for tb in range(cfg.nblk):
                        bank = 4 + (oc2 + tb) % 2
                        self.bwait(bank)
                        for oc8 in range(8):
                            b.I("tensor", lambda t, b3=b3, oc8=oc8, tb=tb, bank=bank:
                                t.matmul(self.pb[bank][:, 0:512],
                                         b3[:, oc8 * 128:(oc8 + 1) * 128],
                                         self.arn(16384 + (8 * tb + oc8) * 512, 512),
                                         start=(oc8 == 0), stop=(oc8 == 7),
                                         skip_group_check=True), s_t)
                        tdn = s_t.n
                        c0 = 512 * tb
                        xr_ap = self.xres[:, oc2 * tpc + c0:oc2 * tpc + c0 + 512]
                        b.W("vector", s_t, tdn)
                        b.W("vector", s_sc, s_sc.n)
                        b.E("vector", lambda v, tb=tb, bank=bank: v.tensor_tensor(
                            self.sqb[:, 0:512], self.pb[bank][:, 0:512],
                            self.webc_v(tb), ALU.mult))
                        b.I("vector", lambda v, bank=bank: v.drain(), bs[bank])
                        b.I("vector", lambda v, x=xr_ap: v.tensor_add(
                            x, x, self.sqb[:, 0:512]), self.s_fin)
                        b.F("vector", s_v)
                    wsl.release(j3)
        b.F("vector", s_v)
        self.V_MOE = s_v.n
        b.F("tensor", s_t)
        self.T_MOE = (s_t, s_t.n)

    def ph7_l1_pre(self):
        b, cfg = self.b, self.cfg
        tpc = cfg.tpc
        s_t, s_v, s_sc = self.s_t, self.s_v, self.s_sc
        bs = self.bsem
        s_r1 = b.sem("s_r1")
        b.W("sync", s_t, self.T_MOE[1])
        b.I("sync", lambda e: e.dma_start(out=self.rt_sb[:, :],
                                          in_=self.dr["w_rt1"][:, :]), s_r1, k=16)
        self.R1N = s_r1.n
        self.s_r1 = s_r1
        tokk = self.wq.load(self.wkv[:, 0:512], self.dr["w_wk"].ap(),
                            pre_wait=self.T_MOE)
        tokv = self.wq.load(self.wkv[:, 512:1024], self.dr["w_wv"].ap())
        self.WKV = (tokk, tokv)
        b.W("gpsimd", s_t, self.T_MOE[1])
        b.E("gpsimd", lambda g: g.memset(self.onescb[:, :], 1.0))
        b.F("gpsimd", self.s_g)
        self.G_OCB = self.s_g.n
        for tb in range(cfg.nblk):
            c0 = 512 * tb
            self.bwait(2)
            self.bwait(3)
            for dc in range(DCH):
                xr = self.xres[:, dc * tpc + c0:dc * tpc + c0 + 512]
                b.W("scalar", s_v, self.V_MOE)
                if dc > 0:
                    b.W("scalar", s_t, s_t.n)
                b.I("scalar", lambda s, x=xr: s.activation(
                    self.sqb[:, 0:512], x, AF.Square), s_sc)
                self.MMg(self.pb[2][0:1, 0:512], self.onesc[:, :], xr,
                         start=(dc == 0), stop=(dc == DCH - 1),
                         waits=(((s_v, self.V_MOE),) if dc == 0 and tb == 0 else ()))
                b.W("tensor", s_sc, s_sc.n)
                self.MMg(self.pb[3][0:1, 0:512], self.onesc[:, :],
                         self.sqb[:, 0:512], start=(dc == 0), stop=(dc == DCH - 1))
            tss = s_t.n
            m_ap = self.arsl[0:1, c0:c0 + 512]
            b.W("vector", s_t, tss)
            b.I("vector", lambda v, m=m_ap: v.tensor_scalar_mul(
                m, self.pb[2][0:1, 0:512], -1.0 / D), bs[2])
            b.D("vector")
            b.I("vector", lambda v: v.tensor_scalar_mul(
                self.l2row[0:1, 0:512], self.pb[3][0:1, 0:512], 1.0 / D), bs[3])
            b.D("vector")
            b.E("vector", lambda v, m=m_ap: v.tensor_mul(
                self.sqtmp[0:1, 0:512], m, m))
            b.D("vector")
            b.E("vector", lambda v: v.tensor_sub(
                self.l2row[0:1, 0:512], self.l2row[0:1, 0:512],
                self.sqtmp[0:1, 0:512]))
            b.F("vector", s_v)
            b.W("scalar", s_v, s_v.n)
            b.I("scalar", lambda s: s.activation(
                self.l2row[0:1, 0:512], self.l2row[0:1, 0:512], AF.Sqrt,
                bias=self.cst(EPS_LN, 1)), s_sc)
            b.W("vector", s_sc, s_sc.n)
            b.E("vector", lambda v: v.reciprocal(self.l2row[0:1, 0:512],
                                                 self.l2row[0:1, 0:512]))
            b.F("vector", s_v)
            vrs = s_v.n
            self.bwait(4)
            self.MMg(self.pb[4][32:33, 0:512], self.onesall[0:1, 0:1],
                     self.l2row[0:1, 0:512], tp=(0, 32), waits=((s_v, vrs),))
            b.W("vector", s_t, s_t.n)
            b.I("vector", lambda v, c0=c0: v.tensor_copy(
                self.arsl[32:33, c0:c0 + 512], self.pb[4][32:33, 0:512]), bs[4])
            b.F("vector", s_v)
        self.V_SLIVERS = self.s_v.n
        wrt = self.rt_sb[:, 0:DCH * 128]
        self.bwait(2)
        b.W("tensor", self.s_r1, self.R1N)
        for dc in range(DCH):
            self.MMg(self.pb[2][0:16, 0:1], wrt[:, 128 * dc:128 * dc + 16],
                     self.onesc[:, :], start=(dc == 0), stop=(dc == DCH - 1))
        b.W("vector", s_t, s_t.n)
        b.I("vector", lambda v: v.tensor_copy(self.cs16[:, 0:1],
                                              self.pb[2][0:16, 0:1]), bs[2])
        b.F("vector", s_v)
        vcs = s_v.n
        for tb in range(cfg.nblk):
            c0 = 512 * tb
            self.bwait(2)
            for dc in range(DCH):
                self.MMg(self.pb[2][0:16, 0:512], wrt[:, 128 * dc:128 * dc + 16],
                         self.xres[:, dc * tpc + c0: dc * tpc + c0 + 512],
                         start=(dc == 0), stop=(dc == DCH - 1))
            self.bwait(3)
            self.MMg(self.pb[3][0:16, 0:512], self.onesall[0:1, 0:16],
                     self.arsl[0:1, c0:c0 + 512],
                     waits=((s_v, self.V_SLIVERS),))
            self.bwait(4)
            self.MMg(self.pb[4][0:16, 0:512], self.onesall[32:33, 0:16],
                     self.arsl[32:33, c0:c0 + 512])
            tn = s_t.n
            b.W("vector", s_t, tn)
            b.W("vector", s_v, vcs)
            b.E("vector", lambda v: v.tensor_copy(self.sqtmp[0:16, 0:512],
                                                  self.pb[2][0:16, 0:512]))
            b.I("vector", lambda v: v.drain(), bs[2])
            b.E("vector", lambda v: v.scalar_tensor_tensor(
                self.sqtmp[0:16, 0:512], self.pb[3][0:16, 0:512], self.cs16[0:16, 0:1],
                self.sqtmp[0:16, 0:512], ALU.mult, ALU.add))
            b.I("vector", lambda v: v.drain(), bs[3])
            b.I("vector", lambda v: v.tensor_tensor(
                self.sqb[0:16, 0:512], self.sqtmp[0:16, 0:512],
                self.pb[4][0:16, 0:512], ALU.mult), bs[4])
            b.F("vector", s_v)
            g_ap = self.ar16[96:112, c0:c0 + 512]
            b.W("scalar", s_v, s_v.n)
            b.I("scalar", lambda s, g=g_ap: s.activation(
                g, self.sqb[0:16, 0:512], AF.Sigmoid), s_sc)
        b.F("scalar", s_sc)
        self.SC1 = s_sc.n
        for tb in range(cfg.nblk):
            self.xn_block(tb, dst=self.arn(4096 * tb, 4096))
        self.V_XN2 = self.s_v.n
        kfb = self.arn(16384, 512)
        vtb = self.arn(16896, 512)
        b.W("tensor", self.s_g, self.G_OCB)
        b.W("vector", s_sc, s_sc.n)
        b.E("vector", lambda v: v.memset(self.kvz[:, :], 0.0))
        b.F("vector", s_v)
        for tb in range(cfg.nblk):
            for j in range(8):
                xp_ap = self.arn(4096 * tb + 512 * j, 512)
                self.bwait(0)
                b.W("tensor", self.WKV[0][0], self.WKV[0][1])
                b.W("tensor", self.WKV[1][0], self.WKV[1][1])
                b.W("tensor", s_v, self.V_XN2)
                b.I("tensor", lambda t, j=j, x=xp_ap: t.matmul(
                    self.pb[0][0:64, 0:512], self.wkv[0:64, 64 * j:64 * (j + 1)],
                    x[0:64, :], start=True, stop=True, tile_position=(0, 0),
                    skip_group_check=True), s_t)
                b.I("tensor", lambda t, j=j, x=xp_ap: t.matmul(
                    self.pb[0][64:128, 0:512], self.wkv[64:128, 64 * j:64 * (j + 1)],
                    x[64:128, :], start=True, stop=True, tile_position=(64, 64),
                    skip_group_check=True), s_t)
                self.bwait(1)
                b.I("tensor", lambda t, j=j, x=xp_ap: t.matmul(
                    self.pb[1][0:64, 0:512], self.wkv[0:64, 512 + 64 * j:512 + 64 * (j + 1)],
                    x[0:64, :], start=True, stop=True, tile_position=(0, 0),
                    skip_group_check=True), s_t)
                b.I("tensor", lambda t, j=j, x=xp_ap: t.matmul(
                    self.pb[1][64:128, 0:512], self.wkv[64:128, 512 + 64 * j:512 + 64 * (j + 1)],
                    x[64:128, :], start=True, stop=True, tile_position=(64, 64),
                    skip_group_check=True), s_t)
                tkv = s_t.n
                b.W("vector", s_t, tkv)
                b.W("vector", s_sc, s_sc.n)
                b.E("vector", lambda v: v.tensor_scalar(
                    self.sqtmp[:, :], self.pb[0][:, 0:512], 0.0, 0.0,
                    ALU.min, ALU.add))
                b.D("vector")
                b.I("vector", lambda v, d=vtb: v.tensor_copy(d, self.pb[1][:, 0:512]),
                    bs[1])
                b.F("vector", s_v)
                vmin = s_v.n
                b.W("scalar", s_v, vmin)
                b.I("scalar", lambda s: s.activation(
                    self.sqtmp[:, :], self.sqtmp[:, :], AF.Exp), s_sc)
                b.D("scalar")
                b.I("scalar", lambda s: s.activation(
                    self.cext[:, 0:512], self.pb[0][:, 0:512], AF.Relu), s_sc)
                b.I("scalar", lambda s: s.drain(), bs[0])
                b.W("vector", s_sc, s_sc.n)
                b.I("vector", lambda v, d=kfb: v.tensor_tensor(
                    d, self.sqtmp[:, :], self.cext[:, 0:512], ALU.add), s_v)
                b.F("vector", s_v)
                vkf = s_v.n
                for st in range(4):
                    ktm = self.arn(17408 + 128 * (st % 2), 128)
                    vtm = self.arn(17664 + 128 * (st % 2), 128)
                    self.bwait(6)
                    t6 = self.TRg(self.pb[6][:, 0:128], kfb[:, 128 * st:128 * (st + 1)],
                                  self.ident[:, :], waits=((s_v, vkf),))
                    self.bwait(7)
                    t7 = self.TRg(self.pb[7][:, 0:128], vtb[:, 128 * st:128 * (st + 1)],
                                  self.ident[:, :])
                    b.W("vector", s_t, t7)
                    b.I("vector", lambda v, d=ktm: v.tensor_copy(d, self.pb[6][:, 0:128]),
                        bs[6])
                    b.D("vector")
                    b.I("vector", lambda v, d=vtm: v.tensor_copy(d, self.pb[7][:, 0:128]),
                        bs[7])
                    b.F("vector", s_v)
                    vst = s_v.n
                    if st == 0:
                        self.bwait(4)
                        self.bwait(5)
                    b.W("tensor", s_v, vst)
                    b.I("tensor", lambda t, j=j, ktm=ktm, vtm=vtm, st=st:
                        t.matmul(self.pb[5][0:64, 64 * j:64 * (j + 1)], ktm[:, 0:64],
                                 vtm[:, 0:64], start=(st == 0), stop=(st == 3),
                                 tile_position=(0, 0), skip_group_check=True), s_t)
                    b.I("tensor", lambda t, j=j, ktm=ktm, vtm=vtm, st=st:
                        t.matmul(self.pb[5][64:128, 64 * j:64 * (j + 1)], ktm[:, 64:128],
                                 vtm[:, 64:128], start=(st == 0), stop=(st == 3),
                                 tile_position=(0, 64), skip_group_check=True), s_t)
                    b.I("tensor", lambda t, j=j, ktm=ktm, st=st:
                        t.matmul(self.pb[4][0:64, j:j + 1], ktm[:, 0:64],
                                 self.onescb[:, :], start=(st == 0), stop=(st == 3),
                                 tile_position=(0, 0), skip_group_check=True), s_t)
                    b.I("tensor", lambda t, j=j, ktm=ktm, st=st:
                        t.matmul(self.pb[4][64:128, j:j + 1], ktm[:, 64:128],
                                 self.onescb[:, :], start=(st == 0), stop=(st == 3),
                                 tile_position=(0, 64), skip_group_check=True), s_t)
                tacc = s_t.n
                b.W("vector", s_t, tacc)
                b.E("vector", lambda v, j=j: v.tensor_tensor(
                    self.kvz[:, 64 * j:64 * (j + 1)], self.kvz[:, 64 * j:64 * (j + 1)],
                    self.pb[5][:, 64 * j:64 * (j + 1)], ALU.add))
                b.I("vector", lambda v: v.drain(), bs[5])
                b.E("vector", lambda v, j=j: v.tensor_tensor(
                    self.kvz[:, 512 + j:513 + j], self.kvz[:, 512 + j:513 + j],
                    self.pb[4][:, j:j + 1], ALU.add))
                b.I("vector", lambda v: v.drain(), bs[4])
                b.F("vector", s_v)
        self.V_KVL = self.s_v.n

    def ph8_ag2(self):
        b, cfg = self.b, self.cfg
        cc1_in, cc1_out = self.ccs[2], self.ccs[3]
        s_t, s_v, s_sc = self.s_t, self.s_v, self.s_sc
        bs = self.bsem
        s_ag = b.sem("s_ag2")
        s_agl = b.sem("s_agl2")
        b.W("sync", s_v, self.V_KVL)
        b.I("sync", lambda e: e.dma_start(out=cc1_in.ap()[:, :], in_=self.kvz[:, :]),
            s_ag, k=16)
        b.W("gpsimd", s_ag, s_ag.n)
        b.I("gpsimd", lambda g: g.collective_compute(
            "AllGather", ALU.bypass, replica_groups=[list(range(NCORE))],
            ins=[cc1_in.ap().opt()], outs=[cc1_out.ap().opt()]), self.s_cc)
        CC1 = self.s_cc.n
        b.W("vector", s_ag, s_ag.n)
        b.E("vector", lambda v: v.memset(self.kvz[:, :], 0.0))
        b.F("vector", s_v)
        b.W("sync", self.s_cc, CC1)
        for l in range(NCORE):
            b.W("sync", s_v, s_v.n)
            b.I("sync", lambda e, l=l: e.dma_start(
                out=self.kvtmp, in_=cc1_out.ap()[128 * l:128 * (l + 1), :]),
                s_agl, k=16)
            b.W("vector", s_agl, s_agl.n)
            b.E("vector", lambda v, l=l: v.scalar_tensor_tensor(
                self.kvz[:, :], self.kvtmp, self.nubc[:, l:l + 1], self.kvz[:, :],
                ALU.mult, ALU.add))
            b.F("vector", s_v)
        b.E("vector", lambda v: v.tensor_copy(self.kvzb[:, :], self.kvz[:, :]))
        b.D("vector")
        b.E("vector", lambda v: v.memset(self.zbd[:, :], 0.0))
        b.D("vector")
        for j in range(8):
            b.E("vector", lambda v, j=j: v.tensor_copy(
                self.zbd[0:64, 2 * j:2 * j + 1], self.kvzb[0:64, 512 + j:513 + j]))
            b.D("vector")
            b.E("vector", lambda v, j=j: v.tensor_copy(
                self.zbd[64:128, 2 * j + 1:2 * j + 2], self.kvzb[64:128, 512 + j:513 + j]))
            b.D("vector")
        b.F("vector", s_v)
        self.bwait(0)
        self.MMg(self.pb[0][:, 0:16], self.onesall[0:1, 0:128], self.ebrow[0:1, 0:16],
                 waits=((s_v, self.V_NEGAL),))
        b.W("vector", s_t, s_t.n)
        b.I("vector", lambda v: v.tensor_copy(self.ebb[:, 0:16], self.pb[0][:, 0:16]),
            bs[0])
        b.F("vector", s_v)
        self.V_KVZ = s_v.n

    def ph9_l1_blocks(self):
        b, cfg = self.b, self.cfg
        tpc = cfg.tpc
        s_t, s_v, s_sc = self.s_t, self.s_v, self.s_sc
        bs = self.bsem
        b.W("tensor", s_sc, self.SC1)
        b.W("vector", s_v, self.V_KVZ)
        self.route_tiles(96, E1, 0, use_bias=True, probs=False)
        b.F("tensor", s_t)
        wsl = WS(self, [self.arn(36864 + 1024 * i, 1024) for i in range(6)])
        wsl.barrier((s_t, s_t.n))
        tokq = self.wq.load(self.wq_sb[:, :], self.dr["w_wq"].ap(),
                            pre_wait=(s_t, s_t.n))
        self.WQTOK = tokq
        qfb = self.arn(43520, 512)
        ctx = lambda j: self.arn(16384 + 512 * j, 512)
        for tb in range(cfg.nblk):
            c0 = 512 * tb
            for j in range(8):
                xp_ap = self.arn(4096 * tb + 512 * j, 512)
                self.bwait(0)
                b.W("tensor", self.WQTOK[0], self.WQTOK[1])
                b.I("tensor", lambda t, j=j, x=xp_ap: t.matmul(
                    self.pb[0][0:64, 0:512], self.wq_sb[0:64, 64 * j:64 * (j + 1)],
                    x[0:64, :], start=True, stop=True, tile_position=(0, 0),
                    skip_group_check=True), s_t)
                b.I("tensor", lambda t, j=j, x=xp_ap: t.matmul(
                    self.pb[0][64:128, 0:512], self.wq_sb[64:128, 64 * j:64 * (j + 1)],
                    x[64:128, :], start=True, stop=True, tile_position=(64, 64),
                    skip_group_check=True), s_t)
                tq = s_t.n
                b.W("vector", s_t, tq)
                b.W("vector", s_sc, s_sc.n)
                b.E("vector", lambda v: v.tensor_scalar(
                    self.sqtmp[:, :], self.pb[0][:, 0:512], 0.0, 0.0,
                    ALU.min, ALU.add))
                b.F("vector", s_v)
                vmin = s_v.n
                b.W("scalar", s_v, vmin)
                b.I("scalar", lambda s: s.activation(
                    self.sqtmp[:, :], self.sqtmp[:, :], AF.Exp), s_sc)
                b.D("scalar")
                b.I("scalar", lambda s: s.activation(
                    self.cext[:, 0:512], self.pb[0][:, 0:512], AF.Relu), s_sc)
                b.I("scalar", lambda s: s.drain(), bs[0])
                b.W("vector", s_sc, s_sc.n)
                b.I("vector", lambda v, d=qfb: v.tensor_tensor(
                    d, self.sqtmp[:, :], self.cext[:, 0:512], ALU.add), s_v)
                b.F("vector", s_v)
                vqf = s_v.n
                self.bwait(1)
                b.W("tensor", s_v, vqf)
                b.I("tensor", lambda t, j=j: t.matmul(
                    self.pb[1][0:64, 0:512], self.kvzb[0:64, 64 * j:64 * (j + 1)],
                    qfb[0:64, :], start=True, stop=True, tile_position=(0, 0),
                    skip_group_check=True), s_t)
                b.I("tensor", lambda t, j=j: t.matmul(
                    self.pb[1][64:128, 0:512], self.kvzb[64:128, 64 * j:64 * (j + 1)],
                    qfb[64:128, :], start=True, stop=True, tile_position=(64, 64),
                    skip_group_check=True), s_t)
                self.bwait(2)
                b.I("tensor", lambda t, j=j: t.matmul(
                    self.pb[2][0:2, 0:512], self.zbd[:, 2 * j:2 * j + 2],
                    qfb[:, :], start=True, stop=True, skip_group_check=True), s_t)
                tden = s_t.n
                b.W("scalar", s_t, tden)
                b.W("scalar", s_v, s_v.n)
                b.I("scalar", lambda s: s.activation(
                    self.l2row[0:2, 0:512], self.pb[2][0:2, 0:512], AF.Abs,
                    bias=self.cst(EPS_DEN, 2)), s_sc)
                b.I("scalar", lambda s: s.drain(), bs[2])
                b.W("vector", s_sc, s_sc.n)
                b.E("vector", lambda v: v.reciprocal(self.l2row[0:2, 0:512],
                                                     self.l2row[0:2, 0:512]))
                b.F("vector", s_v)
                vden = s_v.n
                self.bwait(3)
                self.MMg(self.pb[3][:, 0:512], self.ones2T[:, :],
                         self.l2row[0:2, 0:512], waits=((s_v, vden),))
                tbc = s_t.n
                b.W("vector", s_t, tbc)
                b.E("vector", lambda v: v.tensor_copy(
                    self.sqtmp[:, :], self.pb[1][:, 0:512]))
                b.I("vector", lambda v: v.drain(), bs[1])
                b.I("vector", lambda v, d=ctx(j): v.tensor_tensor(
                    d, self.sqtmp[:, :], self.pb[3][:, 0:512], ALU.mult), bs[3])
                b.F("vector", s_v)
            VCTX = s_v.n
            img = self.dr["w_wo"]
            for oc in range(DCH):
                j2, buf, tok = wsl.get(img[:, oc * DCH * 128:(oc + 1) * DCH * 128])
                bank = 4 + oc % 2
                b.W("tensor", tok[0], tok[1])
                b.W("tensor", s_v, VCTX)
                self.bwait(bank)
                for dcp in range(DCH):
                    b.I("tensor", lambda t, buf=buf, dcp=dcp, bank=bank: t.matmul(
                        self.pb[bank][:, 0:512], buf[:, dcp * 128:(dcp + 1) * 128],
                        self.arn(16384 + 512 * dcp, 512),
                        start=(dcp == 0), stop=(dcp == DCH - 1),
                        skip_group_check=True), s_t)
                wsl.release(j2)
                two = s_t.n
                xr_ap = self.xres[:, oc * tpc + c0:oc * tpc + c0 + 512]
                b.W("vector", s_t, two)
                b.D("vector")
                b.I("vector", lambda v, x=xr_ap, bank=bank: v.tensor_tensor(
                    x, x, self.pb[bank][:, 0:512], ALU.add), bs[bank])
            b.F("vector", s_v)
        s1i, s2i, s3i = self.dr["w_s1"], self.dr["w_s2"], self.dr["w_s3"]
        for q4 in range(4):
            self.moe_pass(wsl, lambda oc8, q4=q4: s1i[:, (8 * q4 + oc8) * 1024:
                                                      (8 * q4 + oc8 + 1) * 1024],
                          lambda oc8, q4=q4: s2i[:, (8 * q4 + oc8) * 1024:
                                                  (8 * q4 + oc8 + 1) * 1024],
                          lambda oc2, q4=q4: s3i[:, oc2 * 4096 + q4 * 1024:
                                                  oc2 * 4096 + q4 * 1024 + 1024],
                          webc=False)
        r1i, r2i, r3i = self.dr["w_r1"], self.dr["w_r2"], self.dr["w_r3"]
        for e in range(E1):
            self.webc_build(e, E1, 0)
            self.moe_pass(wsl, lambda oc8, e=e: r1i.ap()[e][:, oc8 * 1024:(oc8 + 1) * 1024],
                          lambda oc8, e=e: r2i.ap()[e][:, oc8 * 1024:(oc8 + 1) * 1024],
                          lambda oc2, e=e: r3i.ap()[e][:, oc2 * 1024:(oc2 + 1) * 1024],
                          webc=True)
        b.F("vector", s_v)
        self.V_L1 = s_v.n

    def moe_pass(self, wsl, f1, f2, f3, webc):
        b, cfg = self.b, self.cfg
        tpc = cfg.tpc
        s_t, s_v, s_sc = self.s_t, self.s_v, self.s_sc
        bs = self.bsem
        for oc8 in range(8):
            bk1 = 2 * (oc8 % 2)
            bk2 = 1 + 2 * (oc8 % 2)
            j1, b1, tk1 = wsl.get(f1(oc8))
            j2, b2, tk2 = wsl.get(f2(oc8))
            b.W("tensor", tk1[0], tk1[1])
            b.W("tensor", tk2[0], tk2[1])
            for tb in range(cfg.nblk):
                self.bwait(bk1)
                self.bwait(bk2)
                for dc in range(DCH):
                    b.I("tensor", lambda t, b1=b1, dc=dc, tb=tb, bk1=bk1: t.matmul(
                        self.pb[bk1][:, 0:512], b1[:, dc * 128:(dc + 1) * 128],
                        self.arn(4096 * tb + 512 * dc, 512),
                        start=(dc == 0), stop=(dc == DCH - 1),
                        skip_group_check=True), s_t)
                for dc in range(DCH):
                    b.I("tensor", lambda t, b2=b2, dc=dc, tb=tb, bk2=bk2: t.matmul(
                        self.pb[bk2][:, 0:512], b2[:, dc * 128:(dc + 1) * 128],
                        self.arn(4096 * tb + 512 * dc, 512),
                        start=(dc == 0), stop=(dc == DCH - 1),
                        skip_group_check=True), s_t)
                th = s_t.n
                b.W("scalar", s_t, th)
                b.W("scalar", s_v, s_v.n)
                b.I("scalar", lambda s, bk1=bk1: s.activation(
                    self.sqtmp[:, :], self.pb[bk1][:, 0:512], AF.Silu), s_sc)
                b.I("scalar", lambda s: s.drain(), bs[bk1])
                b.W("vector", s_sc, s_sc.n)
                b.I("vector", lambda v, tb=tb, oc8=oc8, bk2=bk2: v.tensor_tensor(
                    self.arn(20480 + (8 * tb + oc8) * 512, 512), self.sqtmp[:, :],
                    self.pb[bk2][:, 0:512], ALU.mult), bs[bk2])
                b.F("vector", s_v)
            wsl.release(j1)
            wsl.release(j2)
        VHID = s_v.n
        for oc2 in range(DCH):
            j3, b3, tk3 = wsl.get(f3(oc2))
            b.W("tensor", tk3[0], tk3[1])
            b.W("tensor", s_v, VHID)
            for tb in range(cfg.nblk):
                bank = 4 + (oc2 + tb) % 2
                self.bwait(bank)
                for oc8 in range(8):
                    b.I("tensor", lambda t, b3=b3, oc8=oc8, tb=tb, bank=bank: t.matmul(
                        self.pb[bank][:, 0:512], b3[:, oc8 * 128:(oc8 + 1) * 128],
                        self.arn(20480 + (8 * tb + oc8) * 512, 512),
                        start=(oc8 == 0), stop=(oc8 == 7),
                        skip_group_check=True), s_t)
                tdn = s_t.n
                c0 = 512 * tb
                xr_ap = self.xres[:, oc2 * tpc + c0:oc2 * tpc + c0 + 512]
                b.W("vector", s_t, tdn)
                if webc:
                    b.W("vector", s_sc, s_sc.n)
                    b.E("vector", lambda v, tb=tb, bank=bank: v.tensor_tensor(
                        self.sqb[:, 0:512], self.pb[bank][:, 0:512],
                        self.webc_v(tb), ALU.mult))
                    b.I("vector", lambda v, bank=bank: v.drain(), bs[bank])
                    b.I("vector", lambda v, x=xr_ap: v.tensor_add(
                        x, x, self.sqb[:, 0:512]), self.s_fin)
                else:
                    b.D("vector")
                    b.I("vector", lambda v, x=xr_ap, bank=bank: v.tensor_tensor(
                        x, x, self.pb[bank][:, 0:512], ALU.add), bs[bank])
                b.F("vector", s_v)
            wsl.release(j3)

    def dump(self, name, ap, width, dtype, parts=128):
        if name not in self.dbg:
            return
        b = self.b
        if not hasattr(self, "s_dbg"):
            self.s_dbg = b.sem("s_dbg")
        d = self.nc.dram_tensor("dbg_" + name, [parts, width], dtype,
                                kind="ExternalOutput")
        b.F("vector", self.s_v)
        b.F("scalar", self.s_sc)
        b.W("sync", self.s_v, self.s_v.n)
        b.W("sync", self.s_sc, self.s_sc.n)
        b.I("sync", lambda e, d=d, ap=ap: e.dma_start(out=d.ap(), in_=ap),
            self.s_dbg, k=16)
        b.W("vector", self.s_dbg, self.s_dbg.n)
        b.W("scalar", self.s_dbg, self.s_dbg.n)
        b.W("gpsimd", self.s_dbg, self.s_dbg.n)

    def ph_out(self):
        b, cfg = self.b, self.cfg
        tpc, ntile = cfg.tpc, cfg.ntile
        s_t, s_v = self.s_t, self.s_v
        s_out = [b.sem("out0"), b.sem("out1")]
        s_oq = b.sem("oq")
        b.F("vector", s_v)
        XF = s_v.n
        b.W("tensor", s_v, XF)
        buffree = [None, None]
        for i in range(ntile):
            jb = i % 2
            buf = self.xtm[jb]
            for dc in range(DCH):
                bank = dc % 2
                self.bwait(bank)
                b.I("tensor", lambda t, i=i, dc=dc, bank=bank: t.transpose(
                    self.pb[bank][:, 0:128],
                    self.xres[:, dc * tpc + 128 * i:dc * tpc + 128 * (i + 1)],
                    self.identf[:, :]), s_t)
                tn = s_t.n
                b.W("vector", s_t, tn)
                if dc < 2 and buffree[jb] is not None:
                    b.W("vector", buffree[jb][0], buffree[jb][1])
                b.I("vector", lambda v, buf=buf, dc=dc, bank=bank: v.tensor_copy(
                    buf[:, 128 * dc:128 * (dc + 1)], self.pb[bank][:, 0:128]),
                    self.bsem[bank])
            b.F("vector", s_v)
            vst = s_v.n
            b.W("sync", s_v, vst)
            b.I("sync", lambda e, i=i, buf=buf: e.dma_start(
                out=self.y.ap()[128 * i:128 * (i + 1), :], in_=buf[:, :]),
                s_out[jb], k=16)
            buffree[jb] = (s_out[jb], s_out[jb].n)
        b.W("vector", s_out[0], s_out[0].n)
        b.W("vector", s_out[1], s_out[1].n)
        if hasattr(self, "s_dbg"):
            b.W("vector", self.s_dbg, self.s_dbg.n)
        b.E("vector", lambda v: v.memset(self.onrow[:, :], 0.0))


def _img_lhsT(w):
    din, dout = w.shape
    ndc, noc = (din + 127) // 128, (dout + 127) // 128
    padr = ndc * 128 - din
    padc = noc * 128 - dout
    if padr or padc:
        w2 = np.zeros((ndc * 128, noc * 128), w.dtype)
        w2[:din, :dout] = w
        w = w2
    img = w.reshape(ndc, 128, noc, 128).transpose(1, 2, 0, 3).reshape(128, noc * ndc * 128)
    return np.ascontiguousarray(img)


def prep_weights(I):
    W = {}
    W["w_norm"] = np.ascontiguousarray(np.asarray(I["L0_attn_norm_w"]).reshape(DCH, 128).T)
    W["w_pnw"] = np.ascontiguousarray(np.asarray(I["L0_attn_post_norm_w"]).reshape(DCH, 128).T)
    W["w_q"] = _img_lhsT(I["L0_attn_q_w"])
    W["w_k"] = _img_lhsT(I["L0_attn_k_w"])
    W["w_v"] = _img_lhsT(I["L0_attn_v_w"])
    ba = np.concatenate([I["L0_attn_b_w"], I["L0_attn_a_w"]], 1)
    W["w_ba"] = _img_lhsT(ba)
    conv = np.concatenate([I["L0_attn_conv_q_w"], I["L0_attn_conv_k_w"],
                           I["L0_attn_conv_v_w"]], 0)
    W["w_conv"] = np.ascontiguousarray(
        conv.reshape(-1, 128, 4).transpose(1, 0, 2).reshape(128, -1))
    W["w_Alog"] = np.ascontiguousarray(np.asarray(I["L0_attn_A_log"])[:, None])
    W["w_dtb"] = np.ascontiguousarray(np.asarray(I["L0_attn_dt_bias"])[:, None])
    W["w_g"] = _img_lhsT(I["L0_attn_g_w"])
    W["w_onw"] = np.ascontiguousarray(np.asarray(I["L0_attn_o_norm_w"])[:, None])
    W["w_o"] = _img_lhsT(I["L0_attn_o_w"])
    W["w_gate"] = _img_lhsT(I["L0_attn_gate_w"])
    W["w_up"] = _img_lhsT(I["L0_attn_up_w"])
    W["w_down"] = _img_lhsT(I["L0_attn_down_w"])
    W["w_rt0"] = _img_lhsT(np.asarray(I["L0_ffn_Wg"]).T)
    W["w_m1"] = np.stack([_img_lhsT(np.asarray(I["L0_ffn_W1"])[e].T) for e in range(E0)])
    W["w_m2"] = np.stack([_img_lhsT(np.asarray(I["L0_ffn_W2"])[e].T) for e in range(E0)])
    W["w_m3"] = np.stack([_img_lhsT(np.asarray(I["L0_ffn_W3"])[e].T) for e in range(E0)])
    for nm, key in (("w_wq", "L1_attn_Wq"), ("w_wk", "L1_attn_Wk"), ("w_wv", "L1_attn_Wv")):
        w = np.asarray(I[key])
        img = np.zeros((128, (NH // 2) * DKH), np.float32)
        for j in range(NH // 2):
            img[0:64, j * 64:(j + 1) * 64] = w[2 * j]
            img[64:128, j * 64:(j + 1) * 64] = w[2 * j + 1]
        W[nm] = img
    W["w_wo"] = _img_lhsT(I["L1_attn_Wo"])
    W["w_rt1"] = _img_lhsT(I["L1_ffn_router_w"])
    W["w_eb"] = np.ascontiguousarray(np.asarray(I["L1_ffn_expert_bias"])[None, :])
    W["w_s1"] = _img_lhsT(np.asarray(I["L1_ffn_shared_w1"])[0])
    W["w_s2"] = _img_lhsT(np.asarray(I["L1_ffn_shared_w2"])[0])
    W["w_s3"] = _img_lhsT(np.asarray(I["L1_ffn_shared_w3"])[0])
    W["w_r1"] = np.stack([_img_lhsT(np.asarray(I["L1_ffn_routed_w1"])[e]) for e in range(E1)])
    W["w_r2"] = np.stack([_img_lhsT(np.asarray(I["L1_ffn_routed_w2"])[e]) for e in range(E1)])
    W["w_r3"] = np.stack([_img_lhsT(np.asarray(I["L1_ffn_routed_w3"])[e]) for e in range(E1)])
    return {k: np.ascontiguousarray(np.asarray(v, np.float32)) for k, v in W.items()}


def make_inmaps(inputs, cfg):
    x = np.asarray(inputs["L0_attn_hidden_states"])
    Bb, T, _ = x.shape
    nsc = NCORE // Bb
    W = prep_weights(inputs)
    in_maps = []
    for c in range(NCORE):
        seq, pos = c // nsc, c % nsc
        s = pos * cfg.tpc
        xs = np.ascontiguousarray(x[seq, s:s + cfg.tpc])
        xp = np.zeros((4, D), np.float32)
        if pos > 0:
            xp[0:3] = x[seq, s - 3:s]
        mu = np.zeros(NCORE, np.float32)
        nu = np.zeros(NCORE, np.float32)
        for l in range(NCORE):
            if l // nsc == seq:
                nu[l] = 1.0
                if l % nsc < pos:
                    mu[l] = 1.0
        cm = np.stack([mu, nu]).astype(np.float32)
        m = {"x": xs, "xp": xp, "cmask": cm}
        m.update(W)
        in_maps.append(m)
    return in_maps


_CFG = Cfg(2048)
_NC_CACHE = {}


def kernel(**inputs):
    cfg = _CFG
    in_maps = make_inmaps(inputs, cfg)
    key = cfg.tpc
    if key not in _NC_CACHE:
        _NC_CACHE[key] = build(cfg)
    nc = _NC_CACHE[key]
    res = run_bass_kernel_spmd(nc, in_maps, core_ids=list(range(NCORE)))
    x = np.asarray(inputs["L0_attn_hidden_states"])
    Bb, T, _ = x.shape
    nsc = NCORE // Bb
    out = np.empty((Bb, T, D), np.float32)
    for c in range(NCORE):
        seq, pos = c // nsc, c % nsc
        out[seq, pos * cfg.tpc:(pos + 1) * cfg.tpc] = res.results[c]["y"]
    return out


# version 05
# speedup vs baseline: 1.0619x; 1.0619x over previous
import numpy as np
import concourse.bass as bass
import concourse.mybir as mybir
from concourse.bass_utils import run_bass_kernel_spmd
from contextlib import ExitStack

F32 = mybir.dt.float32
BF16 = mybir.dt.bfloat16
AF = mybir.ActivationFunctionType
ALU = mybir.AluOpType
AX = mybir.AxisListType

D = 1024
NH = 16
DKH = 64
DVH = 128
DQK = 1024
DVAL = 2048
FF = 2752
E0, H0 = 8, 2048
E1, RH = 16, 1024
SH = 4096
NCORE = 8
EPS_LN = 1e-5
EPS_RMS = 1e-6
EPS_L2 = 1e-6
EPS_DEN = 1e-6
SCALE_DK = DKH ** -0.5
DCH = 8
VCH = 16
FFCH = 22
FF_LAST = FF - 21 * 128
SHCH = SH // 128
RHCH = RH // 128


class Cfg:
    def __init__(self, tpc=2048):
        assert tpc % 512 == 0
        self.tpc = tpc
        self.nblk = tpc // 512
        self.nch = tpc // 128
        self.ntile = tpc // 128


class Sem:
    __slots__ = ("h", "n")

    def __init__(self, h):
        self.h = h
        self.n = 0


class B:
    ENGS = ("sync", "tensor", "vector", "scalar", "gpsimd")

    def __init__(self, nc, cfg):
        self.nc = nc
        self.cfg = cfg
        self.ops = {e: [] for e in self.ENGS}
        self.ctx = ExitStack()

    def sem(self, name):
        return Sem(self.ctx.enter_context(self.nc.semaphore(name)))

    def sb(self, name, shape, dtype):
        return self.nc.alloc_sbuf_tensor(name, shape, dtype)

    def ps(self, name, shape, dtype=F32):
        return self.nc.alloc_psum_tensor(name, shape, dtype)

    def E(self, eng, fn):
        self.ops[eng].append(fn)

    def I(self, eng, fn, sem, k=1):
        sem.n += k

        def wrapped(e, fn=fn, h=sem.h, k=k):
            fn(e).then_inc(h, k)
        self.ops[eng].append(wrapped)

    def W(self, eng, sem, n=None):
        n = sem.n if n is None else n
        self.ops[eng].append(lambda e, h=sem.h, n=n: e.wait_ge(h, n))

    def D(self, eng):
        self.ops[eng].append(lambda e: e.drain())

    def F(self, eng, sem, k=1):
        sem.n += k
        self.ops[eng].append(lambda e, h=sem.h, k=k: e.drain().then_inc(h, k))

    def finalize(self, block):
        for name in self.ENGS:
            fns = self.ops[name]

            def runner(e, fns=fns):
                for f in fns:
                    f(e)
            getattr(block, name)(runner)


class WQ:

    def __init__(self, b, name, nrot=6):
        self.b = b
        self.sems = [b.sem(f"{name}{i}") for i in range(nrot)]
        self.i = 0

    def load(self, dst_ap, src_ap, pre_wait=None):
        s = self.sems[self.i % len(self.sems)]
        self.i += 1
        bb = self.b
        if pre_wait is not None:
            bb.W("gpsimd", pre_wait[0], pre_wait[1])
        bb.W("gpsimd", s, s.n)
        s.n += 16

        def fn(g, dst=dst_ap, src=src_ap, h=s.h):
            g.dma_start(out=dst, in_=src).then_inc(h, 16)
        bb.E("gpsimd", fn)
        return (s, s.n)


class WS:

    def __init__(self, K, views):
        self.K = K
        self.bufs = views
        self.free = [None] * len(views)
        self.i = 0

    def get(self, src_ap):
        j = self.i % len(self.bufs)
        self.i += 1
        tok = self.K.wq.load(self.bufs[j], src_ap, pre_wait=self.free[j])
        return j, self.bufs[j], tok

    def release(self, j):
        K = self.K
        K.b.F("tensor", K.s_t)
        self.free[j] = (K.s_t, K.s_t.n)

    def barrier(self, tok):
        self.free = [tok] * len(self.bufs)


def build(cfg: Cfg, debug=()):
    nc = bass.Bass("TRN2", target_bir_lowering=False)
    tpc = cfg.tpc
    dr = {}

    def din(name, shape, dtype=F32):
        dr[name] = nc.dram_tensor(name, shape, dtype, kind="ExternalInput")
        return dr[name]

    din("x", [tpc, D])
    din("xp", [4, D])
    din("cmask", [2, NCORE])
    din("w_norm", [128, DCH])
    din("w_pnw", [128, DCH])
    din("w_q", [128, DCH * DCH * 128])
    din("w_k", [128, DCH * DCH * 128])
    din("w_v", [128, VCH * DCH * 128])
    din("w_ba", [128, DCH * 128])
    din("w_conv", [128, 128])
    din("w_Alog", [NH, 1])
    din("w_dtb", [NH, 1])
    din("w_g", [128, VCH * DCH * 128])
    din("w_onw", [128, 1])
    din("w_o", [128, DCH * VCH * 128])
    din("w_gate", [128, FFCH * DCH * 128])
    din("w_up", [128, FFCH * DCH * 128])
    din("w_down", [128, DCH * FFCH * 128])
    din("w_rt0", [128, DCH * 128])
    din("w_m1", [E0, 128, (H0 // 128) * DCH * 128])
    din("w_m2", [E0, 128, (H0 // 128) * DCH * 128])
    din("w_m3", [E0, 128, DCH * (H0 // 128) * 128])
    din("w_wq", [128, (NH // 2) * DKH])
    din("w_wk", [128, (NH // 2) * DKH])
    din("w_wv", [128, (NH // 2) * DKH])
    din("w_wo", [128, DCH * DCH * 128])
    din("w_rt1", [128, DCH * 128])
    din("w_eb", [1, E1])
    din("w_s1", [128, SHCH * DCH * 128])
    din("w_s2", [128, SHCH * DCH * 128])
    din("w_s3", [128, DCH * SHCH * 128])
    din("w_r1", [E1, 128, RHCH * DCH * 128])
    din("w_r2", [E1, 128, RHCH * DCH * 128])
    din("w_r3", [E1, 128, DCH * RHCH * 128])

    y = nc.dram_tensor("y", [tpc, D], F32, kind="ExternalOutput")
    oloc_d = nc.dram_tensor("oloc_d", [cfg.nch, 128, NH * 128], BF16)
    g_d = nc.dram_tensor("g_d", [cfg.nch, 128, 8 * 128], BF16)
    cc0_in = nc.dram_tensor("cc0_in", [NH * DKH, DKH + DVH], F32)
    cc0_out = nc.dram_tensor("cc0_out", [NCORE * NH * DKH, DKH + DVH], F32)
    cc1_in = nc.dram_tensor("cc1_in", [128, 8 * 64 + 8], F32)
    cc1_out = nc.dram_tensor("cc1_out", [NCORE * 128, 8 * 64 + 8], F32)

    dbg = set(debug)

    b = B(nc, cfg)
    with b.ctx:
        K = Kern(nc, b, cfg, dr, y, oloc_d, g_d, (cc0_in, cc0_out, cc1_in, cc1_out), dbg)
        K.run()
        with nc.Block() as block:
            b.finalize(block)
    return nc


class Kern:
    def __init__(self, nc, b, cfg, dr, y, oloc_d, g_d, ccs, dbg):
        self.nc, self.b, self.cfg, self.dr = nc, b, cfg, dr
        self.y, self.oloc_d, self.g_d, self.ccs, self.dbg = y, oloc_d, g_d, ccs, dbg
        tpc = cfg.tpc
        self.xres = b.sb("xres", [128, DCH * tpc], F32)
        self.arsl = b.sb("arsl", [128, tpc], F32)
        self.ar16 = b.sb("ar16", [128, tpc], F32)
        self.ARN = b.sb("ARN", [128, 45312], BF16)
        A = self.ARN
        self.hb = A[:, 0:4096]
        self.qb = A[:, 4096:8192]
        self.kb = A[:, 8192:12288]
        self.vb = A[:, 12288:20480]
        self.wp = A[:, 20480:28672]
        self.obc = A[:, 28672:30720]
        self.gst = A[:, 30720:32768]
        self.sel16 = A[0:16, 32768:33792]
        self.oneh16 = A[0:16, 33792:35840]
        self.PT = A[:, 35840:36608]
        self.Pti = A[:, 36608:36736]
        self.RHS = A[:, 36736:36928]
        self.Ybuf = A[:, 36928:37312]
        self.Mt_bf = A[:, 37312:37440]
        self.Mti_bf = A[:, 37440:37568]
        self.At_bf = A[:, 37568:37696]
        self.Wf2 = A[:, 37696:37824]
        self.Qt2 = A[:, 37824:37952]
        self.Kb2 = A[:, 37952:38080]
        self.Z2 = A[:, 38080:38208]
        self.Kbt = A[:, 38208:38272]
        self.AchT = A[:, 38272:38336]
        self.bexp = A[0:16, 38336:38464]
        self.eg_b = A[0:16, 38464:38592]
        self.edec = A[0:16, 38592:38720]
        self.gbf = A[0:16, 38720:38848]
        self.bbf = A[0:16, 38848:38976]
        self.Ar_bf = A[:, 38976:39488]
        self.h3fm = A[:, 39488:39520]
        self.xnb = A[:, 39552:43648]
        self.AtTb = A[:, 4096:4608]
        self.wkv = A[:, 20480:21504]
        self.onescb = A[:, 21504:21505]
        self.kvzb = A[:, 44544:45064]
        self.zbd = A[:, 45064:45080]
        self.ARNF = b.sb("ARNF", [128, 4096], F32)
        AF_ = self.ARNF
        self.xtm = [AF_[:, 0:1024], AF_[:, 1024:2048]]
        self.xp_t = self.xtm[0][0:4, :]
        self.sqb = AF_[:, 2048:2560]
        self.S_f = AF_[:, 0:1024]
        self.Ar_f = AF_[:, 1024:1536]
        self.Atmp = AF_[:, 1536:2048]
        self.AtT = self.Atmp
        self.AID = AF_[:, 2560:3072]
        self.Btmp = AF_[:, 3072:4096]
        self.kvz = AF_[:, 2048:2568]
        self.kvtmp = AF_[:, 3072:3592]
        wfc = [0]

        def vf(cols, parts=128, dtype=F32):
            a = wfc[0]
            wfc[0] += cols
            return self._WSF[0:parts, a:a + cols]
        self._WSF = b.sb("WSF", [128, 5248], F32)
        self.identf = vf(128)
        self.mask_s = vf(128)
        self.mask_i1 = vf(128)
        self.onesc = vf(1)
        self.onesr = vf(128, 1)
        self.onesall = vf(128)
        self.statall = vf(2 * cfg.ntile)
        self.cstt = vf(16)
        self._cst = {}
        self.gtm = vf(16)
        self.ngtm = vf(16)
        self.gfm = vf(128, 16)
        self.egfm = vf(128, 16)
        self.btm = vf(16)
        self.egCc = vf(1)
        self.Dsb = vf(128)
        self.tmpA = vf(128)
        self.tmpB = vf(128)
        self.o1sb = vf(128)
        self.mksel = vf(128, 2)
        self.nksel = vf(128, 2)
        self.mubc = vf(NCORE)
        self.nubc = vf(NCORE)
        self.otile = vf(128)
        self.onrow = vf(128, 1)
        self.cs16 = vf(1, 16)
        self.ebb = vf(16)
        self.ebrow = vf(16, 1)
        self.mx8 = vf(8)
        self.rtm = vf(32)
        self.rw1 = vf(16)
        self.nw = vf(DCH)
        self.pnw = vf(DCH)
        self.convw = vf(128)
        self.dtb = vf(1, 16)
        self.negal = vf(1, 16)
        self.onw = vf(1)
        self.cmk = vf(NCORE, 2)
        self.ones2 = vf(2)
        self.ones2sc = vf(2)
        self.ones2T = vf(128, 2)
        self.ones2Tsc = vf(128, 2)
        self.cext = vf(516)
        self.ctails = vf(32 * 3)
        self.sqtmp = vf(512)
        self.l2row = vf(512, 2)
        self.xp_s = vf(4, 4)
        self.rt_sb = vf(DCH * 128)
        assert wfc[0] <= 5248, wfc[0]
        self._WSB = b.sb("WSB", [128, 1280], BF16)
        self.ident = self._WSB[:, 0:128]
        self.S_bf = self._WSB[:, 128:1152]
        self.wq_sb = self.ARN[:, 43008:43520]
        self.pb = [b.ps(f"pb{i}", [128, 512]) for i in range(6)]
        self.pbh = [b.ps(f"pbh{i}", [128, 1024], BF16) for i in range(2)]
        self.pb = self.pb + self.pbh
        self.s_t = b.sem("s_t")
        self.s_v = b.sem("s_v")
        self.s_sc = b.sem("s_sc")
        self.s_g = b.sem("s_g")
        self.s_cc = b.sem("s_cc")
        self.bsem = [b.sem(f"bk{i}") for i in range(8)]
        self.wq = WQ(b, "wq", nrot=6)
        self.R_LNM, self.R_LNRS, self.R_RMS = 0, 32, 64

    def arn(self, a, w):
        return self.ARN[:, a:a + w]

    def cst(self, val, parts=128):
        if val not in self._cst:
            k = len(self._cst)
            assert k < 16
            self._cst[val] = k
            self.b.E("gpsimd", lambda g, k=k, v=float(val):
                     g.memset(self.cstt[:, k:k + 1], v))
        return self.cstt[0:parts, self._cst[val]:self._cst[val] + 1]

    def bwait(self, i, eng="tensor"):
        self.b.W(eng, self.bsem[i], self.bsem[i].n)

    def MMg(self, bank_reg, lhsT, rhs, start=True, stop=True, tp=None, waits=()):
        b = self.b
        for sem, n in waits:
            b.W("tensor", sem, n)
        b.I("tensor", lambda t, o=bank_reg, l=lhsT, r=rhs, st=start, sp=stop, tp=tp:
            t.matmul(o, l, r, start=st, stop=sp, tile_position=tp,
                     skip_group_check=True), self.s_t)
        return self.s_t.n

    def TRg(self, bank_reg, in_, ident, tp=None, waits=()):
        b = self.b
        for sem, n in waits:
            b.W("tensor", sem, n)
        b.I("tensor", lambda t, o=bank_reg, i=in_, d=ident, tp=tp:
            t.transpose(o, i, d, tile_position=tp), self.s_t)
        return self.s_t.n

    def xn_block(self, tb, dst=None, also_f32=None):
        b, cfg = self.b, self.cfg
        tpc = cfg.tpc
        c0 = 512 * tb
        dst = self.xnb if dst is None else dst
        b.W("tensor", self.s_v, self.V_SLIVERS)
        self.bwait(3)
        self.MMg(self.pb[3][:, 0:512], self.onesall[0:1, 0:128],
                 self.arsl[0:1, c0:c0 + 512])
        self.bwait(4)
        self.MMg(self.pb[4][:, 0:512], self.onesall[32:33, 0:128],
                 self.arsl[32:33, c0:c0 + 512])
        tn = self.s_t.n
        b.W("vector", self.s_t, tn)
        for dc in range(DCH):
            xr = self.xres[:, dc * tpc + c0:dc * tpc + c0 + 512]
            d_ap = dst[:, dc * 512:(dc + 1) * 512]
            b.D("vector")
            b.E("vector", lambda v, x=xr: v.tensor_add(
                self.sqtmp[:, :], x, self.pb[3][:, 0:512]))
            b.D("vector")
            last = dc == DCH - 1
            b.I("vector", lambda v, d=d_ap: v.tensor_tensor(
                d, self.sqtmp[:, :], self.pb[4][:, 0:512], ALU.mult),
                self.bsem[4] if last else self.s_v)
            if last:
                b.I("vector", lambda v: v.drain(), self.bsem[3])
        b.F("vector", self.s_v)
        return self.s_v.n

    def h_block(self, tb):
        b, cfg = self.b, self.cfg
        c0 = 512 * tb
        b.W("tensor", self.s_v, self.s_v.n)
        self.bwait(4)
        self.MMg(self.pb[4][:, 0:512], self.onesall[64:65, 0:128],
                 self.arsl[64:65, c0:c0 + 512])
        tn = self.s_t.n
        b.W("vector", self.s_t, tn)
        for dc in range(DCH):
            xn_ap = self.xnb[:, dc * 512:(dc + 1) * 512]
            h_ap = self.hb[:, dc * 512:(dc + 1) * 512]
            b.D("vector")
            b.E("vector", lambda v, x=xn_ap, dc=dc: v.tensor_scalar_mul(
                self.sqtmp[:, :], x, self.nw[:, dc:dc + 1]))
            b.D("vector")
            last = dc == DCH - 1
            b.I("vector", lambda v, h=h_ap: v.tensor_tensor(
                h, self.sqtmp[:, :], self.pb[4][:, 0:512], ALU.mult),
                self.bsem[4] if last else self.s_v)
        b.F("vector", self.s_v)
        return self.s_v.n

    def run(self):
        tpc = self.cfg.tpc
        self.ph0_consts()
        self.ph1_load_ln()
        self.ph2_small()
        self.ph2_router0()
        self.ph3_l0_sweep()
        self.dump("qb", self.qb, 4096, BF16)
        self.dump("kb", self.kb, 4096, BF16)
        self.dump("vb", self.vb, 8192, BF16)
        self.dump("ar16", self.ar16[:, :], tpc, F32)
        self.dump("S_f", self.S_f[:, :], 8 * 128, F32)
        self.dump("Ar_f", self.Ar_f[:, :], 8 * 64, F32)
        self.ph4_ag_compose()
        self.dump("Sin", self.S_f[:, :], 8 * 128, F32)
        self.ph5_l0_blocks()
        self.dump("xres1", self.xres[:, :], DCH * tpc, F32)
        self.ph5c_moe0()
        self.dump("xres2", self.xres[:, :], DCH * tpc, F32)
        self.dump("w0rt", self.ar16[:, :], tpc, F32)
        self.ph7_l1_pre()
        self.dump("arsl2", self.arsl[:, :], tpc, F32)
        self.dump("xn2", self.ARN[:, 0:16384], 16384, BF16)
        self.dump("kvz", self.kvz[:, :], 520, F32)
        self.ph8_ag2()
        self.dump("kvzc", self.kvz[:, :], 520, F32)
        self.ph9_l1_blocks()
        self.dump("w1rt", self.ar16[:, :], tpc, F32)
        self.ph_out()

    def ph0_consts(self):
        b = self.b

        def gp(f):
            b.E("gpsimd", f)
        gp(lambda g: g.memset(self.ident[:, :], 1.0))
        gp(lambda g: g.memset(self.identf[:, :], 1.0))
        gp(lambda g: g.memset(self.mask_s[:, :], 1.0))
        gp(lambda g: g.memset(self.mask_i1[:, :], 1.0))
        gp(lambda g: g.memset(self.onesc[:, :], 1.0))
        gp(lambda g: g.memset(self.onesr[:, :], 1.0))
        gp(lambda g: g.memset(self.onesall[:, :], 1.0))
        gp(lambda g: g.memset(self.sel16[:, :], 1.0))
        gp(lambda g: g.memset(self.oneh16[:, :], 1.0))
        gp(lambda g: g.memset(self.arsl[:, :], 0.0))
        gp(lambda g: g.memset(self.ar16[:, :], 0.0))
        gp(lambda g: g.memset(self.S_bf[:, :], 0.0))
        gp(lambda g: g.memset(self.mksel[:, :], 0.0))
        gp(lambda g: g.memset(self.nksel[:, :], 1.0))
        gp(lambda g: g.memset(self.ones2[:, :], 0.0))
        gp(lambda g: g.memset(self.ones2sc[:, :], 0.0))
        b.D("gpsimd")
        gp(lambda g: g.affine_select(out=self.ident[:, :], in_=self.ident[:, :],
                                     compare_op=ALU.is_equal, fill=0.0, base=0,
                                     pattern=[[-1, 128]], channel_multiplier=1))
        gp(lambda g: g.affine_select(out=self.identf[:, :], in_=self.identf[:, :],
                                     compare_op=ALU.is_equal, fill=0.0, base=0,
                                     pattern=[[-1, 128]], channel_multiplier=1))
        gp(lambda g: g.affine_select(out=self.mask_i1[:, :], in_=self.mask_i1[:, :],
                                     compare_op=ALU.is_ge, fill=0.0, base=0,
                                     pattern=[[1, 128]], channel_multiplier=-1))
        gp(lambda g: g.affine_select(out=self.mask_s[:, :], in_=self.mask_s[:, :],
                                     compare_op=ALU.is_ge, fill=0.0, base=-1,
                                     pattern=[[1, 128]], channel_multiplier=-1))
        gp(lambda g: g.affine_select(
            out=self.sel16[:, :].rearrange("h (m r) -> h m r", r=64),
            in_=self.sel16[:, :].rearrange("h (m r) -> h m r", r=64),
            compare_op=ALU.is_equal, fill=0.0, base=0,
            pattern=[[-1, 16], [0, 64]], channel_multiplier=1))
        gp(lambda g: g.affine_select(
            out=self.oneh16[:, :].rearrange("h (s t) -> h s t", t=128),
            in_=self.oneh16[:, :].rearrange("h (s t) -> h s t", t=128),
            compare_op=ALU.is_equal, fill=0.0, base=0,
            pattern=[[-1, 16], [0, 128]], channel_multiplier=1))
        gp(lambda g: g.affine_select(
            out=self.nksel[:, :], in_=self.nksel[:, :], compare_op=ALU.is_ge,
            fill=0.0, base=-1, pattern=[[0, 128]], channel_multiplier=1))
        b.D("gpsimd")
        gp(lambda g: g.memset(self.mksel[0:1, :], 1.0))
        gp(lambda g: g.memset(self.ones2[0:64, 0:1], 1.0))
        gp(lambda g: g.memset(self.ones2[64:128, 1:2], 1.0))
        gp(lambda g: g.memset(self.ones2sc[0:64, 0:1], SCALE_DK))
        gp(lambda g: g.memset(self.ones2sc[64:128, 1:2], SCALE_DK))
        for v in (EPS_LN, -EPS_LN, 1.0 + EPS_RMS, EPS_L2, EPS_DEN, 1.0 / D,
                  1.0 / DVH, EPS_RMS):
            self.cst(v)
        b.F("gpsimd", self.s_g)
        self.G_CONST = self.s_g.n

    def ph1_load_ln(self):
        b, cfg = self.b, self.cfg
        tpc, ntile = cfg.tpc, cfg.ntile
        x_d = self.dr["x"]
        s_ldb = [b.sem("p1l0"), b.sem("p1l1")]
        s_tr = b.sem("p1tr")
        buffree = [0, 0]
        for i in range(ntile):
            j = i % 2
            buf = self.xtm[j]
            if i >= 2:
                b.W("sync", s_tr, buffree[j])
            b.I("sync", lambda e, buf=buf, src=x_d[128 * i:128 * (i + 1), :]:
                e.dma_start(out=buf[:, :], in_=src), s_ldb[j], k=16)
            ldn = s_ldb[j].n
            m_ap = self.statall[:, 2 * i:2 * i + 1]
            r_ap = self.statall[:, 2 * i + 1:2 * i + 2]
            b.W("vector", s_ldb[j], ldn)
            b.E("vector", lambda v, buf=buf, m=m_ap: v.reduce_sum(m, buf[:, :], axis=AX.X))
            b.D("vector")
            b.E("vector", lambda v, m=m_ap: v.tensor_scalar_mul(m, m, -1.0 / D))
            b.F("vector", self.s_v)
            mrdy = self.s_v.n
            b.W("scalar", self.s_g, self.G_CONST)
            b.W("scalar", s_ldb[j], ldn)
            b.W("scalar", self.s_v, mrdy)
            b.I("scalar", lambda s, buf=buf, m=m_ap: s.activation(
                self.sqb[:, :], buf[:, 0:512], AF.Square, bias=m), self.s_sc)
            sqrdy = self.s_sc.n
            b.W("vector", self.s_sc, sqrdy)
            b.E("vector", lambda v, r=r_ap: v.reduce_sum(r, self.sqb[:, :], axis=AX.X))
            b.F("vector", self.s_v)
            b.W("scalar", self.s_v, self.s_v.n)
            b.I("scalar", lambda s, buf=buf, m=m_ap: s.activation(
                self.sqb[:, :], buf[:, 512:1024], AF.Square, bias=m), self.s_sc)
            b.W("vector", self.s_sc, self.s_sc.n)
            b.E("vector", lambda v: v.reduce_sum(self.gtm[:, 0:1], self.sqb[:, :],
                                                 axis=AX.X))
            b.D("vector")
            b.E("vector", lambda v, r=r_ap: v.tensor_add(r, r, self.gtm[:, 0:1]))
            b.F("vector", self.s_v)
            vrdy = self.s_v.n
            b.W("scalar", self.s_v, vrdy)
            b.I("scalar", lambda s, r=r_ap: s.activation(
                r, r, AF.Sqrt, scale=self.cst(1.0 / D), bias=self.cst(EPS_LN)),
                self.s_sc)
            srdy = self.s_sc.n
            b.W("vector", self.s_sc, srdy)
            b.E("vector", lambda v, r=r_ap: v.reciprocal(r, r))
            b.F("vector", self.s_v)
            xnrdy = self.s_v.n
            b.W("tensor", self.s_g, self.G_CONST)
            b.W("tensor", s_ldb[j], ldn)
            b.W("tensor", self.s_v, xnrdy)
            for dc in range(DCH):
                bank = dc % 2
                px = self.pb[bank][:, 0:128]
                xr_ap = self.xres[:, dc * tpc + 128 * i: dc * tpc + 128 * (i + 1)]
                src_x = buf[:, 128 * dc:128 * (dc + 1)]
                self.bwait(bank)
                b.I("tensor", lambda t, px=px, s_=src_x: t.transpose(
                    px, s_, self.identf[:, :]), s_tr)
                b.W("vector", s_tr, s_tr.n)
                b.I("vector", lambda v, d=xr_ap, p=px: v.tensor_copy(d, p),
                    self.bsem[bank])
            buffree[j] = s_tr.n
        b.W("tensor", self.s_v, self.s_v.n)
        for i in range(ntile):
            pm = self.pb[4][0:1, 0:128]
            pr = self.pb[5][0:1, 0:128]
            d1 = self.arsl[0:1, 128 * i:128 * (i + 1)]
            d2 = self.arsl[32:33, 128 * i:128 * (i + 1)]
            self.bwait(4)
            self.bwait(5)
            b.I("tensor", lambda t, i=i, pm=pm: t.transpose(
                pm, self.statall[:, 2 * i:2 * i + 1], self.identf[:, :]), s_tr)
            b.I("tensor", lambda t, i=i, pr=pr: t.transpose(
                pr, self.statall[:, 2 * i + 1:2 * i + 2], self.identf[:, :]), s_tr)
            b.W("vector", s_tr, s_tr.n)
            b.I("vector", lambda v, d=d1, p=pm: v.tensor_copy(d, p), self.bsem[4])
            b.I("vector", lambda v, d=d2, p=pr: v.tensor_copy(d, p), self.bsem[5])
        rms_ap = self.arsl[64:65, 0:tpc]
        rs_ap = self.arsl[32:33, 0:tpc]
        b.D("vector")
        b.E("vector", lambda v: v.tensor_mul(rms_ap, rs_ap, rs_ap))
        b.F("vector", self.s_v)
        b.W("scalar", self.s_v, self.s_v.n)
        b.I("scalar", lambda s: s.activation(
            rms_ap, rms_ap, AF.Sqrt, scale=self.cst(-EPS_LN, 1),
            bias=self.cst(1.0 + EPS_RMS, 1)), self.s_sc)
        b.W("vector", self.s_sc, self.s_sc.n)
        b.E("vector", lambda v: v.reciprocal(rms_ap, rms_ap))
        b.F("vector", self.s_v)
        self.V_SLIVERS = self.s_v.n
        self.P1TR = (s_tr, s_tr.n)

    def ph2_small(self):
        b = self.b
        s_sl = b.sem("sml")
        b.W("sync", self.P1TR[0], self.P1TR[1])
        for dst, src in ((self.nw, self.dr["w_norm"]), (self.pnw, self.dr["w_pnw"]),
                         (self.convw, self.dr["w_conv"]), (self.dtb, self.dr["w_dtb"]),
                         (self.negal, self.dr["w_Alog"]), (self.onw, self.dr["w_onw"]),
                         (self.cmk, self.dr["cmask"]), (self.xp_t, self.dr["xp"]),
                         (self.rt_sb[:, 0:DCH * 128], self.dr["w_rt0"]),
                         (self.ebrow[:, :], self.dr["w_eb"])):
            b.W("sync", s_sl, s_sl.n)
            d_ap = dst[:, :] if hasattr(dst, "shape") and not hasattr(dst, "tensor") else dst
            b.I("sync", lambda e, d=d_ap, s=src: e.dma_start(out=d, in_=s[:, :]),
                s_sl, k=16)
        self.SML = s_sl.n
        b.W("scalar", s_sl, s_sl.n)
        b.I("scalar", lambda s: s.activation(self.negal[:, :], self.negal[:, :], AF.Exp),
            self.s_sc)
        b.W("vector", self.s_sc, self.s_sc.n)
        b.E("vector", lambda v: v.tensor_scalar_mul(self.negal[:, :], self.negal[:, :], -1.0))
        b.F("vector", self.s_v)
        self.V_NEGAL = self.s_v.n
        b.W("tensor", self.s_g, self.G_CONST)
        self.bwait(4)
        b.I("tensor", lambda t: t.transpose(self.pb[4][0:2, 0:128], self.ones2[:, :],
                                            self.identf[:, :]), self.s_t)
        b.W("vector", self.s_t, self.s_t.n)
        b.I("vector", lambda v: v.tensor_copy(self.ones2T[:, :], self.pb[4][0:2, 0:128]),
            self.bsem[4])
        self.bwait(5)
        b.I("tensor", lambda t: t.transpose(self.pb[5][0:2, 0:128], self.ones2sc[:, :],
                                            self.identf[:, :]), self.s_t)
        b.W("vector", self.s_t, self.s_t.n)
        b.I("vector", lambda v: v.tensor_copy(self.ones2Tsc[:, :], self.pb[5][0:2, 0:128]),
            self.bsem[5])
        st = self.xp_s
        b.W("vector", s_sl, s_sl.n)
        b.E("vector", lambda v: v.reduce_sum(st[0:4, 0:1], self.xp_t[0:4, :], axis=AX.X))
        b.D("vector")
        b.E("vector", lambda v: v.tensor_scalar_mul(st[0:4, 0:1], st[0:4, 0:1], -1.0 / D))
        b.F("vector", self.s_v)
        b.W("scalar", self.s_v, self.s_v.n)
        b.I("scalar", lambda s: s.activation(self.sqtmp[0:4, 0:512], self.xp_t[0:4, 0:512],
                                             AF.Square, bias=st[0:4, 0:1]), self.s_sc)
        b.D("scalar")
        b.I("scalar", lambda s: s.activation(self.cext[0:4, 0:512], self.xp_t[0:4, 512:1024],
                                             AF.Square, bias=st[0:4, 0:1]), self.s_sc)
        b.W("vector", self.s_sc, self.s_sc.n)
        b.E("vector", lambda v: v.reduce_sum(st[0:4, 1:2], self.sqtmp[0:4, 0:512], axis=AX.X))
        b.D("vector")
        b.E("vector", lambda v: v.reduce_sum(st[0:4, 2:3], self.cext[0:4, 0:512], axis=AX.X))
        b.D("vector")
        b.E("vector", lambda v: v.tensor_add(st[0:4, 1:2], st[0:4, 1:2], st[0:4, 2:3]))
        b.F("vector", self.s_v)
        b.W("scalar", self.s_v, self.s_v.n)
        b.I("scalar", lambda s: s.activation(st[0:4, 1:2], st[0:4, 1:2], AF.Sqrt,
                                             scale=self.cst(1.0 / D, 4),
                                             bias=self.cst(EPS_LN, 4)), self.s_sc)
        b.W("vector", self.s_sc, self.s_sc.n)
        b.E("vector", lambda v: v.reciprocal(st[0:4, 1:2], st[0:4, 1:2]))
        b.D("vector")
        b.E("vector", lambda v: v.tensor_mul(st[0:4, 2:3], st[0:4, 1:2], st[0:4, 1:2]))
        b.F("vector", self.s_v)
        b.W("scalar", self.s_v, self.s_v.n)
        b.I("scalar", lambda s: s.activation(st[0:4, 2:3], st[0:4, 2:3], AF.Sqrt,
                                             scale=self.cst(-EPS_LN, 4),
                                             bias=self.cst(1.0 + EPS_RMS, 4)), self.s_sc)
        b.W("vector", self.s_sc, self.s_sc.n)
        b.E("vector", lambda v: v.reciprocal(st[0:4, 2:3], st[0:4, 2:3]))
        b.D("vector")
        b.E("vector", lambda v: v.tensor_mul(st[0:4, 3:4], st[0:4, 1:2], st[0:4, 2:3]))
        b.D("vector")
        b.E("vector", lambda v: v.tensor_scalar(self.xp_t[0:4, :], self.xp_t[0:4, :],
                                                st[0:4, 0:1], st[0:4, 3:4],
                                                ALU.add, ALU.mult))
        b.F("vector", self.s_v)
        hx = self.s_v.n
        b.W("tensor", self.s_v, hx)
        for dc in range(DCH):
            self.bwait(5)
            b.I("tensor", lambda t, dc=dc: t.transpose(
                self.pb[5][0:128, 0:4], self.xp_t[0:4, 128 * dc:128 * (dc + 1)],
                self.identf[0:4, 0:4]), self.s_t)
            b.W("vector", self.s_t, self.s_t.n)
            b.I("vector", lambda v, dc=dc: v.tensor_scalar_mul(
                self.h3fm[:, 4 * dc:4 * (dc + 1)], self.pb[5][0:128, 0:4],
                self.nw[:, dc:dc + 1]), self.bsem[5])
        b.F("vector", self.s_v)
        self.V_H3 = self.s_v.n

    def ph2_router0(self):
        b, cfg = self.b, self.cfg
        tpc = cfg.tpc
        wrt = self.rt_sb[:, 0:DCH * 128]
        b.W("tensor", self.SML and self.s_v, self.V_SLIVERS)
        self.bwait(2)
        for dc in range(DCH):
            self.MMg(self.pb[2][0:8, 0:1], wrt[:, 128 * dc:128 * dc + 8],
                     self.onesc[:, :], start=(dc == 0), stop=(dc == DCH - 1),
                     waits=(((self.s_g, self.G_CONST),) if dc == 0 else ()))
        b.W("vector", self.s_t, self.s_t.n)
        b.I("vector", lambda v: v.tensor_copy(self.cs16[0:8, 0:1], self.pb[2][0:8, 0:1]),
            self.bsem[2])
        vcs = self.s_v.n + 0
        b.F("vector", self.s_v)
        vcs = self.s_v.n
        for tb in range(cfg.nblk):
            c0 = 512 * tb
            self.bwait(2)
            for dc in range(DCH):
                self.MMg(self.pb[2][0:8, 0:512], wrt[:, 128 * dc:128 * dc + 8],
                         self.xres[:, dc * tpc + c0: dc * tpc + c0 + 512],
                         start=(dc == 0), stop=(dc == DCH - 1))
            self.bwait(3)
            self.MMg(self.pb[3][0:8, 0:512], self.onesall[0:1, 0:8],
                     self.arsl[0:1, c0:c0 + 512])
            self.bwait(4)
            self.MMg(self.pb[4][0:8, 0:512], self.onesall[32:33, 0:8],
                     self.arsl[32:33, c0:c0 + 512])
            tn = self.s_t.n
            b.W("vector", self.s_t, tn)
            b.W("vector", self.s_v, vcs)
            b.E("vector", lambda v: v.tensor_copy(self.sqtmp[0:8, 0:512],
                                                  self.pb[2][0:8, 0:512]))
            b.I("vector", lambda v: v.drain(), self.bsem[2])
            b.E("vector", lambda v: v.scalar_tensor_tensor(
                self.sqtmp[0:8, 0:512], self.pb[3][0:8, 0:512], self.cs16[0:8, 0:1],
                self.sqtmp[0:8, 0:512], ALU.mult, ALU.add))
            b.I("vector", lambda v: v.drain(), self.bsem[3])
            g_ap = self.ar16[64:72, c0:c0 + 512]
            b.I("vector", lambda v, g=g_ap: v.tensor_tensor(
                g, self.sqtmp[0:8, 0:512], self.pb[4][0:8, 0:512], ALU.mult),
                self.bsem[4])
            b.F("vector", self.s_v)
        self.V_GATE0 = self.s_v.n
        b.F("tensor", self.s_t)
        self.T_RT0 = self.s_t.n

    def _ar_ident(self, t):
        b = self.b
        b.E("gpsimd", lambda g, t=t: g.memset(t, 1.0))
        b.D("gpsimd")
        for r0 in (0, 64):
            b.E("gpsimd", lambda g, t=t, r0=r0: g.affine_select(
                out=t[r0:r0 + 64, :].rearrange("p (j c) -> p j c", c=64),
                in_=t[r0:r0 + 64, :].rearrange("p (j c) -> p j c", c=64),
                compare_op=ALU.is_equal, fill=0.0, base=0,
                pattern=[[0, 8], [-1, 64]], channel_multiplier=1))

    def ph3_l0_sweep(self):
        b, cfg = self.b, self.cfg
        b.W("gpsimd", self.s_v, self.V_H3)
        b.E("gpsimd", lambda g: g.memset(self.S_f[:, :], 0.0))
        self._ar_ident(self.Ar_f[:, :])
        b.F("gpsimd", self.s_g)
        self.G_S0 = self.s_g.n
        b.W("vector", self.s_g, self.G_S0)
        b.E("vector", lambda v: v.tensor_copy(self.Ar_bf[:, :], self.Ar_f[:, :]))
        b.F("vector", self.s_v)
        b.W("tensor", self.s_g, self.G_S0)
        self.s_kv = b.sem("s_kv")
        self.s_wp = b.sem("s_wp")
        self.s_ba = b.sem("s_ba")
        self.socr = [b.sem("socr0"), b.sem("socr1")]
        self.sgdr = [b.sem("sgdr0"), b.sem("sgdr1")]
        self.soc = {}
        self.sgd = {}
        self.slot_free = [None] * 8
        self.slot_i = 0
        for tb in range(cfg.nblk):
            self.xn_block(tb)
            self.h_block(tb)
            self.l0_block_proj(tb)
            b.F("vector", self.s_kv)
            b.F("scalar", self.s_kv)
            KVN = self.s_kv.n
            BAN = self.s_ba.n
            for c in range(4):
                self.l0_delta_chunk(tb, c, KVN, BAN)

    def wslot(self, src_ap):
        j = self.slot_i % 8
        self.slot_i += 1
        dst = self.wp[:, j * 1024:(j + 1) * 1024]
        tok = self.wq.load(dst, src_ap, pre_wait=self.slot_free[j])
        return j, tok

    def wpc(self, j, dc):
        return self.wp[:, j * 1024 + dc * 128:j * 1024 + dc * 128 + 128]

    def l0_block_proj(self, tb):
        b, cfg = self.b, self.cfg
        s_kv, s_wp, s_ba = self.s_kv, self.s_wp, self.s_ba
        HBv = self.s_v.n
        projs = [("w_q", DCH, 0, self.qb, 1), ("w_k", DCH, 8, self.kb, 2),
                 ("w_v", VCH, 16, self.vb, 0)]
        pk = 0
        for wname, noc, po0, store, do_l2 in projs:
            img = self.dr[wname]
            for oc in range(noc):
                po = po0 + oc
                j, tok = self.wslot(img[:, oc * DCH * 128:(oc + 1) * DCH * 128])
                bank = pk % 2
                pk += 1
                b.W("tensor", tok[0], tok[1])
                b.W("tensor", self.s_v, HBv)
                self.bwait(bank)
                for dc in range(DCH):
                    self.MMg(self.pb[bank][:, 0:512], self.wpc(j, dc),
                             self.hb[:, dc * 512:(dc + 1) * 512],
                             start=(dc == 0), stop=(dc == DCH - 1))
                if tb == 0:
                    self.bwait(5)
                    b.W("tensor", self.s_v, self.V_H3)
                    for dc in range(DCH):
                        self.MMg(self.pb[5][:, 0:4], self.wpc(j, dc),
                                 self.h3fm[:, 4 * dc:4 * (dc + 1)],
                                 start=(dc == 0), stop=(dc == DCH - 1))
                    b.W("vector", self.s_t, self.s_t.n)
                    b.I("vector", lambda v, po=po: v.tensor_copy(
                        self.ctails[:, 3 * po:3 * po + 3], self.pb[5][:, 0:3]),
                        self.bsem[5])
                b.F("tensor", s_wp)
                self.slot_free[j] = (s_wp, s_wp.n)
                tmm = self.s_t.n
                b.W("vector", self.s_t, tmm)
                b.D("vector")
                b.E("vector", lambda v, po=po: v.tensor_copy(
                    self.cext[:, 0:3], self.ctails[:, 3 * po:3 * po + 3]))
                b.D("vector")
                b.I("vector", lambda v, bank=bank: v.tensor_copy(
                    self.cext[:, 3:515], self.pb[bank][:, 0:512]), self.bsem[bank])
                b.D("vector")
                b.E("vector", lambda v, po=po: v.tensor_copy(
                    self.ctails[:, 3 * po:3 * po + 3], self.cext[:, 512:515]))
                b.D("vector")
                b.W("vector", self.s_sc, self.s_sc.n)
                b.E("vector", lambda v, po=po: v.tensor_scalar_mul(
                    self.sqtmp[:, :], self.cext[:, 0:512], self.convw[:, 4 * po:4 * po + 1]))
                for i in range(1, 4):
                    b.D("vector")
                    b.E("vector", lambda v, po=po, i=i: v.scalar_tensor_tensor(
                        self.sqtmp[:, :], self.cext[:, i:i + 512],
                        self.convw[:, 4 * po + i:4 * po + i + 1],
                        self.sqtmp[:, :], ALU.mult, ALU.add))
                b.F("vector", self.s_v)
                cvr = self.s_v.n
                dst = store[:, oc * 512:(oc + 1) * 512]
                b.W("scalar", self.s_v, cvr)
                b.W("scalar", s_kv, s_kv.n)
                if do_l2:
                    b.I("scalar", lambda s: s.activation(
                        self.cext[:, 0:512], self.sqtmp[:, :], AF.Silu), self.s_sc)
                    b.D("scalar")
                    b.I("scalar", lambda s: s.activation(
                        self.sqtmp[:, :], self.cext[:, 0:512], AF.Square), self.s_sc)
                    sq2 = self.s_sc.n
                    self.bwait(4)
                    self.MMg(self.pb[4][0:2, 0:512], self.ones2[:, :], self.sqtmp[:, :],
                             waits=((self.s_sc, sq2),))
                    tl2 = self.s_t.n
                    b.W("scalar", self.s_t, tl2)
                    b.I("scalar", lambda s: s.activation(
                        self.l2row[:, :], self.pb[4][0:2, 0:512], AF.Sqrt,
                        bias=self.cst(EPS_L2, 2)), self.s_sc)
                    b.I("scalar", lambda s: s.drain(), self.bsem[4])
                    b.W("vector", self.s_sc, self.s_sc.n)
                    b.E("vector", lambda v: v.reciprocal(self.l2row[:, :], self.l2row[:, :]))
                    b.F("vector", self.s_v)
                    rcn = self.s_v.n
                    o2t = self.ones2Tsc if do_l2 == 1 else self.ones2T
                    self.bwait(5)
                    self.MMg(self.pb[5][:, 0:512], o2t[:, :], self.l2row[:, :],
                             waits=((self.s_v, rcn),))
                    tbc = self.s_t.n
                    b.W("vector", self.s_t, tbc)
                    b.I("vector", lambda v, dst=dst: v.tensor_tensor(
                        dst, self.cext[:, 0:512], self.pb[5][:, 0:512], ALU.mult),
                        self.bsem[5])
                    b.I("vector", lambda v: v.drain(), s_kv)
                else:
                    b.I("scalar", lambda s, dst=dst: s.activation(
                        dst, self.sqtmp[:, :], AF.Silu), self.s_sc)
                    b.I("scalar", lambda s: s.drain(), s_kv)
        img = self.dr["w_ba"]
        j, tok = self.wslot(img[:, :])
        b.W("tensor", tok[0], tok[1])
        b.W("tensor", self.s_v, HBv)
        self.bwait(2)
        self.bwait(3)
        for dc in range(DCH):
            self.MMg(self.pb[2][0:16, 0:512], self.wpc(j, dc)[:, 0:16],
                     self.hb[:, dc * 512:(dc + 1) * 512],
                     start=(dc == 0), stop=(dc == DCH - 1))
        for dc in range(DCH):
            self.MMg(self.pb[3][0:16, 0:512], self.wpc(j, dc)[:, 16:32],
                     self.hb[:, dc * 512:(dc + 1) * 512],
                     start=(dc == 0), stop=(dc == DCH - 1))
        b.F("tensor", s_wp)
        self.slot_free[j] = (s_wp, s_wp.n)
        tba = self.s_t.n
        c0 = 512 * tb
        beta_ap = self.ar16[0:16, c0:c0 + 512]
        g_ap = self.ar16[32:48, c0:c0 + 512]
        b.W("scalar", self.s_t, tba)
        b.W("scalar", self.s_v, self.V_NEGAL)
        b.I("scalar", lambda s, d=beta_ap: s.activation(
            d, self.pb[2][0:16, 0:512], AF.Sigmoid), self.s_sc)
        b.I("scalar", lambda s: s.drain(), self.bsem[2])
        b.W("scalar", self.s_v, self.s_v.n)
        b.I("scalar", lambda s: s.activation(
            self.sqtmp[0:16, 0:512], self.pb[3][0:16, 0:512], AF.Abs,
            bias=self.dtb[0:16, 0:1]), self.s_sc)
        b.D("scalar")
        b.I("scalar", lambda s: s.activation(
            self.sqtmp[0:16, 0:512], self.sqtmp[0:16, 0:512], AF.Exp, scale=-1.0),
            self.s_sc)
        b.D("scalar")
        b.I("scalar", lambda s: s.activation(
            self.sqtmp[0:16, 0:512], self.sqtmp[0:16, 0:512], AF.Ln, bias=1.0),
            self.s_sc)
        b.D("scalar")
        b.I("scalar", lambda s: s.activation(
            self.cext[0:16, 0:512], self.pb[3][0:16, 0:512], AF.Relu,
            bias=self.dtb[0:16, 0:1]), self.s_sc)
        b.I("scalar", lambda s: s.drain(), self.bsem[3])
        spn = self.s_sc.n
        b.W("vector", self.s_sc, spn)
        b.E("vector", lambda v: v.tensor_add(
            self.sqtmp[0:16, 0:512], self.sqtmp[0:16, 0:512], self.cext[0:16, 0:512]))
        b.D("vector")
        b.I("vector", lambda v, d=g_ap: v.tensor_scalar_mul(
            d, self.sqtmp[0:16, 0:512], self.negal[0:16, 0:1]), self.s_ba)
        b.F("vector", self.s_v)

    def l0_delta_chunk(self, tb, c, KVN, BAN):
        b, cfg = self.b, self.cfg
        c0 = 512 * tb
        gch = tb * 4 + c
        col = 128 * c
        bs = self.bsem
        s_t, s_v, s_sc = self.s_t, self.s_v, self.s_sc
        s_kv, s_ba = self.s_kv, self.s_ba

        g_ap = self.ar16[32:48, c0 + col:c0 + col + 128]
        beta_ap = self.ar16[0:16, c0 + col:c0 + col + 128]
        b.W("tensor", s_ba, BAN)
        self.bwait(4)
        t1 = self.TRg(self.pb[4][0:128, 0:16], g_ap, self.identf[32:48, 32:48])
        b.W("vector", s_t, t1)
        b.I("vector", lambda v: v.tensor_copy(self.gtm[:, :], self.pb[4][0:128, 0:16]),
            bs[4])
        b.F("vector", s_v)
        self.bwait(4)
        t2 = self.MMg(self.pb[4][0:128, 16:32], self.mask_i1[:, :], self.gtm[:, :],
                      waits=((s_v, s_v.n),))
        b.W("vector", s_t, t2)
        b.E("vector", lambda v: v.tensor_copy(self.gtm[:, :], self.pb[4][0:128, 16:32]))
        b.D("vector")
        b.I("vector", lambda v: v.tensor_scalar_mul(self.ngtm[:, :], self.gtm[:, :], -1.0),
            bs[4])
        b.F("vector", s_v)
        vgtm = s_v.n
        self.bwait(5)
        t3 = self.TRg(self.pb[5][0:16, 0:128], self.gtm[:, :], self.identf[:, :],
                      waits=((s_v, vgtm),))
        b.W("vector", s_t, t3)
        b.I("vector", lambda v: v.tensor_copy(self.gfm[:, :], self.pb[5][0:16, 0:128]),
            bs[5])
        b.F("vector", s_v)
        vgfm = s_v.n
        self.bwait(5)
        t4 = self.TRg(self.pb[5][0:128, 128:144], beta_ap, self.identf[0:16, 0:16])
        b.W("vector", s_t, t4)
        b.I("vector", lambda v: v.tensor_copy(self.btm[:, :], self.pb[5][0:128, 128:144]),
            bs[5])
        b.F("vector", s_v)
        b.W("scalar", s_v, vgfm)
        b.I("scalar", lambda s: s.activation(self.egfm[:, :], self.gfm[:, :], AF.Exp),
            s_sc)
        b.D("scalar")
        b.I("scalar", lambda s: s.activation(self.edec[:, :], self.gfm[:, :], AF.Exp,
                                             scale=-1.0, bias=self.gfm[:, 127:128]), s_sc)
        scg = s_sc.n
        b.W("vector", s_sc, scg)
        b.E("vector", lambda v, ba=beta_ap: v.tensor_tensor(
            self.bexp[:, :], ba, self.egfm[:, :], ALU.mult))
        b.D("vector")
        b.E("vector", lambda v: v.tensor_copy(self.eg_b[:, :], self.egfm[:, :]))
        b.D("vector")
        b.E("vector", lambda v, ba=beta_ap: v.tensor_copy(self.bbf[:, :], ba))
        b.F("vector", s_v)
        vpre = s_v.n

        for j in range(8):
            K2 = self.kb[:, j * 512 + col:j * 512 + col + 128]
            Q2 = self.qb[:, j * 512 + col:j * 512 + col + 128]
            sel_j = self.sel16[:, 128 * j:128 * (j + 1)]
            self.bwait(0)
            self.MMg(self.pb[0][:, 0:128], sel_j, self.bexp[:, :],
                     waits=((s_v, vpre),))
            self.bwait(1)
            self.MMg(self.pb[1][:, 0:128], sel_j, self.eg_b[:, :])
            self.bwait(2)
            bb3 = self.MMg(self.pb[2][:, 0:128], sel_j, self.edec[:, :])
            b.W("vector", s_t, bb3)
            b.W("vector", s_kv, KVN)
            b.I("vector", lambda v, K2=K2: v.tensor_tensor(
                self.Wf2[:, :], K2, self.pb[0][:, 0:128], ALU.mult), bs[0])
            b.D("vector")
            b.E("vector", lambda v, Q2=Q2: v.tensor_tensor(
                self.Qt2[:, :], Q2, self.pb[1][:, 0:128], ALU.mult))
            b.D("vector")
            b.I("vector", lambda v: v.tensor_copy(
                self.egCc[:, :], self.pb[1][:, 127:128]), bs[1])
            b.D("vector")
            b.I("vector", lambda v, K2=K2: v.tensor_tensor(
                self.Kb2[:, :], K2, self.pb[2][:, 0:128], ALU.mult), bs[2])
            b.F("vector", s_v)
            vW = s_v.n

            for hh in range(2):
                h = 2 * j + hh
                base = 64 * hh
                Kh = K2[base:base + 64, :]
                Qh = Q2[base:base + 64, :]
                Vh = self.vb[:, h * 512 + col:h * 512 + col + 128]
                idb = self.identf[base:base + 64, base:base + 64]
                idbb = self.ident[base:base + 64, base:base + 64]
                Sfh = self.S_f[base:base + 64, j * 128:(j + 1) * 128]
                Sbh = self.S_bf[base:base + 64, j * 128:(j + 1) * 128]
                Arfh = self.Ar_f[base:base + 64, j * 64:(j + 1) * 64]
                Arbh = self.Ar_bf[base:base + 64, j * 64:(j + 1) * 64]
                self.bwait(3)
                tPK = self.MMg(self.pb[3][:, 0:128], Kh, Kh, waits=((s_kv, KVN),))
                self.bwait(4)
                tKQ = self.MMg(self.pb[4][:, 0:128], Kh, Qh)
                self.bwait(5)
                tex = self.MMg(self.pb[5][0:1, 0:128], self.identf[0:16, h:h + 1],
                               self.gfm[:, :], waits=((s_v, vgfm),))
                b.W("vector", s_t, tex)
                b.I("vector", lambda v: v.tensor_copy(self.onrow[:, :],
                                                      self.pb[5][0:1, 0:128]), bs[5])
                b.F("vector", s_v)
                vex = s_v.n
                self.bwait(5)
                tgb = self.MMg(self.pb[5][:, 0:128], self.onesall[0:1, 0:128],
                               self.onrow[:, :], waits=((s_v, vex),))
                self.bwait(0)
                tbb = self.MMg(self.pb[0][:, 0:128],
                               self.oneh16[:, 128 * h:128 * (h + 1)], self.bbf[:, :])
                b.W("vector", s_t, tgb)
                b.I("vector", lambda v, h=h: v.tensor_scalar(
                    self.tmpA[:, :], self.pb[5][:, 0:128], self.ngtm[:, h:h + 1], 0.0,
                    ALU.add, ALU.min), bs[5])
                b.F("vector", s_v)
                b.W("scalar", s_v, s_v.n)
                b.I("scalar", lambda s: s.activation(self.Dsb[:, :], self.tmpA[:, :],
                                                     AF.Exp), s_sc)
                scD = s_sc.n
                b.W("vector", s_sc, scD)
                b.W("vector", s_t, tKQ)
                b.E("vector", lambda v: v.tensor_tensor(
                    self.tmpA[:, :], self.Dsb[:, :], self.pb[4][:, 0:128], ALU.mult))
                b.I("vector", lambda v: v.drain(), bs[4])
                b.E("vector", lambda v: v.tensor_tensor(
                    self.At_bf[:, :], self.tmpA[:, :], self.mask_i1[:, :], ALU.mult))
                b.D("vector")
                b.E("vector", lambda v: v.tensor_tensor(
                    self.tmpB[:, :], self.Dsb[:, :], self.pb[3][:, 0:128], ALU.mult))
                b.I("vector", lambda v: v.drain(), bs[3])
                b.W("vector", s_t, tbb)
                b.E("vector", lambda v: v.tensor_tensor(
                    self.tmpB[:, :], self.tmpB[:, :], self.pb[0][:, 0:128], ALU.mult))
                b.I("vector", lambda v: v.drain(), bs[0])
                b.E("vector", lambda v: v.tensor_tensor(
                    self.Mt_bf[:, :], self.tmpB[:, :], self.mask_s[:, :], ALU.mult))
                b.F("vector", s_v)
                vM = s_v.n
                self.bwait(6)
                tmt = self.TRg(self.pb[6][:, 0:128], self.Mt_bf[:, :], self.ident[:, :],
                               waits=((s_v, vM),))
                b.W("vector", s_t, tmt)
                b.I("vector", lambda v: v.tensor_copy(self.Mti_bf[:, :],
                                                      self.pb[6][:, 0:128]), bs[6])
                b.F("vector", s_v)
                vmti = s_v.n
                self.bwait(4)
                tws = self.MMg(self.pb[4][:, 0:128], self.Wf2[base:base + 64, :], Sbh,
                               waits=((s_v, vW),))
                self.bwait(7)
                tvt = self.TRg(self.pb[7][:, 0:128], Vh, self.ident[:, :])
                b.W("vector", s_t, tvt)
                b.E("vector", lambda v, h=h: v.tensor_scalar_mul(
                    self.tmpA[:, :], self.pb[7][:, 0:128], self.btm[:, h:h + 1]))
                b.I("vector", lambda v: v.drain(), bs[7])
                b.E("vector", lambda v: v.tensor_sub(
                    self.RHS[:, 0:128], self.tmpA[:, :], self.pb[4][:, 0:128]))
                b.I("vector", lambda v: v.drain(), bs[4])
                self.bwait(7)
                twt = self.TRg(self.pb[7][:, 128:192], self.Wf2[base:base + 64, :], idbb)
                b.W("vector", s_t, twt)
                b.I("vector", lambda v: v.tensor_copy(
                    self.RHS[:, 128:192], self.pb[7][:, 128:192]), bs[7])
                b.F("vector", s_v)
                vrhs = s_v.n
                NSQ = 4
                curT, curti = self.Mt_bf[:, :], self.Mti_bf[:, :]
                for lvl in range(NSQ):
                    pt = self.PT[:, 128 * lvl:128 * (lvl + 1)]
                    pti = self.PT[:, 128 * (4 + lvl % 2):128 * (5 + lvl % 2)]
                    self.bwait(2)
                    tsq = self.MMg(self.pb[2][:, 0:128], curT, curti,
                                   waits=((s_v, s_v.n),))
                    self.bwait(3)
                    tsq2 = self.MMg(self.pb[3][:, 0:128], curti, curT)
                    b.W("vector", s_t, tsq2)
                    b.I("vector", lambda v, pti=pti: v.tensor_copy(
                        pti, self.pb[2][:, 0:128]), bs[2])
                    b.D("vector")
                    b.I("vector", lambda v, pt=pt: v.tensor_copy(pt, self.pb[3][:, 0:128]),
                        bs[3])
                    b.F("vector", s_v)
                    curT, curti = pt, pti
                vsq = s_v.n
                NAPP = NSQ + 1
                Y = self.RHS[:, :]
                b.W("tensor", s_v, vsq)
                for a in range(NAPP):
                    lhs = (self.Mt_bf[:, :] if a == NAPP - 1 else
                           self.PT[:, 128 * (NSQ - 1 - a):128 * (NSQ - a)])
                    Yn = self.Ybuf[:, 192 * (a % 2):192 * (a % 2) + 192]
                    self.bwait(5)
                    tap = self.MMg(self.pb[5][:, 0:192], lhs, Y,
                                   waits=((s_v, vrhs if a == 0 else s_v.n),))
                    b.W("vector", s_t, tap)
                    if a == NAPP - 1:
                        b.I("vector", lambda v, Yn=Yn, Y=Y: v.tensor_sub(
                            Yn, Y, self.pb[5][:, 0:192]), bs[5])
                    else:
                        b.I("vector", lambda v, Yn=Yn, Y=Y: v.tensor_add(
                            Yn, Y, self.pb[5][:, 0:192]), bs[5])
                    b.F("vector", s_v)
                    Y = Yn
                U = Y
                vU = s_v.n
                self.bwait(0)
                to1 = self.MMg(self.pb[0][:, 0:128], Sbh, self.Qt2[base:base + 64, :])
                self.bwait(1)
                to2 = self.MMg(self.pb[1][:, 0:128], U[:, 0:128], self.At_bf[:, :],
                               waits=((s_v, vU),))
                b.W("vector", s_t, to2)
                b.E("vector", lambda v: v.tensor_copy(self.o1sb[:, :], self.pb[0][:, 0:128]))
                b.I("vector", lambda v: v.drain(), bs[0])
                ob_ap = self.obc[:, h * 128:(h + 1) * 128]
                if gch >= 1 and h == 0:
                    so, sn = self.soc[gch - 1]
                    b.W("vector", so, sn)
                b.I("vector", lambda v, d=ob_ap: v.tensor_add(
                    d, self.o1sb[:, :], self.pb[1][:, 0:128]), bs[1])
                b.F("vector", s_v)
                self.bwait(7)
                tkb = self.TRg(self.pb[7][:, 256:320], self.Kb2[base:base + 64, :], idbb)
                b.W("vector", s_t, tkb)
                b.I("vector", lambda v: v.tensor_copy(self.Kbt[:, :],
                                                      self.pb[7][:, 256:320]), bs[7])
                b.F("vector", s_v)
                vkbt = s_v.n
                self.bwait(2)
                tsu = self.MMg(self.pb[2][base:base + 64, 0:192], self.Kbt[:, :],
                               U[:, 0:192], tp=(0, base), waits=((s_v, vkbt),))
                b.W("vector", s_t, tsu)
                eg_ap = self.egCc[base:base + 64, 0:1]
                ps_ap = self.pb[2][base:base + 64, 0:128]
                b.E("vector", lambda v, Sfh=Sfh, eg=eg_ap, ps=ps_ap:
                    v.scalar_tensor_tensor(Sfh, Sfh, eg, ps, ALU.mult, ALU.add))
                b.D("vector")
                b.E("vector", lambda v, Sfh=Sfh, Sbh=Sbh: v.tensor_copy(Sbh, Sfh))
                b.D("vector")
                tB_ap = self.tmpB[base:base + 64, 0:64]
                pa_ap = self.pb[2][base:base + 64, 128:192]
                b.I("vector", lambda v, idb=idb, tB=tB_ap, eg=eg_ap, pa=pa_ap:
                    v.scalar_tensor_tensor(tB, idb, eg, pa, ALU.mult, ALU.subtract),
                    bs[2])
                b.D("vector")
                z_ap = self.Z2[base:base + 64, 0:64]
                b.E("vector", lambda v, z=z_ap, tB=tB_ap: v.tensor_copy(z, tB))
                b.F("vector", s_v)
                vach = s_v.n
                self.bwait(3)
                tz = self.MMg(self.pb[3][base:base + 64, 0:128], U[:, 128:192],
                              self.At_bf[:, :], tp=(0, base))
                b.W("vector", s_t, tz)
                tA_ap = self.tmpA[base:base + 64, 0:128]
                b.I("vector", lambda v, tA=tA_ap, base=base: v.tensor_sub(
                    tA, self.Qt2[base:base + 64, :],
                    self.pb[3][base:base + 64, 0:128]), bs[3])
                b.D("vector")
                zc_ap = self.Mti_bf[base:base + 64, 0:128]
                b.E("vector", lambda v, z=zc_ap, tA=tA_ap: v.tensor_copy(z, tA))
                b.F("vector", s_v)
                vzb = s_v.n
                gslot = gch % 2
                gb_ap = self.gst[base:base + 64, gslot * 1024 + j * 128:
                                 gslot * 1024 + j * 128 + 128]
                self.bwait(4)
                tg = self.MMg(self.pb[4][base:base + 64, 0:128], Arbh, zc_ap,
                              tp=(base, base), waits=((s_v, vzb),))
                b.W("vector", s_t, tg)
                if gch >= 2 and h == 0:
                    so, sn = self.sgd[gch - 2]
                    b.W("vector", so, sn)
                pg_ap = self.pb[4][base:base + 64, 0:128]
                b.I("vector", lambda v, d=gb_ap, p=pg_ap: v.tensor_copy(d, p), bs[4])
                self.bwait(5)
                ta1 = self.MMg(self.pb[5][base:base + 64, 0:64], z_ap, idbb,
                               tp=(base, base), waits=((s_v, vach),))
                b.W("vector", s_t, ta1)
                ac_ap = self.AchT[base:base + 64, 0:64]
                b.I("vector", lambda v, d=ac_ap, base=base: v.tensor_copy(
                    d, self.pb[5][base:base + 64, 0:64]), bs[5])
                b.F("vector", s_v)
                vat = s_v.n
                self.bwait(5)
                ta2 = self.MMg(self.pb[5][base:base + 64, 64:128], ac_ap, Arbh,
                               tp=(base, base), waits=((s_v, vat),))
                b.W("vector", s_t, ta2)
                b.E("vector", lambda v, Arfh=Arfh, base=base: v.tensor_copy(
                    Arfh, self.pb[5][base:base + 64, 64:128]))
                b.D("vector")
                b.I("vector", lambda v, Arbh=Arbh, Arfh=Arfh: v.tensor_copy(Arbh, Arfh),
                    bs[5])
                b.F("vector", s_v)
        b.F("vector", s_v)
        VEND = s_v.n
        soc = self.socr[gch % 2]
        b.W("sync", s_v, VEND)
        b.I("sync", lambda e, gch=gch: e.dma_start(
            out=self.oloc_d.ap()[gch], in_=self.obc[:, :]), soc, k=16)
        self.soc[gch] = (soc, soc.n)
        sgd = self.sgdr[gch % 2]
        gslot = gch % 2
        b.I("sync", lambda e, gch=gch, gslot=gslot: e.dma_start(
            out=self.g_d.ap()[gch], in_=self.gst[:, gslot * 1024:(gslot + 1) * 1024]),
            sgd, k=16)
        self.sgd[gch] = (sgd, sgd.n)

    def ph4_ag_compose(self):
        b, cfg = self.b, self.cfg
        cc0_in, cc0_out = self.ccs[0], self.ccs[1]
        s_ag = b.sem("s_ag")
        s_agl = b.sem("s_agl")
        b.F("vector", self.s_v)
        vfin = self.s_v.n
        b.W("sync", self.s_v, vfin)
        for j in range(8):
            a_dst = cc0_in.ap()[j * 128:(j + 1) * 128, 0:64]
            a_src = self.Ar_f[:, j * 64:(j + 1) * 64]
            b.I("sync", lambda e, a=a_dst, s_=a_src: e.dma_start(out=a, in_=s_),
                s_ag, k=16)
            s_dst = cc0_in.ap()[j * 128:(j + 1) * 128, 64:192]
            s_src = self.S_f[:, j * 128:(j + 1) * 128]
            b.I("sync", lambda e, a=s_dst, s_=s_src: e.dma_start(out=a, in_=s_),
                s_ag, k=16)
        b.W("gpsimd", s_ag, s_ag.n)
        b.I("gpsimd", lambda g: g.collective_compute(
            "AllGather", ALU.bypass, replica_groups=[list(range(NCORE))],
            ins=[cc0_in.ap().opt()], outs=[cc0_out.ap().opt()]), self.s_cc)
        CC0 = self.s_cc.n
        b.W("gpsimd", s_ag, s_ag.n)
        self._ar_ident(self.AID[:, :])
        b.F("gpsimd", self.s_g)
        AIDN = self.s_g.n
        b.W("vector", self.s_g, AIDN)
        b.W("tensor", self.s_g, self.G_CONST)
        b.W("tensor", self.s_v, self.V_NEGAL)
        self.bwait(0)
        b.I("tensor", lambda t: t.matmul(self.pb[0][:, 0:NCORE], self.mksel[:, :],
                                         self.cmk[:, :], start=True, stop=True), self.s_t)
        self.bwait(1)
        b.I("tensor", lambda t: t.matmul(self.pb[1][:, 0:NCORE], self.nksel[:, :],
                                         self.cmk[:, :], start=True, stop=True), self.s_t)
        b.W("vector", self.s_t, self.s_t.n)
        b.I("vector", lambda v: v.tensor_copy(self.mubc[:, :], self.pb[0][:, 0:NCORE]),
            self.bsem[0])
        b.I("vector", lambda v: v.tensor_copy(self.nubc[:, :], self.pb[1][:, 0:NCORE]),
            self.bsem[1])
        b.W("vector", s_ag, s_ag.n)
        b.E("vector", lambda v: v.memset(self.S_f[:, :], 0.0))
        b.D("vector")
        b.E("vector", lambda v: v.memset(self.S_bf[:, :], 0.0))
        b.F("vector", self.s_v)
        b.W("sync", self.s_cc, CC0)
        for l in range(NCORE):
            for j in range(8):
                r0 = l * NH * DKH + j * 128
                b.I("sync", lambda e, r0=r0, j=j: e.dma_start(
                    out=self.Atmp[:, j * 64:(j + 1) * 64],
                    in_=cc0_out.ap()[r0:r0 + 128, 0:64]), s_agl, k=16)
                b.I("sync", lambda e, r0=r0, j=j: e.dma_start(
                    out=self.Btmp[:, j * 128:(j + 1) * 128],
                    in_=cc0_out.ap()[r0:r0 + 128, 64:192]), s_agl, k=16)
            AGL = s_agl.n
            b.W("tensor", s_agl, AGL)
            for jj in range(8):
                for sub in range(2):
                    bb = 64 * sub
                    src = self.Atmp[bb:bb + 64, jj * 64:(jj + 1) * 64]
                    dst = self.AtT[bb:bb + 64, jj * 64:(jj + 1) * 64]
                    pbk = 2 + (jj % 2)
                    self.bwait(pbk)
                    reg = self.pb[pbk][bb:bb + 64, 0:64]
                    b.I("tensor", lambda t, r=reg, s_=src, bb=bb: t.matmul(
                        r, s_, self.identf[bb:bb + 64, bb:bb + 64],
                        start=True, stop=True, tile_position=(bb, bb),
                        skip_group_check=True), self.s_t)
                    b.W("vector", self.s_t, self.s_t.n)
                    b.I("vector", lambda v, d=dst, r=reg: v.tensor_copy(d, r),
                        self.bsem[pbk])
            b.F("vector", self.s_v)
            mu = self.mubc[:, l:l + 1]
            b.E("vector", lambda v: v.tensor_sub(self.AtT[:, :], self.AtT[:, :],
                                                 self.AID[:, :]))
            b.D("vector")
            b.E("vector", lambda v, mu=mu: v.scalar_tensor_tensor(
                self.AtT[:, :], self.AtT[:, :], mu, self.AID[:, :], ALU.mult, ALU.add))
            b.D("vector")
            b.E("vector", lambda v: v.tensor_copy(self.AtTb[:, :], self.AtT[:, :]))
            b.F("vector", self.s_v)
            vbl = self.s_v.n
            b.W("tensor", self.s_v, vbl)
            for jj in range(8):
                for sub in range(2):
                    bb = 64 * sub
                    lhs = self.AtTb[bb:bb + 64, jj * 64:(jj + 1) * 64]
                    rhs = self.S_bf[bb:bb + 64, jj * 128:(jj + 1) * 128]
                    pbk = 4 + (jj % 2)
                    self.bwait(pbk)
                    reg = self.pb[pbk][bb:bb + 64, 0:128]
                    b.I("tensor", lambda t, r=reg, l_=lhs, rh=rhs, bb=bb: t.matmul(
                        r, l_, rh, start=True, stop=True, tile_position=(bb, bb)),
                        self.s_t)
                    b.W("vector", self.s_t, self.s_t.n)
                    sfh = self.S_f[bb:bb + 64, jj * 128:(jj + 1) * 128]
                    bth = self.Btmp[bb:bb + 64, jj * 128:(jj + 1) * 128]
                    muh = self.mubc[bb:bb + 64, l:l + 1]
                    b.I("vector", lambda v, s_=sfh, bt=bth, mu=muh, r=reg:
                        v.scalar_tensor_tensor(s_, bt, mu, r, ALU.mult, ALU.add),
                        self.bsem[pbk])
            b.F("vector", self.s_v)
            b.E("vector", lambda v: v.tensor_copy(self.S_bf[:, :], self.S_f[:, :]))
            b.F("vector", self.s_v)
            b.F("tensor", self.s_t)
            b.W("sync", self.s_t, self.s_t.n)
            b.W("sync", self.s_v, self.s_v.n)
        self.V_SIN = self.s_v.n

    def ph5_l0_blocks(self):
        b, cfg = self.b, self.cfg
        s_t, s_v, s_sc = self.s_t, self.s_v, self.s_sc
        self.s_fin = b.sem("s_fin")
        self.s_or = b.sem("s_or")
        self.s_gr = b.sem("s_gr")
        for s in (self.socr[0], self.socr[1], self.sgdr[0], self.sgdr[1]):
            b.W("sync", s, s.n)
        for tb in range(cfg.nblk):
            self.xn_block(tb, dst=self.arn(4096 * tb, 4096))
        self.V_XNF = self.s_v.n
        wgg = WS(self, [self.arn(32768 + 1024 * i, 1024) for i in range(4)])
        wgu = WS(self, [self.arn(20480 + 1024 * i, 1024) for i in range(8)])
        wdn = WS(self, [self.arn(42496 + 1408 * i, 1408) for i in range(2)])
        wwo = WS(self, [self.arn(36864 + 2048 * i, 2048) for i in range(2)])
        wwo.barrier((self.s_v, self.V_XNF))
        wgg.barrier((self.s_v, self.V_XNF))
        wdn.barrier((self.s_v, self.V_XNF))
        self.ofree = None
        self.gfence = {}
        for tb in range(cfg.nblk):
            self.l0_block_fin(tb, wgg, wgu, wdn, wwo)
        b.F("tensor", s_t)
        self.PH5B_T = (s_t, s_t.n)

    def h5_block(self, tb):
        b, cfg = self.b, self.cfg
        c0 = 512 * tb
        b.W("tensor", self.s_v, self.V_XNF)
        self.bwait(4)
        self.MMg(self.pb[4][:, 0:512], self.onesall[64:65, 0:128],
                 self.arsl[64:65, c0:c0 + 512])
        tn = self.s_t.n
        b.W("vector", self.s_t, tn)
        for dc in range(DCH):
            xn_ap = self.arn(4096 * tb + 512 * dc, 512)
            h_ap = self.arn(16384 + 512 * dc, 512)
            b.D("vector")
            b.W("vector", self.s_sc, self.s_sc.n)
            b.E("vector", lambda v, x=xn_ap, dc=dc: v.tensor_scalar_mul(
                self.sqtmp[:, :], x, self.nw[:, dc:dc + 1]))
            b.D("vector")
            last = dc == DCH - 1
            b.I("vector", lambda v, h=h_ap: v.tensor_tensor(
                h, self.sqtmp[:, :], self.pb[4][:, 0:512], ALU.mult),
                self.bsem[4] if last else self.s_v)
        b.F("vector", self.s_v)
        return self.s_v.n

    def l0_block_fin(self, tb, wgg, wgu, wdn, wwo):
        b, cfg = self.b, self.cfg
        tpc = cfg.tpc
        c0 = 512 * tb
        bs = self.bsem
        s_t, s_v, s_sc, s_fin = self.s_t, self.s_v, self.s_sc, self.s_fin
        hb5 = lambda dc: self.arn(16384 + 512 * dc, 512)
        gos = lambda h, c: self.arn(20480 + 512 * h + 128 * c, 128)
        obc5 = self.arn(28672, 2048)
        x1s = lambda dc: self.arn(32768 + 512 * dc, 512)
        HB = self.h5_block(tb)
        img = self.dr["w_g"]
        for oc in range(NH):
            j, buf, tok = wgg.get(img[:, oc * DCH * 128:(oc + 1) * DCH * 128])
            bank = oc % 2
            b.W("tensor", tok[0], tok[1])
            b.W("tensor", s_v, HB)
            self.bwait(bank)
            for dc in range(DCH):
                b.I("tensor", lambda t, buf=buf, dc=dc, bank=bank: t.matmul(
                    self.pb[bank][:, 0:512], buf[:, dc * 128:(dc + 1) * 128],
                    hb5(dc), start=(dc == 0), stop=(dc == DCH - 1),
                    skip_group_check=True), s_t)
            wgg.release(j)
            tn = s_t.n
            dst = self.arn(20480 + 512 * oc, 512)
            b.W("scalar", s_t, tn)
            b.I("scalar", lambda s, d=dst, bank=bank: s.activation(
                d, self.pb[bank][:, 0:512], AF.Silu), s_sc)
            b.I("scalar", lambda s, bank=bank: s.drain(), bs[bank])
        b.F("scalar", s_sc)
        GS = s_sc.n
        b.W("vector", s_sc, GS)
        for c in range(4):
            gch = tb * 4 + c
            gslot = gch % 2
            gbuf = self.arn(30720 + 1024 * gslot, 1024)
            if self.ofree is not None:
                b.W("sync", self.ofree[0], self.ofree[1])
            b.I("sync", lambda e, gch=gch, d=obc5: e.dma_start(
                out=d, in_=self.oloc_d.ap()[gch]), self.s_or, k=16)
            if gch >= 2:
                fo = self.gfence[gch - 2]
                b.W("sync", fo[0], fo[1])
            b.I("sync", lambda e, gch=gch, d=gbuf: e.dma_start(
                out=d, in_=self.g_d.ap()[gch]), self.s_gr, k=16)
            ORN, GRN = self.s_or.n, self.s_gr.n
            for h in range(NH):
                jj, sub = h // 2, h % 2
                bb = 64 * sub
                Sin = self.S_bf[bb:bb + 64, jj * 128:(jj + 1) * 128]
                Gh = gbuf[bb:bb + 64, jj * 128:(jj + 1) * 128]
                ob_ap = obc5[:, h * 128:(h + 1) * 128]
                og_ap = gos(h, c)
                self.bwait(0)
                b.W("tensor", s_v, self.V_SIN)
                b.W("tensor", self.s_gr, GRN)
                self.MMg(self.pb[0][:, 0:128], Sin, Gh, tp=(bb, 0))
                tcor = s_t.n
                if h == 0:
                    b.W("vector", self.s_or, ORN)
                b.W("vector", s_t, tcor)
                b.W("vector", s_sc, s_sc.n)
                b.I("vector", lambda v, o=ob_ap: v.tensor_add(
                    self.otile[:, :], o, self.pb[0][:, 0:128]), bs[0])
                if h == NH - 1:
                    b.F("vector", s_v)
                    self.ofree = (s_v, s_v.n)
                else:
                    b.F("vector", s_v)
                vcor = s_v.n
                b.W("scalar", s_v, vcor)
                b.I("scalar", lambda s: s.activation(
                    self.sqtmp[:, 0:128], self.otile[:, :], AF.Square), s_sc)
                b.W("tensor", s_sc, s_sc.n)
                self.bwait(1)
                self.MMg(self.pb[1][0:1, 0:128], self.onesc[:, :], self.sqtmp[:, 0:128])
                tsum = s_t.n
                b.W("scalar", s_t, tsum)
                b.I("scalar", lambda s: s.activation(
                    self.onrow[:, :], self.pb[1][0:1, 0:128], AF.Sqrt,
                    scale=self.cst(1.0 / DVH, 1), bias=self.cst(EPS_RMS, 1)), s_sc)
                b.I("scalar", lambda s: s.drain(), bs[1])
                b.W("vector", s_sc, s_sc.n)
                b.E("vector", lambda v: v.reciprocal(self.onrow[:, :], self.onrow[:, :]))
                b.F("vector", s_v)
                vrn = s_v.n
                b.W("tensor", s_v, vrn)
                self.bwait(2)
                self.MMg(self.pb[2][:, 0:128], self.onesr[:, :], self.onrow[:, :])
                tbc = s_t.n
                b.W("vector", s_t, tbc)
                b.E("vector", lambda v: v.tensor_tensor(
                    self.otile[:, :], self.otile[:, :], self.pb[2][:, 0:128], ALU.mult))
                b.I("vector", lambda v: v.drain(), bs[2])
                b.E("vector", lambda v: v.tensor_scalar_mul(
                    self.otile[:, :], self.otile[:, :], self.onw[:, 0:1]))
                b.D("vector")
                b.I("vector", lambda v, d=og_ap: v.tensor_tensor(
                    d, self.otile[:, :], d, ALU.mult), s_fin)
                b.F("vector", s_v)
            b.F("tensor", s_t)
            self.gfence[gch] = (s_t, s_t.n)
        b.F("vector", s_fin)
        OGN = s_fin.n
        img = self.dr["w_o"]
        b.W("tensor", s_v, self.V_SLIVERS)
        self.bwait(3)
        self.MMg(self.pb[3][:, 0:512], self.onesall[0:1, 0:128],
                 self.arsl[0:1, c0:c0 + 512])
        self.bwait(5)
        self.MMg(self.pb[5][:, 0:512], self.onesall[32:33, 0:128],
                 self.arsl[32:33, c0:c0 + 512])
        self.bwait(2)
        for dc in range(DCH):
            j, buf, tok = wwo.get(img[:, dc * NH * 128:(dc + 1) * NH * 128])
            bank = dc % 2
            b.W("tensor", tok[0], tok[1])
            b.W("tensor", s_fin, OGN)
            self.bwait(bank)
            for vc in range(NH):
                b.I("tensor", lambda t, buf=buf, vc=vc, bank=bank: t.matmul(
                    self.pb[bank][:, 0:512], buf[:, vc * 128:(vc + 1) * 128],
                    self.arn(20480 + 512 * vc, 512),
                    start=(vc == 0), stop=(vc == NH - 1),
                    skip_group_check=True), s_t)
            wwo.release(j)
            tmix = s_t.n
            xr_ap = self.xres[:, dc * tpc + c0:dc * tpc + c0 + 512]
            b.W("vector", s_t, tmix)
            b.W("vector", s_sc, s_sc.n)
            b.E("vector", lambda v, x=xr_ap: v.tensor_add(
                self.sqtmp[:, :], x, self.pb[3][:, 0:512]))
            b.D("vector")
            b.E("vector", lambda v: v.tensor_tensor(
                self.sqtmp[:, :], self.sqtmp[:, :], self.pb[5][:, 0:512], ALU.mult))
            b.D("vector")
            b.I("vector", lambda v, bank=bank: v.tensor_add(
                self.cext[:, 0:512], self.sqtmp[:, :], self.pb[bank][:, 0:512]),
                bs[bank])
            b.D("vector")
            b.E("vector", lambda v, x=xr_ap: v.tensor_add(x, x, self.cext[:, 0:512]))
            b.D("vector")
            b.I("vector", lambda v, dc=dc: v.tensor_copy(x1s(dc), self.cext[:, 0:512]),
                s_fin)
            b.F("vector", s_v)
            vx1 = s_v.n
            b.W("scalar", s_v, vx1)
            b.I("scalar", lambda s: s.activation(
                self.sqb[:, 0:512], self.cext[:, 0:512], AF.Square), s_sc)
            b.W("tensor", s_sc, s_sc.n)
            b.I("tensor", lambda t, dc=dc: t.matmul(
                self.pb[2][0:1, 0:512], self.onesc[:, :], self.sqb[:, 0:512],
                start=(dc == 0), stop=(dc == DCH - 1), skip_group_check=True), s_t)
            if dc < DCH - 1:
                b.W("scalar", s_t, s_t.n)
        b.I("vector", lambda v: v.drain(), bs[3])
        b.I("vector", lambda v: v.drain(), bs[5])
        b.F("vector", s_v)
        tss = s_t.n
        b.W("scalar", s_t, tss)
        b.I("scalar", lambda s: s.activation(
            self.l2row[0:1, 0:512], self.pb[2][0:1, 0:512], AF.Sqrt,
            scale=self.cst(1.0 / D, 1), bias=self.cst(EPS_RMS, 1)), s_sc)
        b.I("scalar", lambda s: s.drain(), bs[2])
        b.W("vector", s_sc, s_sc.n)
        b.E("vector", lambda v: v.reciprocal(self.l2row[0:1, 0:512],
                                             self.l2row[0:1, 0:512]))
        b.F("vector", s_v)
        vrm = s_v.n
        b.W("tensor", s_v, vrm)
        self.bwait(3)
        self.MMg(self.pb[3][:, 0:512], self.onesall[0:1, 0:128],
                 self.l2row[0:1, 0:512])
        trb = s_t.n
        b.W("vector", s_t, trb)
        for dc in range(DCH):
            b.D("vector")
            b.E("vector", lambda v, dc=dc: v.tensor_tensor(
                self.sqtmp[:, :], x1s(dc), self.pb[3][:, 0:512], ALU.mult))
            b.D("vector")
            b.I("vector", lambda v, dc=dc: v.tensor_scalar_mul(
                x1s(dc), self.sqtmp[:, :], self.pnw[:, dc:dc + 1]),
                bs[3] if dc == DCH - 1 else s_v)
        b.F("vector", s_v)
        VH2 = s_v.n
        gimg, uimg, dimg = self.dr["w_gate"], self.dr["w_up"], self.dr["w_down"]
        b.F("tensor", s_t)
        wgu.barrier((s_t, s_t.n))
        wdn.barrier((s_t, s_t.n))
        for half in range(2):
            for fi in range(11):
                ffc = half * 11 + fi
                w = 128 if ffc < FFCH - 1 else FF_LAST
                jg, bg, tokg = wgu.get(gimg[:, ffc * DCH * 128:(ffc + 1) * DCH * 128])
                ju, bu, toku = wgu.get(uimg[:, ffc * DCH * 128:(ffc + 1) * DCH * 128])
                b.W("tensor", tokg[0], tokg[1])
                b.W("tensor", toku[0], toku[1])
                b.W("tensor", s_v, VH2)
                self.bwait(0)
                self.bwait(1)
                for dc in range(DCH):
                    b.I("tensor", lambda t, bg=bg, dc=dc, w=w: t.matmul(
                        self.pb[0][0:w, 0:512], bg[:, dc * 128:dc * 128 + w],
                        x1s(dc), start=(dc == 0), stop=(dc == DCH - 1),
                        skip_group_check=True), s_t)
                for dc in range(DCH):
                    b.I("tensor", lambda t, bu=bu, dc=dc, w=w: t.matmul(
                        self.pb[1][0:w, 0:512], bu[:, dc * 128:dc * 128 + w],
                        x1s(dc), start=(dc == 0), stop=(dc == DCH - 1),
                        skip_group_check=True), s_t)
                wgu.release(jg)
                wgu.release(ju)
                tmlp = s_t.n
                hid_ap = self.arn(36864 + 512 * fi, 512)[0:w, :]
                b.W("scalar", s_t, tmlp)
                b.W("scalar", s_v, s_v.n)
                b.I("scalar", lambda s, w=w: s.activation(
                    self.sqtmp[0:w, 0:512], self.pb[0][0:w, 0:512], AF.Silu), s_sc)
                b.I("scalar", lambda s: s.drain(), bs[0])
                scs = s_sc.n
                b.W("vector", s_sc, scs)
                b.I("vector", lambda v, hid=hid_ap, w=w: v.tensor_tensor(
                    hid, self.sqtmp[0:w, 0:512], self.pb[1][0:w, 0:512], ALU.mult),
                    bs[1])
                b.F("vector", s_v)
            VHID = s_v.n
            for oc2 in range(DCH):
                j, buf, tok = wdn.get(
                    dimg[:, oc2 * FFCH * 128 + half * 11 * 128:
                         oc2 * FFCH * 128 + half * 11 * 128 + 1408])
                bank = 4 + oc2 % 2
                b.W("tensor", tok[0], tok[1])
                b.W("tensor", s_v, VHID)
                self.bwait(bank)
                for fi in range(11):
                    ffc = half * 11 + fi
                    w = 128 if ffc < FFCH - 1 else FF_LAST
                    b.I("tensor", lambda t, buf=buf, fi=fi, w=w, bank=bank: t.matmul(
                        self.pb[bank][:, 0:512], buf[0:w, fi * 128:fi * 128 + 128],
                        self.arn(36864 + 512 * fi, 512)[0:w, :],
                        start=(fi == 0), stop=(fi == 10), skip_group_check=True), s_t)
                wdn.release(j)
                tdn = s_t.n
                xr_ap = self.xres[:, oc2 * tpc + c0:oc2 * tpc + c0 + 512]
                b.W("vector", s_t, tdn)
                b.D("vector")
                b.I("vector", lambda v, x=xr_ap, bank=bank: v.tensor_tensor(
                    x, x, self.pb[bank][:, 0:512], ALU.add), bs[bank])
            b.F("vector", s_v)
        b.F("tensor", s_t)
        wgg.barrier((s_t, s_t.n))
        wwo.barrier((s_t, s_t.n))

    def route_tiles(self, src_row, nexp, dst_row, use_bias, probs):
        b, cfg = self.b, self.cfg
        s_t, s_v, s_sc = self.s_t, self.s_v, self.s_sc
        bs = self.bsem
        idsl = self.identf[src_row:src_row + nexp, src_row:src_row + nexp]
        VW = self.s_v.n
        for ti in range(cfg.ntile):
            cols = 128 * ti
            self.bwait(2)
            t1 = self.TRg(self.pb[2][0:128, 0:nexp],
                          self.ar16[src_row:src_row + nexp, cols:cols + 128],
                          idsl, tp=(src_row, 0),
                          waits=((s_v, VW),) if ti == 0 else ())
            b.W("vector", s_t, t1)
            b.I("vector", lambda v, n=nexp: v.tensor_copy(
                self.rtm[:, 0:n], self.pb[2][0:128, 0:n]), bs[2])
            b.D("vector")
            if use_bias:
                b.E("vector", lambda v, n=nexp: v.tensor_add(
                    self.rw1[:, 0:n], self.rtm[:, 0:n], self.ebb[:, 0:n]))
                sel = self.rw1
            else:
                sel = self.rtm
            b.D("vector")
            b.E("vector", lambda v, sel=sel, n=nexp: v.max(
                self.mx8[:, :], sel[:, 0:n]))
            b.D("vector")
            b.E("vector", lambda v: v.tensor_sub(
                self.sqtmp[:, 0:1], self.mx8[:, 0:1], self.mx8[:, 1:2]))
            b.F("vector", s_v)
            vmx = s_v.n
            if probs:
                b.W("scalar", s_v, vmx)
                b.I("scalar", lambda s: s.activation(
                    self.sqtmp[:, 1:2], self.sqtmp[:, 0:1], AF.Sigmoid), s_sc)
                b.D("scalar")
                b.I("scalar", lambda s: s.activation(
                    self.sqtmp[:, 2:3], self.sqtmp[:, 0:1], AF.Sigmoid, scale=-1.0),
                    s_sc)
                b.W("vector", s_sc, s_sc.n)
                b.E("vector", lambda v, sel=sel, n=nexp: v.tensor_scalar(
                    self.rtm[:, 16:16 + n], sel[:, 0:n], self.mx8[:, 0:1],
                    self.sqtmp[:, 1:2], ALU.is_equal, ALU.mult))
                b.D("vector")
                b.E("vector", lambda v, sel=sel, n=nexp: v.tensor_scalar(
                    self.sqtmp[:, 4:4 + n], sel[:, 0:n], self.mx8[:, 1:2],
                    self.sqtmp[:, 2:3], ALU.is_equal, ALU.mult))
                b.D("vector")
                b.E("vector", lambda v, n=nexp: v.tensor_add(
                    self.rtm[:, 16:16 + n], self.rtm[:, 16:16 + n],
                    self.sqtmp[:, 4:4 + n]))
            else:
                b.E("vector", lambda v, sel=sel, n=nexp: v.tensor_scalar(
                    self.rtm[:, 16:16 + n], sel[:, 0:n], self.mx8[:, 0:1], 1.0,
                    ALU.is_equal, ALU.mult))
                b.D("vector")
                b.E("vector", lambda v, sel=sel, n=nexp: v.tensor_scalar(
                    self.sqtmp[:, 4:4 + n], sel[:, 0:n], self.mx8[:, 1:2], 1.0,
                    ALU.is_equal, ALU.mult))
                b.D("vector")
                b.E("vector", lambda v, n=nexp: v.tensor_add(
                    self.rtm[:, 16:16 + n], self.rtm[:, 16:16 + n],
                    self.sqtmp[:, 4:4 + n]))
                b.D("vector")
                b.E("vector", lambda v, n=nexp: v.tensor_mul(
                    self.rtm[:, 16:16 + n], self.rtm[:, 16:16 + n], self.rtm[:, 0:n]))
            b.F("vector", s_v)
            vw2 = s_v.n
            self.bwait(3)
            t2 = self.TRg(self.pb[3][0:nexp, 0:128], self.rtm[:, 16:16 + nexp],
                          self.identf[:, :], waits=((s_v, vw2),))
            b.W("vector", s_t, t2)
            b.I("vector", lambda v, cols=cols, n=nexp: v.tensor_copy(
                self.ar16[dst_row:dst_row + n, cols:cols + 128],
                self.pb[3][0:n, 0:128]), bs[3])
            b.F("vector", s_v)

    def webc_build(self, e, nexp, wrow):
        b, cfg = self.b, self.cfg
        s_t, s_v = self.s_t, self.s_v
        bs = self.bsem
        for tb in range(cfg.nblk):
            c0 = 512 * tb
            self.bwait(0)
            self.MMg(self.pb[0][0:1, 0:512], self.identf[0:nexp, e:e + 1],
                     self.ar16[wrow:wrow + nexp, c0:c0 + 512],
                     waits=((s_v, s_v.n),))
            b.W("vector", s_t, s_t.n)
            b.I("vector", lambda v: v.tensor_copy(self.l2row[0:1, 0:512],
                                                  self.pb[0][0:1, 0:512]), bs[0])
            b.F("vector", s_v)
            vv = s_v.n
            self.bwait(0)
            self.MMg(self.pb[0][:, 0:512], self.onesall[0:1, 0:128],
                     self.l2row[0:1, 0:512], waits=((s_v, vv),))
            b.W("vector", s_t, s_t.n)
            b.I("vector", lambda v, tb=tb: v.tensor_copy(
                self.webc_v(tb), self.pb[0][:, 0:512]), bs[0])
            b.F("vector", s_v)

    def webc_v(self, tb):
        return self.xtm[tb // 2][:, 512 * (tb % 2):512 * (tb % 2) + 512]

    def ph5c_moe0(self):
        b, cfg = self.b, self.cfg
        tpc = cfg.tpc
        s_t, s_v, s_sc = self.s_t, self.s_v, self.s_sc
        bs = self.bsem
        b.W("tensor", s_v, self.s_v.n)
        self.route_tiles(64, E0, 0, use_bias=False, probs=True)
        wsl = WS(self, [self.arn(32768 + 1024 * i, 1024) for i in range(12)])
        wsl.barrier(self.PH5B_T)
        m1i, m2i, m3i = self.dr["w_m1"], self.dr["w_m2"], self.dr["w_m3"]
        hid = lambda tb, oc8: self.arn(16384 + (8 * tb + oc8) * 512, 512)
        for e in range(E0):
            self.webc_build(e, E0, 0)
            VWE = s_v.n
            for half in range(2):
                for oc8 in range(8):
                    oc = 8 * half + oc8
                    bk1 = 2 * (oc8 % 2)
                    bk2 = 1 + 2 * (oc8 % 2)
                    j1, b1, tk1 = wsl.get(m1i.ap()[e][:, oc * 1024:(oc + 1) * 1024])
                    j2, b2, tk2 = wsl.get(m2i.ap()[e][:, oc * 1024:(oc + 1) * 1024])
                    b.W("tensor", tk1[0], tk1[1])
                    b.W("tensor", tk2[0], tk2[1])
                    for tb in range(cfg.nblk):
                        self.bwait(bk1)
                        self.bwait(bk2)
                        for dc in range(DCH):
                            b.I("tensor", lambda t, b1=b1, dc=dc, tb=tb, bk1=bk1:
                                t.matmul(self.pb[bk1][:, 0:512],
                                         b1[:, dc * 128:(dc + 1) * 128],
                                         self.arn(4096 * tb + 512 * dc, 512),
                                         start=(dc == 0), stop=(dc == DCH - 1),
                                         skip_group_check=True), s_t)
                        for dc in range(DCH):
                            b.I("tensor", lambda t, b2=b2, dc=dc, tb=tb, bk2=bk2:
                                t.matmul(self.pb[bk2][:, 0:512],
                                         b2[:, dc * 128:(dc + 1) * 128],
                                         self.arn(4096 * tb + 512 * dc, 512),
                                         start=(dc == 0), stop=(dc == DCH - 1),
                                         skip_group_check=True), s_t)
                        th = s_t.n
                        b.W("scalar", s_t, th)
                        b.W("scalar", s_v, s_v.n)
                        b.I("scalar", lambda s, bk1=bk1: s.activation(
                            self.sqtmp[:, :], self.pb[bk1][:, 0:512], AF.Silu), s_sc)
                        b.I("scalar", lambda s: s.drain(), bs[bk1])
                        b.W("vector", s_sc, s_sc.n)
                        b.I("vector", lambda v, d=hid(tb, oc8), bk2=bk2: v.tensor_tensor(
                            d, self.sqtmp[:, :], self.pb[bk2][:, 0:512], ALU.mult),
                            bs[bk2])
                        b.F("vector", s_v)
                    wsl.release(j1)
                    wsl.release(j2)
                VHID = s_v.n
                for oc2 in range(DCH):
                    j3, b3, tk3 = wsl.get(
                        m3i.ap()[e][:, oc2 * 2048 + half * 1024:
                                    oc2 * 2048 + half * 1024 + 1024])
                    b.W("tensor", tk3[0], tk3[1])
                    b.W("tensor", s_v, VHID)
                    for tb in range(cfg.nblk):
                        bank = 4 + (oc2 + tb) % 2
                        self.bwait(bank)
                        for oc8 in range(8):
                            b.I("tensor", lambda t, b3=b3, oc8=oc8, tb=tb, bank=bank:
                                t.matmul(self.pb[bank][:, 0:512],
                                         b3[:, oc8 * 128:(oc8 + 1) * 128],
                                         self.arn(16384 + (8 * tb + oc8) * 512, 512),
                                         start=(oc8 == 0), stop=(oc8 == 7),
                                         skip_group_check=True), s_t)
                        tdn = s_t.n
                        c0 = 512 * tb
                        xr_ap = self.xres[:, oc2 * tpc + c0:oc2 * tpc + c0 + 512]
                        b.W("vector", s_t, tdn)
                        b.W("vector", s_sc, s_sc.n)
                        b.E("vector", lambda v, tb=tb, bank=bank: v.tensor_tensor(
                            self.sqb[:, 0:512], self.pb[bank][:, 0:512],
                            self.webc_v(tb), ALU.mult))
                        b.I("vector", lambda v, bank=bank: v.drain(), bs[bank])
                        b.I("vector", lambda v, x=xr_ap: v.tensor_add(
                            x, x, self.sqb[:, 0:512]), self.s_fin)
                        b.F("vector", s_v)
                    wsl.release(j3)
        b.F("vector", s_v)
        self.V_MOE = s_v.n
        b.F("tensor", s_t)
        self.T_MOE = (s_t, s_t.n)

    def ph7_l1_pre(self):
        b, cfg = self.b, self.cfg
        tpc = cfg.tpc
        s_t, s_v, s_sc = self.s_t, self.s_v, self.s_sc
        bs = self.bsem
        s_r1 = b.sem("s_r1")
        b.W("sync", s_t, self.T_MOE[1])
        b.I("sync", lambda e: e.dma_start(out=self.rt_sb[:, :],
                                          in_=self.dr["w_rt1"][:, :]), s_r1, k=16)
        self.R1N = s_r1.n
        self.s_r1 = s_r1
        tokk = self.wq.load(self.wkv[:, 0:512], self.dr["w_wk"].ap(),
                            pre_wait=self.T_MOE)
        tokv = self.wq.load(self.wkv[:, 512:1024], self.dr["w_wv"].ap())
        self.WKV = (tokk, tokv)
        b.W("gpsimd", s_t, self.T_MOE[1])
        b.E("gpsimd", lambda g: g.memset(self.onescb[:, :], 1.0))
        b.F("gpsimd", self.s_g)
        self.G_OCB = self.s_g.n
        for tb in range(cfg.nblk):
            c0 = 512 * tb
            self.bwait(2)
            self.bwait(3)
            for dc in range(DCH):
                xr = self.xres[:, dc * tpc + c0:dc * tpc + c0 + 512]
                b.W("scalar", s_v, self.V_MOE)
                if dc > 0:
                    b.W("scalar", s_t, s_t.n)
                b.I("scalar", lambda s, x=xr: s.activation(
                    self.sqb[:, 0:512], x, AF.Square), s_sc)
                self.MMg(self.pb[2][0:1, 0:512], self.onesc[:, :], xr,
                         start=(dc == 0), stop=(dc == DCH - 1),
                         waits=(((s_v, self.V_MOE),) if dc == 0 and tb == 0 else ()))
                b.W("tensor", s_sc, s_sc.n)
                self.MMg(self.pb[3][0:1, 0:512], self.onesc[:, :],
                         self.sqb[:, 0:512], start=(dc == 0), stop=(dc == DCH - 1))
            tss = s_t.n
            m_ap = self.arsl[0:1, c0:c0 + 512]
            b.W("vector", s_t, tss)
            b.I("vector", lambda v, m=m_ap: v.tensor_scalar_mul(
                m, self.pb[2][0:1, 0:512], -1.0 / D), bs[2])
            b.D("vector")
            b.I("vector", lambda v: v.tensor_scalar_mul(
                self.l2row[0:1, 0:512], self.pb[3][0:1, 0:512], 1.0 / D), bs[3])
            b.D("vector")
            b.E("vector", lambda v, m=m_ap: v.tensor_mul(
                self.sqtmp[0:1, 0:512], m, m))
            b.D("vector")
            b.E("vector", lambda v: v.tensor_sub(
                self.l2row[0:1, 0:512], self.l2row[0:1, 0:512],
                self.sqtmp[0:1, 0:512]))
            b.F("vector", s_v)
            b.W("scalar", s_v, s_v.n)
            b.I("scalar", lambda s: s.activation(
                self.l2row[0:1, 0:512], self.l2row[0:1, 0:512], AF.Sqrt,
                bias=self.cst(EPS_LN, 1)), s_sc)
            b.W("vector", s_sc, s_sc.n)
            b.E("vector", lambda v: v.reciprocal(self.l2row[0:1, 0:512],
                                                 self.l2row[0:1, 0:512]))
            b.F("vector", s_v)
            vrs = s_v.n
            self.bwait(4)
            self.MMg(self.pb[4][32:33, 0:512], self.onesall[0:1, 0:1],
                     self.l2row[0:1, 0:512], tp=(0, 32), waits=((s_v, vrs),))
            b.W("vector", s_t, s_t.n)
            b.I("vector", lambda v, c0=c0: v.tensor_copy(
                self.arsl[32:33, c0:c0 + 512], self.pb[4][32:33, 0:512]), bs[4])
            b.F("vector", s_v)
        self.V_SLIVERS = self.s_v.n
        wrt = self.rt_sb[:, 0:DCH * 128]
        self.bwait(2)
        b.W("tensor", self.s_r1, self.R1N)
        for dc in range(DCH):
            self.MMg(self.pb[2][0:16, 0:1], wrt[:, 128 * dc:128 * dc + 16],
                     self.onesc[:, :], start=(dc == 0), stop=(dc == DCH - 1))
        b.W("vector", s_t, s_t.n)
        b.I("vector", lambda v: v.tensor_copy(self.cs16[:, 0:1],
                                              self.pb[2][0:16, 0:1]), bs[2])
        b.F("vector", s_v)
        vcs = s_v.n
        for tb in range(cfg.nblk):
            c0 = 512 * tb
            self.bwait(2)
            for dc in range(DCH):
                self.MMg(self.pb[2][0:16, 0:512], wrt[:, 128 * dc:128 * dc + 16],
                         self.xres[:, dc * tpc + c0: dc * tpc + c0 + 512],
                         start=(dc == 0), stop=(dc == DCH - 1))
            self.bwait(3)
            self.MMg(self.pb[3][0:16, 0:512], self.onesall[0:1, 0:16],
                     self.arsl[0:1, c0:c0 + 512],
                     waits=((s_v, self.V_SLIVERS),))
            self.bwait(4)
            self.MMg(self.pb[4][0:16, 0:512], self.onesall[32:33, 0:16],
                     self.arsl[32:33, c0:c0 + 512])
            tn = s_t.n
            b.W("vector", s_t, tn)
            b.W("vector", s_v, vcs)
            b.E("vector", lambda v: v.tensor_copy(self.sqtmp[0:16, 0:512],
                                                  self.pb[2][0:16, 0:512]))
            b.I("vector", lambda v: v.drain(), bs[2])
            b.E("vector", lambda v: v.scalar_tensor_tensor(
                self.sqtmp[0:16, 0:512], self.pb[3][0:16, 0:512], self.cs16[0:16, 0:1],
                self.sqtmp[0:16, 0:512], ALU.mult, ALU.add))
            b.I("vector", lambda v: v.drain(), bs[3])
            b.I("vector", lambda v: v.tensor_tensor(
                self.sqb[0:16, 0:512], self.sqtmp[0:16, 0:512],
                self.pb[4][0:16, 0:512], ALU.mult), bs[4])
            b.F("vector", s_v)
            g_ap = self.ar16[96:112, c0:c0 + 512]
            b.W("scalar", s_v, s_v.n)
            b.I("scalar", lambda s, g=g_ap: s.activation(
                g, self.sqb[0:16, 0:512], AF.Sigmoid), s_sc)
        b.F("scalar", s_sc)
        self.SC1 = s_sc.n
        for tb in range(cfg.nblk):
            self.xn_block(tb, dst=self.arn(4096 * tb, 4096))
        self.V_XN2 = self.s_v.n
        kfb = self.arn(16384, 512)
        vtb = self.arn(16896, 512)
        b.W("tensor", self.s_g, self.G_OCB)
        b.W("vector", s_sc, s_sc.n)
        b.E("vector", lambda v: v.memset(self.kvz[:, :], 0.0))
        b.F("vector", s_v)
        for tb in range(cfg.nblk):
            for j in range(8):
                xp_ap = self.arn(4096 * tb + 512 * j, 512)
                self.bwait(0)
                b.W("tensor", self.WKV[0][0], self.WKV[0][1])
                b.W("tensor", self.WKV[1][0], self.WKV[1][1])
                b.W("tensor", s_v, self.V_XN2)
                b.I("tensor", lambda t, j=j, x=xp_ap: t.matmul(
                    self.pb[0][0:64, 0:512], self.wkv[0:64, 64 * j:64 * (j + 1)],
                    x[0:64, :], start=True, stop=True, tile_position=(0, 0),
                    skip_group_check=True), s_t)
                b.I("tensor", lambda t, j=j, x=xp_ap: t.matmul(
                    self.pb[0][64:128, 0:512], self.wkv[64:128, 64 * j:64 * (j + 1)],
                    x[64:128, :], start=True, stop=True, tile_position=(64, 64),
                    skip_group_check=True), s_t)
                self.bwait(1)
                b.I("tensor", lambda t, j=j, x=xp_ap: t.matmul(
                    self.pb[1][0:64, 0:512], self.wkv[0:64, 512 + 64 * j:512 + 64 * (j + 1)],
                    x[0:64, :], start=True, stop=True, tile_position=(0, 0),
                    skip_group_check=True), s_t)
                b.I("tensor", lambda t, j=j, x=xp_ap: t.matmul(
                    self.pb[1][64:128, 0:512], self.wkv[64:128, 512 + 64 * j:512 + 64 * (j + 1)],
                    x[64:128, :], start=True, stop=True, tile_position=(64, 64),
                    skip_group_check=True), s_t)
                tkv = s_t.n
                b.W("vector", s_t, tkv)
                b.W("vector", s_sc, s_sc.n)
                b.E("vector", lambda v: v.tensor_scalar(
                    self.sqtmp[:, :], self.pb[0][:, 0:512], 0.0, 0.0,
                    ALU.min, ALU.add))
                b.D("vector")
                b.I("vector", lambda v, d=vtb: v.tensor_copy(d, self.pb[1][:, 0:512]),
                    bs[1])
                b.F("vector", s_v)
                vmin = s_v.n
                b.W("scalar", s_v, vmin)
                b.I("scalar", lambda s: s.activation(
                    self.sqtmp[:, :], self.sqtmp[:, :], AF.Exp), s_sc)
                b.D("scalar")
                b.I("scalar", lambda s: s.activation(
                    self.cext[:, 0:512], self.pb[0][:, 0:512], AF.Relu), s_sc)
                b.I("scalar", lambda s: s.drain(), bs[0])
                b.W("vector", s_sc, s_sc.n)
                b.I("vector", lambda v, d=kfb: v.tensor_tensor(
                    d, self.sqtmp[:, :], self.cext[:, 0:512], ALU.add), s_v)
                b.F("vector", s_v)
                vkf = s_v.n
                for st in range(4):
                    ktm = self.arn(17408 + 128 * (st % 2), 128)
                    vtm = self.arn(17664 + 128 * (st % 2), 128)
                    self.bwait(6)
                    t6 = self.TRg(self.pb[6][:, 0:128], kfb[:, 128 * st:128 * (st + 1)],
                                  self.ident[:, :], waits=((s_v, vkf),))
                    self.bwait(7)
                    t7 = self.TRg(self.pb[7][:, 0:128], vtb[:, 128 * st:128 * (st + 1)],
                                  self.ident[:, :])
                    b.W("vector", s_t, t7)
                    b.I("vector", lambda v, d=ktm: v.tensor_copy(d, self.pb[6][:, 0:128]),
                        bs[6])
                    b.D("vector")
                    b.I("vector", lambda v, d=vtm: v.tensor_copy(d, self.pb[7][:, 0:128]),
                        bs[7])
                    b.F("vector", s_v)
                    vst = s_v.n
                    if st == 0:
                        self.bwait(4)
                        self.bwait(5)
                    b.W("tensor", s_v, vst)
                    b.I("tensor", lambda t, j=j, ktm=ktm, vtm=vtm, st=st:
                        t.matmul(self.pb[5][0:64, 64 * j:64 * (j + 1)], ktm[:, 0:64],
                                 vtm[:, 0:64], start=(st == 0), stop=(st == 3),
                                 tile_position=(0, 0), skip_group_check=True), s_t)
                    b.I("tensor", lambda t, j=j, ktm=ktm, vtm=vtm, st=st:
                        t.matmul(self.pb[5][64:128, 64 * j:64 * (j + 1)], ktm[:, 64:128],
                                 vtm[:, 64:128], start=(st == 0), stop=(st == 3),
                                 tile_position=(0, 64), skip_group_check=True), s_t)
                    b.I("tensor", lambda t, j=j, ktm=ktm, st=st:
                        t.matmul(self.pb[4][0:64, j:j + 1], ktm[:, 0:64],
                                 self.onescb[:, :], start=(st == 0), stop=(st == 3),
                                 tile_position=(0, 0), skip_group_check=True), s_t)
                    b.I("tensor", lambda t, j=j, ktm=ktm, st=st:
                        t.matmul(self.pb[4][64:128, j:j + 1], ktm[:, 64:128],
                                 self.onescb[:, :], start=(st == 0), stop=(st == 3),
                                 tile_position=(0, 64), skip_group_check=True), s_t)
                tacc = s_t.n
                b.W("vector", s_t, tacc)
                b.E("vector", lambda v, j=j: v.tensor_tensor(
                    self.kvz[:, 64 * j:64 * (j + 1)], self.kvz[:, 64 * j:64 * (j + 1)],
                    self.pb[5][:, 64 * j:64 * (j + 1)], ALU.add))
                b.I("vector", lambda v: v.drain(), bs[5])
                b.E("vector", lambda v, j=j: v.tensor_tensor(
                    self.kvz[:, 512 + j:513 + j], self.kvz[:, 512 + j:513 + j],
                    self.pb[4][:, j:j + 1], ALU.add))
                b.I("vector", lambda v: v.drain(), bs[4])
                b.F("vector", s_v)
        self.V_KVL = self.s_v.n

    def ph8_ag2(self):
        b, cfg = self.b, self.cfg
        cc1_in, cc1_out = self.ccs[2], self.ccs[3]
        s_t, s_v, s_sc = self.s_t, self.s_v, self.s_sc
        bs = self.bsem
        s_ag = b.sem("s_ag2")
        s_agl = b.sem("s_agl2")
        b.W("sync", s_v, self.V_KVL)
        b.I("sync", lambda e: e.dma_start(out=cc1_in.ap()[:, :], in_=self.kvz[:, :]),
            s_ag, k=16)
        b.W("gpsimd", s_ag, s_ag.n)
        b.I("gpsimd", lambda g: g.collective_compute(
            "AllGather", ALU.bypass, replica_groups=[list(range(NCORE))],
            ins=[cc1_in.ap().opt()], outs=[cc1_out.ap().opt()]), self.s_cc)
        CC1 = self.s_cc.n
        b.W("vector", s_ag, s_ag.n)
        b.E("vector", lambda v: v.memset(self.kvz[:, :], 0.0))
        b.F("vector", s_v)
        b.W("sync", self.s_cc, CC1)
        for l in range(NCORE):
            b.W("sync", s_v, s_v.n)
            b.I("sync", lambda e, l=l: e.dma_start(
                out=self.kvtmp, in_=cc1_out.ap()[128 * l:128 * (l + 1), :]),
                s_agl, k=16)
            b.W("vector", s_agl, s_agl.n)
            b.E("vector", lambda v, l=l: v.scalar_tensor_tensor(
                self.kvz[:, :], self.kvtmp, self.nubc[:, l:l + 1], self.kvz[:, :],
                ALU.mult, ALU.add))
            b.F("vector", s_v)
        b.E("vector", lambda v: v.tensor_copy(self.kvzb[:, :], self.kvz[:, :]))
        b.D("vector")
        b.E("vector", lambda v: v.memset(self.zbd[:, :], 0.0))
        b.D("vector")
        for j in range(8):
            b.E("vector", lambda v, j=j: v.tensor_copy(
                self.zbd[0:64, 2 * j:2 * j + 1], self.kvzb[0:64, 512 + j:513 + j]))
            b.D("vector")
            b.E("vector", lambda v, j=j: v.tensor_copy(
                self.zbd[64:128, 2 * j + 1:2 * j + 2], self.kvzb[64:128, 512 + j:513 + j]))
            b.D("vector")
        b.F("vector", s_v)
        self.bwait(0)
        self.MMg(self.pb[0][:, 0:16], self.onesall[0:1, 0:128], self.ebrow[0:1, 0:16],
                 waits=((s_v, self.V_NEGAL),))
        b.W("vector", s_t, s_t.n)
        b.I("vector", lambda v: v.tensor_copy(self.ebb[:, 0:16], self.pb[0][:, 0:16]),
            bs[0])
        b.F("vector", s_v)
        self.V_KVZ = s_v.n

    def ph9_l1_blocks(self):
        b, cfg = self.b, self.cfg
        tpc = cfg.tpc
        s_t, s_v, s_sc = self.s_t, self.s_v, self.s_sc
        bs = self.bsem
        b.W("tensor", s_sc, self.SC1)
        b.W("vector", s_v, self.V_KVZ)
        self.route_tiles(96, E1, 0, use_bias=True, probs=False)
        b.F("tensor", s_t)
        wsl = WS(self, [self.arn(36864 + 1024 * i, 1024) for i in range(6)])
        wsl.barrier((s_t, s_t.n))
        tokq = self.wq.load(self.wq_sb[:, :], self.dr["w_wq"].ap(),
                            pre_wait=(s_t, s_t.n))
        self.WQTOK = tokq
        qfb = self.arn(43520, 512)
        ctx = lambda j: self.arn(16384 + 512 * j, 512)
        for tb in range(cfg.nblk):
            c0 = 512 * tb
            for j in range(8):
                xp_ap = self.arn(4096 * tb + 512 * j, 512)
                self.bwait(0)
                b.W("tensor", self.WQTOK[0], self.WQTOK[1])
                b.I("tensor", lambda t, j=j, x=xp_ap: t.matmul(
                    self.pb[0][0:64, 0:512], self.wq_sb[0:64, 64 * j:64 * (j + 1)],
                    x[0:64, :], start=True, stop=True, tile_position=(0, 0),
                    skip_group_check=True), s_t)
                b.I("tensor", lambda t, j=j, x=xp_ap: t.matmul(
                    self.pb[0][64:128, 0:512], self.wq_sb[64:128, 64 * j:64 * (j + 1)],
                    x[64:128, :], start=True, stop=True, tile_position=(64, 64),
                    skip_group_check=True), s_t)
                tq = s_t.n
                b.W("vector", s_t, tq)
                b.W("vector", s_sc, s_sc.n)
                b.E("vector", lambda v: v.tensor_scalar(
                    self.sqtmp[:, :], self.pb[0][:, 0:512], 0.0, 0.0,
                    ALU.min, ALU.add))
                b.F("vector", s_v)
                vmin = s_v.n
                b.W("scalar", s_v, vmin)
                b.I("scalar", lambda s: s.activation(
                    self.sqtmp[:, :], self.sqtmp[:, :], AF.Exp), s_sc)
                b.D("scalar")
                b.I("scalar", lambda s: s.activation(
                    self.cext[:, 0:512], self.pb[0][:, 0:512], AF.Relu), s_sc)
                b.I("scalar", lambda s: s.drain(), bs[0])
                b.W("vector", s_sc, s_sc.n)
                b.I("vector", lambda v, d=qfb: v.tensor_tensor(
                    d, self.sqtmp[:, :], self.cext[:, 0:512], ALU.add), s_v)
                b.F("vector", s_v)
                vqf = s_v.n
                self.bwait(1)
                b.W("tensor", s_v, vqf)
                b.I("tensor", lambda t, j=j: t.matmul(
                    self.pb[1][0:64, 0:512], self.kvzb[0:64, 64 * j:64 * (j + 1)],
                    qfb[0:64, :], start=True, stop=True, tile_position=(0, 0),
                    skip_group_check=True), s_t)
                b.I("tensor", lambda t, j=j: t.matmul(
                    self.pb[1][64:128, 0:512], self.kvzb[64:128, 64 * j:64 * (j + 1)],
                    qfb[64:128, :], start=True, stop=True, tile_position=(64, 64),
                    skip_group_check=True), s_t)
                self.bwait(2)
                b.I("tensor", lambda t, j=j: t.matmul(
                    self.pb[2][0:2, 0:512], self.zbd[:, 2 * j:2 * j + 2],
                    qfb[:, :], start=True, stop=True, skip_group_check=True), s_t)
                tden = s_t.n
                b.W("scalar", s_t, tden)
                b.W("scalar", s_v, s_v.n)
                b.I("scalar", lambda s: s.activation(
                    self.l2row[0:2, 0:512], self.pb[2][0:2, 0:512], AF.Abs,
                    bias=self.cst(EPS_DEN, 2)), s_sc)
                b.I("scalar", lambda s: s.drain(), bs[2])
                b.W("vector", s_sc, s_sc.n)
                b.E("vector", lambda v: v.reciprocal(self.l2row[0:2, 0:512],
                                                     self.l2row[0:2, 0:512]))
                b.F("vector", s_v)
                vden = s_v.n
                self.bwait(3)
                self.MMg(self.pb[3][:, 0:512], self.ones2T[:, :],
                         self.l2row[0:2, 0:512], waits=((s_v, vden),))
                tbc = s_t.n
                b.W("vector", s_t, tbc)
                b.E("vector", lambda v: v.tensor_copy(
                    self.sqtmp[:, :], self.pb[1][:, 0:512]))
                b.I("vector", lambda v: v.drain(), bs[1])
                b.I("vector", lambda v, d=ctx(j): v.tensor_tensor(
                    d, self.sqtmp[:, :], self.pb[3][:, 0:512], ALU.mult), bs[3])
                b.F("vector", s_v)
            VCTX = s_v.n
            img = self.dr["w_wo"]
            for oc in range(DCH):
                j2, buf, tok = wsl.get(img[:, oc * DCH * 128:(oc + 1) * DCH * 128])
                bank = 4 + oc % 2
                b.W("tensor", tok[0], tok[1])
                b.W("tensor", s_v, VCTX)
                self.bwait(bank)
                for dcp in range(DCH):
                    b.I("tensor", lambda t, buf=buf, dcp=dcp, bank=bank: t.matmul(
                        self.pb[bank][:, 0:512], buf[:, dcp * 128:(dcp + 1) * 128],
                        self.arn(16384 + 512 * dcp, 512),
                        start=(dcp == 0), stop=(dcp == DCH - 1),
                        skip_group_check=True), s_t)
                wsl.release(j2)
                two = s_t.n
                xr_ap = self.xres[:, oc * tpc + c0:oc * tpc + c0 + 512]
                b.W("vector", s_t, two)
                b.D("vector")
                b.I("vector", lambda v, x=xr_ap, bank=bank: v.tensor_tensor(
                    x, x, self.pb[bank][:, 0:512], ALU.add), bs[bank])
            b.F("vector", s_v)
        s1i, s2i, s3i = self.dr["w_s1"], self.dr["w_s2"], self.dr["w_s3"]
        for q4 in range(4):
            self.moe_pass(wsl, lambda oc8, q4=q4: s1i[:, (8 * q4 + oc8) * 1024:
                                                      (8 * q4 + oc8 + 1) * 1024],
                          lambda oc8, q4=q4: s2i[:, (8 * q4 + oc8) * 1024:
                                                  (8 * q4 + oc8 + 1) * 1024],
                          lambda oc2, q4=q4: s3i[:, oc2 * 4096 + q4 * 1024:
                                                  oc2 * 4096 + q4 * 1024 + 1024],
                          webc=False)
        r1i, r2i, r3i = self.dr["w_r1"], self.dr["w_r2"], self.dr["w_r3"]
        for e in range(E1):
            self.webc_build(e, E1, 0)
            self.moe_pass(wsl, lambda oc8, e=e: r1i.ap()[e][:, oc8 * 1024:(oc8 + 1) * 1024],
                          lambda oc8, e=e: r2i.ap()[e][:, oc8 * 1024:(oc8 + 1) * 1024],
                          lambda oc2, e=e: r3i.ap()[e][:, oc2 * 1024:(oc2 + 1) * 1024],
                          webc=True)
        b.F("vector", s_v)
        self.V_L1 = s_v.n

    def moe_pass(self, wsl, f1, f2, f3, webc):
        b, cfg = self.b, self.cfg
        tpc = cfg.tpc
        s_t, s_v, s_sc = self.s_t, self.s_v, self.s_sc
        bs = self.bsem
        for oc8 in range(8):
            bk1 = 2 * (oc8 % 2)
            bk2 = 1 + 2 * (oc8 % 2)
            j1, b1, tk1 = wsl.get(f1(oc8))
            j2, b2, tk2 = wsl.get(f2(oc8))
            b.W("tensor", tk1[0], tk1[1])
            b.W("tensor", tk2[0], tk2[1])
            for tb in range(cfg.nblk):
                self.bwait(bk1)
                self.bwait(bk2)
                for dc in range(DCH):
                    b.I("tensor", lambda t, b1=b1, dc=dc, tb=tb, bk1=bk1: t.matmul(
                        self.pb[bk1][:, 0:512], b1[:, dc * 128:(dc + 1) * 128],
                        self.arn(4096 * tb + 512 * dc, 512),
                        start=(dc == 0), stop=(dc == DCH - 1),
                        skip_group_check=True), s_t)
                for dc in range(DCH):
                    b.I("tensor", lambda t, b2=b2, dc=dc, tb=tb, bk2=bk2: t.matmul(
                        self.pb[bk2][:, 0:512], b2[:, dc * 128:(dc + 1) * 128],
                        self.arn(4096 * tb + 512 * dc, 512),
                        start=(dc == 0), stop=(dc == DCH - 1),
                        skip_group_check=True), s_t)
                th = s_t.n
                b.W("scalar", s_t, th)
                b.W("scalar", s_v, s_v.n)
                b.I("scalar", lambda s, bk1=bk1: s.activation(
                    self.sqtmp[:, :], self.pb[bk1][:, 0:512], AF.Silu), s_sc)
                b.I("scalar", lambda s: s.drain(), bs[bk1])
                b.W("vector", s_sc, s_sc.n)
                b.I("vector", lambda v, tb=tb, oc8=oc8, bk2=bk2: v.tensor_tensor(
                    self.arn(20480 + (8 * tb + oc8) * 512, 512), self.sqtmp[:, :],
                    self.pb[bk2][:, 0:512], ALU.mult), bs[bk2])
                b.F("vector", s_v)
            wsl.release(j1)
            wsl.release(j2)
        VHID = s_v.n
        for oc2 in range(DCH):
            j3, b3, tk3 = wsl.get(f3(oc2))
            b.W("tensor", tk3[0], tk3[1])
            b.W("tensor", s_v, VHID)
            for tb in range(cfg.nblk):
                bank = 4 + (oc2 + tb) % 2
                self.bwait(bank)
                for oc8 in range(8):
                    b.I("tensor", lambda t, b3=b3, oc8=oc8, tb=tb, bank=bank: t.matmul(
                        self.pb[bank][:, 0:512], b3[:, oc8 * 128:(oc8 + 1) * 128],
                        self.arn(20480 + (8 * tb + oc8) * 512, 512),
                        start=(oc8 == 0), stop=(oc8 == 7),
                        skip_group_check=True), s_t)
                tdn = s_t.n
                c0 = 512 * tb
                xr_ap = self.xres[:, oc2 * tpc + c0:oc2 * tpc + c0 + 512]
                b.W("vector", s_t, tdn)
                if webc:
                    b.W("vector", s_sc, s_sc.n)
                    b.E("vector", lambda v, tb=tb, bank=bank: v.tensor_tensor(
                        self.sqb[:, 0:512], self.pb[bank][:, 0:512],
                        self.webc_v(tb), ALU.mult))
                    b.I("vector", lambda v, bank=bank: v.drain(), bs[bank])
                    b.I("vector", lambda v, x=xr_ap: v.tensor_add(
                        x, x, self.sqb[:, 0:512]), self.s_fin)
                else:
                    b.D("vector")
                    b.I("vector", lambda v, x=xr_ap, bank=bank: v.tensor_tensor(
                        x, x, self.pb[bank][:, 0:512], ALU.add), bs[bank])
                b.F("vector", s_v)
            wsl.release(j3)

    def dump(self, name, ap, width, dtype, parts=128):
        if name not in self.dbg:
            return
        b = self.b
        if not hasattr(self, "s_dbg"):
            self.s_dbg = b.sem("s_dbg")
        d = self.nc.dram_tensor("dbg_" + name, [parts, width], dtype,
                                kind="ExternalOutput")
        b.F("vector", self.s_v)
        b.F("scalar", self.s_sc)
        b.W("sync", self.s_v, self.s_v.n)
        b.W("sync", self.s_sc, self.s_sc.n)
        b.I("sync", lambda e, d=d, ap=ap: e.dma_start(out=d.ap(), in_=ap),
            self.s_dbg, k=16)
        b.W("vector", self.s_dbg, self.s_dbg.n)
        b.W("scalar", self.s_dbg, self.s_dbg.n)
        b.W("gpsimd", self.s_dbg, self.s_dbg.n)

    def ph_out(self):
        b, cfg = self.b, self.cfg
        tpc, ntile = cfg.tpc, cfg.ntile
        s_t, s_v = self.s_t, self.s_v
        s_out = [b.sem("out0"), b.sem("out1")]
        s_oq = b.sem("oq")
        b.F("vector", s_v)
        XF = s_v.n
        b.W("tensor", s_v, XF)
        buffree = [None, None]
        for i in range(ntile):
            jb = i % 2
            buf = self.xtm[jb]
            for dc in range(DCH):
                bank = dc % 2
                self.bwait(bank)
                b.I("tensor", lambda t, i=i, dc=dc, bank=bank: t.transpose(
                    self.pb[bank][:, 0:128],
                    self.xres[:, dc * tpc + 128 * i:dc * tpc + 128 * (i + 1)],
                    self.identf[:, :]), s_t)
                tn = s_t.n
                b.W("vector", s_t, tn)
                if dc < 2 and buffree[jb] is not None:
                    b.W("vector", buffree[jb][0], buffree[jb][1])
                b.I("vector", lambda v, buf=buf, dc=dc, bank=bank: v.tensor_copy(
                    buf[:, 128 * dc:128 * (dc + 1)], self.pb[bank][:, 0:128]),
                    self.bsem[bank])
            b.F("vector", s_v)
            vst = s_v.n
            b.W("sync", s_v, vst)
            b.I("sync", lambda e, i=i, buf=buf: e.dma_start(
                out=self.y.ap()[128 * i:128 * (i + 1), :], in_=buf[:, :]),
                s_out[jb], k=16)
            buffree[jb] = (s_out[jb], s_out[jb].n)
        b.W("vector", s_out[0], s_out[0].n)
        b.W("vector", s_out[1], s_out[1].n)
        if hasattr(self, "s_dbg"):
            b.W("vector", self.s_dbg, self.s_dbg.n)
        b.E("vector", lambda v: v.memset(self.onrow[:, :], 0.0))


def _img_lhsT(w):
    din, dout = w.shape
    ndc, noc = (din + 127) // 128, (dout + 127) // 128
    padr = ndc * 128 - din
    padc = noc * 128 - dout
    if padr or padc:
        w2 = np.zeros((ndc * 128, noc * 128), w.dtype)
        w2[:din, :dout] = w
        w = w2
    img = w.reshape(ndc, 128, noc, 128).transpose(1, 2, 0, 3).reshape(128, noc * ndc * 128)
    return np.ascontiguousarray(img)


def prep_weights(I):
    W = {}
    W["w_norm"] = np.ascontiguousarray(np.asarray(I["L0_attn_norm_w"]).reshape(DCH, 128).T)
    W["w_pnw"] = np.ascontiguousarray(np.asarray(I["L0_attn_post_norm_w"]).reshape(DCH, 128).T)
    W["w_q"] = _img_lhsT(I["L0_attn_q_w"])
    W["w_k"] = _img_lhsT(I["L0_attn_k_w"])
    W["w_v"] = _img_lhsT(I["L0_attn_v_w"])
    ba = np.concatenate([I["L0_attn_b_w"], I["L0_attn_a_w"]], 1)
    W["w_ba"] = _img_lhsT(ba)
    conv = np.concatenate([I["L0_attn_conv_q_w"], I["L0_attn_conv_k_w"],
                           I["L0_attn_conv_v_w"]], 0)
    W["w_conv"] = np.ascontiguousarray(
        conv.reshape(-1, 128, 4).transpose(1, 0, 2).reshape(128, -1))
    W["w_Alog"] = np.ascontiguousarray(np.asarray(I["L0_attn_A_log"])[:, None])
    W["w_dtb"] = np.ascontiguousarray(np.asarray(I["L0_attn_dt_bias"])[:, None])
    W["w_g"] = _img_lhsT(I["L0_attn_g_w"])
    W["w_onw"] = np.ascontiguousarray(np.asarray(I["L0_attn_o_norm_w"])[:, None])
    W["w_o"] = _img_lhsT(I["L0_attn_o_w"])
    W["w_gate"] = _img_lhsT(I["L0_attn_gate_w"])
    W["w_up"] = _img_lhsT(I["L0_attn_up_w"])
    W["w_down"] = _img_lhsT(I["L0_attn_down_w"])
    W["w_rt0"] = _img_lhsT(np.asarray(I["L0_ffn_Wg"]).T)
    W["w_m1"] = np.stack([_img_lhsT(np.asarray(I["L0_ffn_W1"])[e].T) for e in range(E0)])
    W["w_m2"] = np.stack([_img_lhsT(np.asarray(I["L0_ffn_W2"])[e].T) for e in range(E0)])
    W["w_m3"] = np.stack([_img_lhsT(np.asarray(I["L0_ffn_W3"])[e].T) for e in range(E0)])
    for nm, key in (("w_wq", "L1_attn_Wq"), ("w_wk", "L1_attn_Wk"), ("w_wv", "L1_attn_Wv")):
        w = np.asarray(I[key])
        img = np.zeros((128, (NH // 2) * DKH), np.float32)
        for j in range(NH // 2):
            img[0:64, j * 64:(j + 1) * 64] = w[2 * j]
            img[64:128, j * 64:(j + 1) * 64] = w[2 * j + 1]
        W[nm] = img
    W["w_wo"] = _img_lhsT(I["L1_attn_Wo"])
    W["w_rt1"] = _img_lhsT(I["L1_ffn_router_w"])
    W["w_eb"] = np.ascontiguousarray(np.asarray(I["L1_ffn_expert_bias"])[None, :])
    W["w_s1"] = _img_lhsT(np.asarray(I["L1_ffn_shared_w1"])[0])
    W["w_s2"] = _img_lhsT(np.asarray(I["L1_ffn_shared_w2"])[0])
    W["w_s3"] = _img_lhsT(np.asarray(I["L1_ffn_shared_w3"])[0])
    W["w_r1"] = np.stack([_img_lhsT(np.asarray(I["L1_ffn_routed_w1"])[e]) for e in range(E1)])
    W["w_r2"] = np.stack([_img_lhsT(np.asarray(I["L1_ffn_routed_w2"])[e]) for e in range(E1)])
    W["w_r3"] = np.stack([_img_lhsT(np.asarray(I["L1_ffn_routed_w3"])[e]) for e in range(E1)])
    return {k: np.ascontiguousarray(np.asarray(v, np.float32)) for k, v in W.items()}


def make_inmaps(inputs, cfg):
    x = np.asarray(inputs["L0_attn_hidden_states"])
    Bb, T, _ = x.shape
    nsc = NCORE // Bb
    W = prep_weights(inputs)
    in_maps = []
    for c in range(NCORE):
        seq, pos = c // nsc, c % nsc
        s = pos * cfg.tpc
        xs = np.ascontiguousarray(x[seq, s:s + cfg.tpc])
        xp = np.zeros((4, D), np.float32)
        if pos > 0:
            xp[0:3] = x[seq, s - 3:s]
        mu = np.zeros(NCORE, np.float32)
        nu = np.zeros(NCORE, np.float32)
        for l in range(NCORE):
            if l // nsc == seq:
                nu[l] = 1.0
                if l % nsc < pos:
                    mu[l] = 1.0
        cm = np.stack([mu, nu]).astype(np.float32)
        m = {"x": xs, "xp": xp, "cmask": cm}
        m.update(W)
        in_maps.append(m)
    return in_maps


_CFG = Cfg(2048)
_NC_CACHE = {}


def kernel(**inputs):
    cfg = _CFG
    in_maps = make_inmaps(inputs, cfg)
    key = cfg.tpc
    if key not in _NC_CACHE:
        _NC_CACHE[key] = build(cfg)
    nc = _NC_CACHE[key]
    res = run_bass_kernel_spmd(nc, in_maps, core_ids=list(range(NCORE)))
    x = np.asarray(inputs["L0_attn_hidden_states"])
    Bb, T, _ = x.shape
    nsc = NCORE // Bb
    out = np.empty((Bb, T, D), np.float32)
    for c in range(NCORE):
        seq, pos = c // nsc, c % nsc
        out[seq, pos * cfg.tpc:(pos + 1) * cfg.tpc] = res.results[c]["y"]
    return out


# version 06
# speedup vs baseline: 1.0789x; 1.0160x over previous
import numpy as np
import concourse.bass as bass
import concourse.mybir as mybir
from concourse.bass_utils import run_bass_kernel_spmd
from contextlib import ExitStack

F32 = mybir.dt.float32
BF16 = mybir.dt.bfloat16
AF = mybir.ActivationFunctionType
ALU = mybir.AluOpType
AX = mybir.AxisListType

D = 1024
NH = 16
DKH = 64
DVH = 128
DQK = 1024
DVAL = 2048
FF = 2752
E0, H0 = 8, 2048
E1, RH = 16, 1024
SH = 4096
NCORE = 8
EPS_LN = 1e-5
EPS_RMS = 1e-6
EPS_L2 = 1e-6
EPS_DEN = 1e-6
SCALE_DK = DKH ** -0.5
DCH = 8
VCH = 16
FFCH = 22
FF_LAST = FF - 21 * 128
SHCH = SH // 128
RHCH = RH // 128


class Cfg:
    def __init__(self, tpc=2048):
        assert tpc % 512 == 0
        self.tpc = tpc
        self.nblk = tpc // 512
        self.nch = tpc // 128
        self.ntile = tpc // 128


class Sem:
    __slots__ = ("h", "n")

    def __init__(self, h):
        self.h = h
        self.n = 0


class B:
    ENGS = ("sync", "tensor", "vector", "scalar", "gpsimd")

    def __init__(self, nc, cfg):
        self.nc = nc
        self.cfg = cfg
        self.ops = {e: [] for e in self.ENGS}
        self.ctx = ExitStack()

    def sem(self, name):
        return Sem(self.ctx.enter_context(self.nc.semaphore(name)))

    def sb(self, name, shape, dtype):
        return self.nc.alloc_sbuf_tensor(name, shape, dtype)

    def ps(self, name, shape, dtype=F32):
        return self.nc.alloc_psum_tensor(name, shape, dtype)

    def E(self, eng, fn):
        self.ops[eng].append(fn)

    def I(self, eng, fn, sem, k=1):
        sem.n += k

        def wrapped(e, fn=fn, h=sem.h, k=k):
            fn(e).then_inc(h, k)
        self.ops[eng].append(wrapped)

    def W(self, eng, sem, n=None):
        n = sem.n if n is None else n
        self.ops[eng].append(lambda e, h=sem.h, n=n: e.wait_ge(h, n))

    def D(self, eng):
        self.ops[eng].append(lambda e: e.drain())

    def F(self, eng, sem, k=1):
        sem.n += k
        self.ops[eng].append(lambda e, h=sem.h, k=k: e.drain().then_inc(h, k))

    def finalize(self, block):
        for name in self.ENGS:
            fns = self.ops[name]

            def runner(e, fns=fns):
                for f in fns:
                    f(e)
            getattr(block, name)(runner)


class WQ:

    def __init__(self, b, name, nrot=6):
        self.b = b
        self.sems = [b.sem(f"{name}{i}") for i in range(nrot)]
        self.i = 0

    def load(self, dst_ap, src_ap, pre_wait=None):
        s = self.sems[self.i % len(self.sems)]
        self.i += 1
        bb = self.b
        if pre_wait is not None:
            bb.W("gpsimd", pre_wait[0], pre_wait[1])
        bb.W("gpsimd", s, s.n)
        s.n += 16

        def fn(g, dst=dst_ap, src=src_ap, h=s.h):
            g.dma_start(out=dst, in_=src).then_inc(h, 16)
        bb.E("gpsimd", fn)
        return (s, s.n)


class WS:

    def __init__(self, K, views):
        self.K = K
        self.bufs = views
        self.free = [None] * len(views)
        self.i = 0

    def get(self, src_ap):
        j = self.i % len(self.bufs)
        self.i += 1
        tok = self.K.wq.load(self.bufs[j], src_ap, pre_wait=self.free[j])
        return j, self.bufs[j], tok

    def release(self, j):
        K = self.K
        K.b.F("tensor", K.s_t)
        self.free[j] = (K.s_t, K.s_t.n)

    def barrier(self, tok):
        self.free = [tok] * len(self.bufs)


def build(cfg: Cfg, debug=()):
    nc = bass.Bass("TRN2", target_bir_lowering=False)
    tpc = cfg.tpc
    dr = {}

    def din(name, shape, dtype=F32):
        dr[name] = nc.dram_tensor(name, shape, dtype, kind="ExternalInput")
        return dr[name]

    din("x", [tpc, D])
    din("xp", [4, D])
    din("cmask", [2, NCORE])
    din("w_norm", [128, DCH])
    din("w_pnw", [128, DCH])
    din("w_q", [128, DCH * DCH * 128])
    din("w_k", [128, DCH * DCH * 128])
    din("w_v", [128, VCH * DCH * 128])
    din("w_ba", [128, DCH * 128])
    din("w_conv", [128, 128])
    din("w_Alog", [NH, 1])
    din("w_dtb", [NH, 1])
    din("w_g", [128, VCH * DCH * 128])
    din("w_onw", [128, 1])
    din("w_o", [128, DCH * VCH * 128])
    din("w_gate", [128, FFCH * DCH * 128])
    din("w_up", [128, FFCH * DCH * 128])
    din("w_down", [128, DCH * FFCH * 128])
    din("w_rt0", [128, DCH * 128])
    din("w_m1", [E0, 128, (H0 // 128) * DCH * 128])
    din("w_m2", [E0, 128, (H0 // 128) * DCH * 128])
    din("w_m3", [E0, 128, DCH * (H0 // 128) * 128])
    din("w_wq", [128, (NH // 2) * DKH])
    din("w_wk", [128, (NH // 2) * DKH])
    din("w_wv", [128, (NH // 2) * DKH])
    din("w_wo", [128, DCH * DCH * 128])
    din("w_rt1", [128, DCH * 128])
    din("w_eb", [1, E1])
    din("w_s1", [128, SHCH * DCH * 128])
    din("w_s2", [128, SHCH * DCH * 128])
    din("w_s3", [128, DCH * SHCH * 128])
    din("w_r1", [E1, 128, RHCH * DCH * 128])
    din("w_r2", [E1, 128, RHCH * DCH * 128])
    din("w_r3", [E1, 128, DCH * RHCH * 128])

    y = nc.dram_tensor("y", [tpc, D], F32, kind="ExternalOutput")
    oloc_d = nc.dram_tensor("oloc_d", [cfg.nch, 128, NH * 128], BF16)
    g_d = nc.dram_tensor("g_d", [cfg.nch, 128, 8 * 128], BF16)
    cc0_in = nc.dram_tensor("cc0_in", [NH * DKH, DKH + DVH], F32)
    cc0_out = nc.dram_tensor("cc0_out", [NCORE * NH * DKH, DKH + DVH], F32)
    cc1_in = nc.dram_tensor("cc1_in", [128, 8 * 64 + 8], F32)
    cc1_out = nc.dram_tensor("cc1_out", [NCORE * 128, 8 * 64 + 8], F32)

    dbg = set(debug)

    b = B(nc, cfg)
    with b.ctx:
        K = Kern(nc, b, cfg, dr, y, oloc_d, g_d, (cc0_in, cc0_out, cc1_in, cc1_out), dbg)
        K.run()
        with nc.Block() as block:
            b.finalize(block)
    return nc


class Kern:
    def __init__(self, nc, b, cfg, dr, y, oloc_d, g_d, ccs, dbg):
        self.nc, self.b, self.cfg, self.dr = nc, b, cfg, dr
        self.y, self.oloc_d, self.g_d, self.ccs, self.dbg = y, oloc_d, g_d, ccs, dbg
        tpc = cfg.tpc
        self.xres = b.sb("xres", [128, DCH * tpc], F32)
        self.arsl = b.sb("arsl", [128, tpc], F32)
        self.ar16 = b.sb("ar16", [128, tpc], F32)
        self.ARN = b.sb("ARN", [128, 45312], BF16)
        A = self.ARN
        self.hb = A[:, 0:4096]
        self.qb = A[:, 4096:8192]
        self.kb = A[:, 8192:12288]
        self.vb = A[:, 12288:20480]
        self.wp = A[:, 20480:28672]
        self.obc = A[:, 28672:30720]
        self.gst = A[:, 30720:32768]
        self.sel16 = A[0:16, 32768:33792]
        self.oneh16 = A[0:16, 33792:35840]
        self.PT = A[:, 35840:36608]
        self.Pti = A[:, 36608:36736]
        self.RHS = A[:, 36736:36928]
        self.Ybuf = A[:, 36928:37312]
        self.Mt_bf = A[:, 37312:37440]
        self.Mti_bf = A[:, 37440:37568]
        self.At_bf = A[:, 37568:37696]
        self.Wf2 = A[:, 37696:37824]
        self.Qt2 = A[:, 37824:37952]
        self.Kb2 = A[:, 37952:38080]
        self.Z2 = A[:, 38080:38208]
        self.Kbt = A[:, 38208:38272]
        self.AchT = A[:, 38272:38336]
        self.bexp = A[0:16, 38336:38464]
        self.eg_b = A[0:16, 38464:38592]
        self.edec = A[0:16, 38592:38720]
        self.gbf = A[0:16, 38720:38848]
        self.bbf = A[0:16, 38848:38976]
        self.Ar_bf = A[:, 38976:39488]
        self.h3fm = A[:, 39488:39520]
        self.xnb = A[:, 39552:43648]
        self.AtTb = A[:, 4096:4608]
        self.wkv = A[:, 20480:21504]
        self.onescb = A[:, 21504:21505]
        self.kvzb = A[:, 44544:45064]
        self.zbd = A[:, 45064:45080]
        self.ARNF = b.sb("ARNF", [128, 4096], F32)
        AF_ = self.ARNF
        self.xtm = [AF_[:, 0:1024], AF_[:, 1024:2048]]
        self.xp_t = self.xtm[0][0:4, :]
        self.sqb = AF_[:, 2048:2560]
        self.S_f = AF_[:, 0:1024]
        self.Ar_f = AF_[:, 1024:1536]
        self.Atmp = AF_[:, 1536:2048]
        self.AtT = self.Atmp
        self.AID = AF_[:, 2560:3072]
        self.Btmp = AF_[:, 3072:4096]
        self.kvz = AF_[:, 2048:2568]
        self.kvtmp = AF_[:, 3072:3592]
        wfc = [0]

        def vf(cols, parts=128, dtype=F32):
            a = wfc[0]
            wfc[0] += cols
            return self._WSF[0:parts, a:a + cols]
        self._WSF = b.sb("WSF", [128, 5248], F32)
        self.identf = vf(128)
        self.mask_s = vf(128)
        self.mask_i1 = vf(128)
        self.onesc = vf(1)
        self.onesr = vf(128, 1)
        self.onesall = vf(128)
        self.statall = vf(2 * cfg.ntile)
        self.cstt = vf(16)
        self._cst = {}
        self.gtm = vf(16)
        self.ngtm = vf(16)
        self.gfm = vf(128, 16)
        self.egfm = vf(128, 16)
        self.btm = vf(16)
        self.egCc = vf(1)
        self.Dsb = vf(128)
        self.tmpA = vf(128)
        self.tmpB = vf(128)
        self.o1sb = vf(128)
        self.mksel = vf(128, 2)
        self.nksel = vf(128, 2)
        self.mubc = vf(NCORE)
        self.nubc = vf(NCORE)
        self.otile = vf(128)
        self.onrow = vf(128, 1)
        self.cs16 = vf(1, 16)
        self.ebb = vf(16)
        self.ebrow = vf(16, 1)
        self.mx8 = vf(8)
        self.rtm = vf(32)
        self.rw1 = vf(16)
        self.nw = vf(DCH)
        self.pnw = vf(DCH)
        self.convw = vf(128)
        self.dtb = vf(1, 16)
        self.negal = vf(1, 16)
        self.onw = vf(1)
        self.cmk = vf(NCORE, 2)
        self.ones2 = vf(2)
        self.ones2sc = vf(2)
        self.ones2T = vf(128, 2)
        self.ones2Tsc = vf(128, 2)
        self.cext = vf(516)
        self.ctails = vf(32 * 3)
        self.sqtmp = vf(512)
        self.l2row = vf(512, 2)
        self.xp_s = vf(4, 4)
        self.rt_sb = vf(DCH * 128)
        assert wfc[0] <= 5248, wfc[0]
        self._WSB = b.sb("WSB", [128, 1280], BF16)
        self.ident = self._WSB[:, 0:128]
        self.S_bf = self._WSB[:, 128:1152]
        self.wq_sb = self.ARN[:, 43008:43520]
        self.Kbb = self.ARN[:, 43648:43776]
        self.pb = [b.ps(f"pb{i}", [128, 512]) for i in range(6)]
        self.pbh = [b.ps(f"pbh{i}", [128, 1024], BF16) for i in range(2)]
        self.pb = self.pb + self.pbh
        self.s_t = b.sem("s_t")
        self.s_v = b.sem("s_v")
        self.s_sc = b.sem("s_sc")
        self.s_g = b.sem("s_g")
        self.s_cc = b.sem("s_cc")
        self.bsem = [b.sem(f"bk{i}") for i in range(8)]
        self.wq = WQ(b, "wq", nrot=6)
        self.R_LNM, self.R_LNRS, self.R_RMS = 0, 32, 64

    def arn(self, a, w):
        return self.ARN[:, a:a + w]

    def cst(self, val, parts=128):
        if val not in self._cst:
            k = len(self._cst)
            assert k < 16
            self._cst[val] = k
            self.b.E("gpsimd", lambda g, k=k, v=float(val):
                     g.memset(self.cstt[:, k:k + 1], v))
        return self.cstt[0:parts, self._cst[val]:self._cst[val] + 1]

    def bwait(self, i, eng="tensor"):
        self.b.W(eng, self.bsem[i], self.bsem[i].n)

    def MMg(self, bank_reg, lhsT, rhs, start=True, stop=True, tp=None, waits=()):
        b = self.b
        for sem, n in waits:
            b.W("tensor", sem, n)
        b.I("tensor", lambda t, o=bank_reg, l=lhsT, r=rhs, st=start, sp=stop, tp=tp:
            t.matmul(o, l, r, start=st, stop=sp, tile_position=tp,
                     skip_group_check=True), self.s_t)
        return self.s_t.n

    def TRg(self, bank_reg, in_, ident, tp=None, waits=()):
        b = self.b
        for sem, n in waits:
            b.W("tensor", sem, n)
        b.I("tensor", lambda t, o=bank_reg, i=in_, d=ident, tp=tp:
            t.transpose(o, i, d, tile_position=tp), self.s_t)
        return self.s_t.n

    def xn_block(self, tb, dst=None, also_f32=None):
        b, cfg = self.b, self.cfg
        tpc = cfg.tpc
        c0 = 512 * tb
        dst = self.xnb if dst is None else dst
        b.W("tensor", self.s_v, self.V_SLIVERS)
        self.bwait(3)
        self.MMg(self.pb[3][:, 0:512], self.onesall[0:1, 0:128],
                 self.arsl[0:1, c0:c0 + 512])
        self.bwait(4)
        self.MMg(self.pb[4][:, 0:512], self.onesall[32:33, 0:128],
                 self.arsl[32:33, c0:c0 + 512])
        tn = self.s_t.n
        b.W("vector", self.s_t, tn)
        for dc in range(DCH):
            xr = self.xres[:, dc * tpc + c0:dc * tpc + c0 + 512]
            d_ap = dst[:, dc * 512:(dc + 1) * 512]
            b.D("vector")
            b.E("vector", lambda v, x=xr: v.tensor_add(
                self.sqtmp[:, :], x, self.pb[3][:, 0:512]))
            b.D("vector")
            last = dc == DCH - 1
            b.I("vector", lambda v, d=d_ap: v.tensor_tensor(
                d, self.sqtmp[:, :], self.pb[4][:, 0:512], ALU.mult),
                self.bsem[4] if last else self.s_v)
            if last:
                b.I("vector", lambda v: v.drain(), self.bsem[3])
        b.F("vector", self.s_v)
        return self.s_v.n

    def h_block(self, tb):
        b, cfg = self.b, self.cfg
        c0 = 512 * tb
        b.W("tensor", self.s_v, self.s_v.n)
        self.bwait(4)
        self.MMg(self.pb[4][:, 0:512], self.onesall[64:65, 0:128],
                 self.arsl[64:65, c0:c0 + 512])
        tn = self.s_t.n
        b.W("vector", self.s_t, tn)
        for dc in range(DCH):
            xn_ap = self.xnb[:, dc * 512:(dc + 1) * 512]
            h_ap = self.hb[:, dc * 512:(dc + 1) * 512]
            b.D("vector")
            b.E("vector", lambda v, x=xn_ap, dc=dc: v.tensor_scalar_mul(
                self.sqtmp[:, :], x, self.nw[:, dc:dc + 1]))
            b.D("vector")
            last = dc == DCH - 1
            b.I("vector", lambda v, h=h_ap: v.tensor_tensor(
                h, self.sqtmp[:, :], self.pb[4][:, 0:512], ALU.mult),
                self.bsem[4] if last else self.s_v)
        b.F("vector", self.s_v)
        return self.s_v.n

    def run(self):
        tpc = self.cfg.tpc
        self.ph0_consts()
        self.ph1_load_ln()
        self.ph2_small()
        self.ph2_router0()
        self.ph3_l0_sweep()
        self.dump("qb", self.qb, 4096, BF16)
        self.dump("kb", self.kb, 4096, BF16)
        self.dump("vb", self.vb, 8192, BF16)
        self.dump("ar16", self.ar16[:, :], tpc, F32)
        self.dump("S_f", self.S_f[:, :], 8 * 128, F32)
        self.dump("Ar_f", self.Ar_f[:, :], 8 * 64, F32)
        self.ph4_ag_compose()
        self.dump("Sin", self.S_f[:, :], 8 * 128, F32)
        self.ph5_l0_blocks()
        self.dump("xres1", self.xres[:, :], DCH * tpc, F32)
        self.ph5c_moe0()
        self.dump("xres2", self.xres[:, :], DCH * tpc, F32)
        self.dump("w0rt", self.ar16[:, :], tpc, F32)
        self.ph7_l1_pre()
        self.dump("arsl2", self.arsl[:, :], tpc, F32)
        self.dump("xn2", self.ARN[:, 0:16384], 16384, BF16)
        self.dump("kvz", self.kvz[:, :], 520, F32)
        self.ph8_ag2()
        self.dump("kvzc", self.kvz[:, :], 520, F32)
        self.ph9_l1_blocks()
        self.dump("w1rt", self.ar16[:, :], tpc, F32)
        self.ph_out()

    def ph0_consts(self):
        b = self.b

        def gp(f):
            b.E("gpsimd", f)
        gp(lambda g: g.memset(self.ident[:, :], 1.0))
        gp(lambda g: g.memset(self.identf[:, :], 1.0))
        gp(lambda g: g.memset(self.mask_s[:, :], 1.0))
        gp(lambda g: g.memset(self.mask_i1[:, :], 1.0))
        gp(lambda g: g.memset(self.onesc[:, :], 1.0))
        gp(lambda g: g.memset(self.onesr[:, :], 1.0))
        gp(lambda g: g.memset(self.onesall[:, :], 1.0))
        gp(lambda g: g.memset(self.sel16[:, :], 1.0))
        gp(lambda g: g.memset(self.oneh16[:, :], 1.0))
        gp(lambda g: g.memset(self.arsl[:, :], 0.0))
        gp(lambda g: g.memset(self.ar16[:, :], 0.0))
        gp(lambda g: g.memset(self.S_bf[:, :], 0.0))
        gp(lambda g: g.memset(self.mksel[:, :], 0.0))
        gp(lambda g: g.memset(self.nksel[:, :], 1.0))
        gp(lambda g: g.memset(self.ones2[:, :], 0.0))
        gp(lambda g: g.memset(self.ones2sc[:, :], 0.0))
        b.D("gpsimd")
        gp(lambda g: g.affine_select(out=self.ident[:, :], in_=self.ident[:, :],
                                     compare_op=ALU.is_equal, fill=0.0, base=0,
                                     pattern=[[-1, 128]], channel_multiplier=1))
        gp(lambda g: g.affine_select(out=self.identf[:, :], in_=self.identf[:, :],
                                     compare_op=ALU.is_equal, fill=0.0, base=0,
                                     pattern=[[-1, 128]], channel_multiplier=1))
        gp(lambda g: g.affine_select(out=self.mask_i1[:, :], in_=self.mask_i1[:, :],
                                     compare_op=ALU.is_ge, fill=0.0, base=0,
                                     pattern=[[1, 128]], channel_multiplier=-1))
        gp(lambda g: g.affine_select(out=self.mask_s[:, :], in_=self.mask_s[:, :],
                                     compare_op=ALU.is_ge, fill=0.0, base=-1,
                                     pattern=[[1, 128]], channel_multiplier=-1))
        gp(lambda g: g.affine_select(
            out=self.sel16[:, :].rearrange("h (m r) -> h m r", r=64),
            in_=self.sel16[:, :].rearrange("h (m r) -> h m r", r=64),
            compare_op=ALU.is_equal, fill=0.0, base=0,
            pattern=[[-1, 16], [0, 64]], channel_multiplier=1))
        gp(lambda g: g.affine_select(
            out=self.oneh16[:, :].rearrange("h (s t) -> h s t", t=128),
            in_=self.oneh16[:, :].rearrange("h (s t) -> h s t", t=128),
            compare_op=ALU.is_equal, fill=0.0, base=0,
            pattern=[[-1, 16], [0, 128]], channel_multiplier=1))
        gp(lambda g: g.affine_select(
            out=self.nksel[:, :], in_=self.nksel[:, :], compare_op=ALU.is_ge,
            fill=0.0, base=-1, pattern=[[0, 128]], channel_multiplier=1))
        b.D("gpsimd")
        gp(lambda g: g.memset(self.mksel[0:1, :], 1.0))
        gp(lambda g: g.memset(self.ones2[0:64, 0:1], 1.0))
        gp(lambda g: g.memset(self.ones2[64:128, 1:2], 1.0))
        gp(lambda g: g.memset(self.ones2sc[0:64, 0:1], SCALE_DK))
        gp(lambda g: g.memset(self.ones2sc[64:128, 1:2], SCALE_DK))
        for v in (EPS_LN, -EPS_LN, 1.0 + EPS_RMS, EPS_L2, EPS_DEN, 1.0 / D,
                  1.0 / DVH, EPS_RMS):
            self.cst(v)
        b.F("gpsimd", self.s_g)
        self.G_CONST = self.s_g.n

    def ph1_load_ln(self):
        b, cfg = self.b, self.cfg
        tpc, ntile = cfg.tpc, cfg.ntile
        x_d = self.dr["x"]
        s_ldb = [b.sem("p1l0"), b.sem("p1l1")]
        s_tr = b.sem("p1tr")
        buffree = [0, 0]
        for i in range(ntile):
            j = i % 2
            buf = self.xtm[j]
            if i >= 2:
                b.W("sync", s_tr, buffree[j])
            b.I("sync", lambda e, buf=buf, src=x_d[128 * i:128 * (i + 1), :]:
                e.dma_start(out=buf[:, :], in_=src), s_ldb[j], k=16)
            ldn = s_ldb[j].n
            m_ap = self.statall[:, 2 * i:2 * i + 1]
            r_ap = self.statall[:, 2 * i + 1:2 * i + 2]
            b.W("vector", s_ldb[j], ldn)
            b.E("vector", lambda v, buf=buf, m=m_ap: v.reduce_sum(m, buf[:, :], axis=AX.X))
            b.D("vector")
            b.E("vector", lambda v, m=m_ap: v.tensor_scalar_mul(m, m, -1.0 / D))
            b.F("vector", self.s_v)
            mrdy = self.s_v.n
            b.W("scalar", self.s_g, self.G_CONST)
            b.W("scalar", s_ldb[j], ldn)
            b.W("scalar", self.s_v, mrdy)
            b.I("scalar", lambda s, buf=buf, m=m_ap: s.activation(
                self.sqb[:, :], buf[:, 0:512], AF.Square, bias=m), self.s_sc)
            sqrdy = self.s_sc.n
            b.W("vector", self.s_sc, sqrdy)
            b.E("vector", lambda v, r=r_ap: v.reduce_sum(r, self.sqb[:, :], axis=AX.X))
            b.F("vector", self.s_v)
            b.W("scalar", self.s_v, self.s_v.n)
            b.I("scalar", lambda s, buf=buf, m=m_ap: s.activation(
                self.sqb[:, :], buf[:, 512:1024], AF.Square, bias=m), self.s_sc)
            b.W("vector", self.s_sc, self.s_sc.n)
            b.E("vector", lambda v: v.reduce_sum(self.gtm[:, 0:1], self.sqb[:, :],
                                                 axis=AX.X))
            b.D("vector")
            b.E("vector", lambda v, r=r_ap: v.tensor_add(r, r, self.gtm[:, 0:1]))
            b.F("vector", self.s_v)
            vrdy = self.s_v.n
            b.W("scalar", self.s_v, vrdy)
            b.I("scalar", lambda s, r=r_ap: s.activation(
                r, r, AF.Sqrt, scale=self.cst(1.0 / D), bias=self.cst(EPS_LN)),
                self.s_sc)
            srdy = self.s_sc.n
            b.W("vector", self.s_sc, srdy)
            b.E("vector", lambda v, r=r_ap: v.reciprocal(r, r))
            b.F("vector", self.s_v)
            xnrdy = self.s_v.n
            b.W("tensor", self.s_g, self.G_CONST)
            b.W("tensor", s_ldb[j], ldn)
            b.W("tensor", self.s_v, xnrdy)
            for dc in range(DCH):
                bank = dc % 2
                px = self.pb[bank][:, 0:128]
                xr_ap = self.xres[:, dc * tpc + 128 * i: dc * tpc + 128 * (i + 1)]
                src_x = buf[:, 128 * dc:128 * (dc + 1)]
                self.bwait(bank)
                b.I("tensor", lambda t, px=px, s_=src_x: t.transpose(
                    px, s_, self.identf[:, :]), s_tr)
                b.W("vector", s_tr, s_tr.n)
                b.I("vector", lambda v, d=xr_ap, p=px: v.tensor_copy(d, p),
                    self.bsem[bank])
            buffree[j] = s_tr.n
        b.W("tensor", self.s_v, self.s_v.n)
        for i in range(ntile):
            pm = self.pb[4][0:1, 0:128]
            pr = self.pb[5][0:1, 0:128]
            d1 = self.arsl[0:1, 128 * i:128 * (i + 1)]
            d2 = self.arsl[32:33, 128 * i:128 * (i + 1)]
            self.bwait(4)
            self.bwait(5)
            b.I("tensor", lambda t, i=i, pm=pm: t.transpose(
                pm, self.statall[:, 2 * i:2 * i + 1], self.identf[:, :]), s_tr)
            b.I("tensor", lambda t, i=i, pr=pr: t.transpose(
                pr, self.statall[:, 2 * i + 1:2 * i + 2], self.identf[:, :]), s_tr)
            b.W("vector", s_tr, s_tr.n)
            b.I("vector", lambda v, d=d1, p=pm: v.tensor_copy(d, p), self.bsem[4])
            b.I("vector", lambda v, d=d2, p=pr: v.tensor_copy(d, p), self.bsem[5])
        rms_ap = self.arsl[64:65, 0:tpc]
        rs_ap = self.arsl[32:33, 0:tpc]
        b.D("vector")
        b.E("vector", lambda v: v.tensor_mul(rms_ap, rs_ap, rs_ap))
        b.F("vector", self.s_v)
        b.W("scalar", self.s_v, self.s_v.n)
        b.I("scalar", lambda s: s.activation(
            rms_ap, rms_ap, AF.Sqrt, scale=self.cst(-EPS_LN, 1),
            bias=self.cst(1.0 + EPS_RMS, 1)), self.s_sc)
        b.W("vector", self.s_sc, self.s_sc.n)
        b.E("vector", lambda v: v.reciprocal(rms_ap, rms_ap))
        b.F("vector", self.s_v)
        self.V_SLIVERS = self.s_v.n
        self.P1TR = (s_tr, s_tr.n)

    def ph2_small(self):
        b = self.b
        s_sl = b.sem("sml")
        b.W("sync", self.P1TR[0], self.P1TR[1])
        for dst, src in ((self.nw, self.dr["w_norm"]), (self.pnw, self.dr["w_pnw"]),
                         (self.convw, self.dr["w_conv"]), (self.dtb, self.dr["w_dtb"]),
                         (self.negal, self.dr["w_Alog"]), (self.onw, self.dr["w_onw"]),
                         (self.cmk, self.dr["cmask"]), (self.xp_t, self.dr["xp"]),
                         (self.rt_sb[:, 0:DCH * 128], self.dr["w_rt0"]),
                         (self.ebrow[:, :], self.dr["w_eb"])):
            b.W("sync", s_sl, s_sl.n)
            d_ap = dst[:, :] if hasattr(dst, "shape") and not hasattr(dst, "tensor") else dst
            b.I("sync", lambda e, d=d_ap, s=src: e.dma_start(out=d, in_=s[:, :]),
                s_sl, k=16)
        self.SML = s_sl.n
        b.W("scalar", s_sl, s_sl.n)
        b.I("scalar", lambda s: s.activation(self.negal[:, :], self.negal[:, :], AF.Exp),
            self.s_sc)
        b.W("vector", self.s_sc, self.s_sc.n)
        b.E("vector", lambda v: v.tensor_scalar_mul(self.negal[:, :], self.negal[:, :], -1.0))
        b.F("vector", self.s_v)
        self.V_NEGAL = self.s_v.n
        b.W("tensor", self.s_g, self.G_CONST)
        self.bwait(4)
        b.I("tensor", lambda t: t.transpose(self.pb[4][0:2, 0:128], self.ones2[:, :],
                                            self.identf[:, :]), self.s_t)
        b.W("vector", self.s_t, self.s_t.n)
        b.I("vector", lambda v: v.tensor_copy(self.ones2T[:, :], self.pb[4][0:2, 0:128]),
            self.bsem[4])
        self.bwait(5)
        b.I("tensor", lambda t: t.transpose(self.pb[5][0:2, 0:128], self.ones2sc[:, :],
                                            self.identf[:, :]), self.s_t)
        b.W("vector", self.s_t, self.s_t.n)
        b.I("vector", lambda v: v.tensor_copy(self.ones2Tsc[:, :], self.pb[5][0:2, 0:128]),
            self.bsem[5])
        st = self.xp_s
        b.W("vector", s_sl, s_sl.n)
        b.E("vector", lambda v: v.reduce_sum(st[0:4, 0:1], self.xp_t[0:4, :], axis=AX.X))
        b.D("vector")
        b.E("vector", lambda v: v.tensor_scalar_mul(st[0:4, 0:1], st[0:4, 0:1], -1.0 / D))
        b.F("vector", self.s_v)
        b.W("scalar", self.s_v, self.s_v.n)
        b.I("scalar", lambda s: s.activation(self.sqtmp[0:4, 0:512], self.xp_t[0:4, 0:512],
                                             AF.Square, bias=st[0:4, 0:1]), self.s_sc)
        b.D("scalar")
        b.I("scalar", lambda s: s.activation(self.cext[0:4, 0:512], self.xp_t[0:4, 512:1024],
                                             AF.Square, bias=st[0:4, 0:1]), self.s_sc)
        b.W("vector", self.s_sc, self.s_sc.n)
        b.E("vector", lambda v: v.reduce_sum(st[0:4, 1:2], self.sqtmp[0:4, 0:512], axis=AX.X))
        b.D("vector")
        b.E("vector", lambda v: v.reduce_sum(st[0:4, 2:3], self.cext[0:4, 0:512], axis=AX.X))
        b.D("vector")
        b.E("vector", lambda v: v.tensor_add(st[0:4, 1:2], st[0:4, 1:2], st[0:4, 2:3]))
        b.F("vector", self.s_v)
        b.W("scalar", self.s_v, self.s_v.n)
        b.I("scalar", lambda s: s.activation(st[0:4, 1:2], st[0:4, 1:2], AF.Sqrt,
                                             scale=self.cst(1.0 / D, 4),
                                             bias=self.cst(EPS_LN, 4)), self.s_sc)
        b.W("vector", self.s_sc, self.s_sc.n)
        b.E("vector", lambda v: v.reciprocal(st[0:4, 1:2], st[0:4, 1:2]))
        b.D("vector")
        b.E("vector", lambda v: v.tensor_mul(st[0:4, 2:3], st[0:4, 1:2], st[0:4, 1:2]))
        b.F("vector", self.s_v)
        b.W("scalar", self.s_v, self.s_v.n)
        b.I("scalar", lambda s: s.activation(st[0:4, 2:3], st[0:4, 2:3], AF.Sqrt,
                                             scale=self.cst(-EPS_LN, 4),
                                             bias=self.cst(1.0 + EPS_RMS, 4)), self.s_sc)
        b.W("vector", self.s_sc, self.s_sc.n)
        b.E("vector", lambda v: v.reciprocal(st[0:4, 2:3], st[0:4, 2:3]))
        b.D("vector")
        b.E("vector", lambda v: v.tensor_mul(st[0:4, 3:4], st[0:4, 1:2], st[0:4, 2:3]))
        b.D("vector")
        b.E("vector", lambda v: v.tensor_scalar(self.xp_t[0:4, :], self.xp_t[0:4, :],
                                                st[0:4, 0:1], st[0:4, 3:4],
                                                ALU.add, ALU.mult))
        b.F("vector", self.s_v)
        hx = self.s_v.n
        b.W("tensor", self.s_v, hx)
        for dc in range(DCH):
            self.bwait(5)
            b.I("tensor", lambda t, dc=dc: t.transpose(
                self.pb[5][0:128, 0:4], self.xp_t[0:4, 128 * dc:128 * (dc + 1)],
                self.identf[0:4, 0:4]), self.s_t)
            b.W("vector", self.s_t, self.s_t.n)
            b.I("vector", lambda v, dc=dc: v.tensor_scalar_mul(
                self.h3fm[:, 4 * dc:4 * (dc + 1)], self.pb[5][0:128, 0:4],
                self.nw[:, dc:dc + 1]), self.bsem[5])
        b.F("vector", self.s_v)
        self.V_H3 = self.s_v.n

    def ph2_router0(self):
        b, cfg = self.b, self.cfg
        tpc = cfg.tpc
        wrt = self.rt_sb[:, 0:DCH * 128]
        b.W("tensor", self.SML and self.s_v, self.V_SLIVERS)
        self.bwait(2)
        for dc in range(DCH):
            self.MMg(self.pb[2][0:8, 0:1], wrt[:, 128 * dc:128 * dc + 8],
                     self.onesc[:, :], start=(dc == 0), stop=(dc == DCH - 1),
                     waits=(((self.s_g, self.G_CONST),) if dc == 0 else ()))
        b.W("vector", self.s_t, self.s_t.n)
        b.I("vector", lambda v: v.tensor_copy(self.cs16[0:8, 0:1], self.pb[2][0:8, 0:1]),
            self.bsem[2])
        vcs = self.s_v.n + 0
        b.F("vector", self.s_v)
        vcs = self.s_v.n
        for tb in range(cfg.nblk):
            c0 = 512 * tb
            self.bwait(2)
            for dc in range(DCH):
                self.MMg(self.pb[2][0:8, 0:512], wrt[:, 128 * dc:128 * dc + 8],
                         self.xres[:, dc * tpc + c0: dc * tpc + c0 + 512],
                         start=(dc == 0), stop=(dc == DCH - 1))
            self.bwait(3)
            self.MMg(self.pb[3][0:8, 0:512], self.onesall[0:1, 0:8],
                     self.arsl[0:1, c0:c0 + 512])
            self.bwait(4)
            self.MMg(self.pb[4][0:8, 0:512], self.onesall[32:33, 0:8],
                     self.arsl[32:33, c0:c0 + 512])
            tn = self.s_t.n
            b.W("vector", self.s_t, tn)
            b.W("vector", self.s_v, vcs)
            b.E("vector", lambda v: v.tensor_copy(self.sqtmp[0:8, 0:512],
                                                  self.pb[2][0:8, 0:512]))
            b.I("vector", lambda v: v.drain(), self.bsem[2])
            b.E("vector", lambda v: v.scalar_tensor_tensor(
                self.sqtmp[0:8, 0:512], self.pb[3][0:8, 0:512], self.cs16[0:8, 0:1],
                self.sqtmp[0:8, 0:512], ALU.mult, ALU.add))
            b.I("vector", lambda v: v.drain(), self.bsem[3])
            g_ap = self.ar16[64:72, c0:c0 + 512]
            b.I("vector", lambda v, g=g_ap: v.tensor_tensor(
                g, self.sqtmp[0:8, 0:512], self.pb[4][0:8, 0:512], ALU.mult),
                self.bsem[4])
            b.F("vector", self.s_v)
        self.V_GATE0 = self.s_v.n
        b.F("tensor", self.s_t)
        self.T_RT0 = self.s_t.n

    def _ar_ident(self, t):
        b = self.b
        b.E("gpsimd", lambda g, t=t: g.memset(t, 1.0))
        b.D("gpsimd")
        for r0 in (0, 64):
            b.E("gpsimd", lambda g, t=t, r0=r0: g.affine_select(
                out=t[r0:r0 + 64, :].rearrange("p (j c) -> p j c", c=64),
                in_=t[r0:r0 + 64, :].rearrange("p (j c) -> p j c", c=64),
                compare_op=ALU.is_equal, fill=0.0, base=0,
                pattern=[[0, 8], [-1, 64]], channel_multiplier=1))

    def ph3_l0_sweep(self):
        b, cfg = self.b, self.cfg
        b.W("gpsimd", self.s_v, self.V_H3)
        b.E("gpsimd", lambda g: g.memset(self.S_f[:, :], 0.0))
        self._ar_ident(self.Ar_f[:, :])
        b.F("gpsimd", self.s_g)
        self.G_S0 = self.s_g.n
        b.W("vector", self.s_g, self.G_S0)
        b.E("vector", lambda v: v.tensor_copy(self.Ar_bf[:, :], self.Ar_f[:, :]))
        b.F("vector", self.s_v)
        b.W("tensor", self.s_g, self.G_S0)
        self.s_kv = b.sem("s_kv")
        self.s_wp = b.sem("s_wp")
        self.s_ba = b.sem("s_ba")
        self.socr = [b.sem("socr0"), b.sem("socr1")]
        self.sgdr = [b.sem("sgdr0"), b.sem("sgdr1")]
        self.soc = {}
        self.sgd = {}
        self.slot_free = [None] * 8
        self.slot_i = 0
        for tb in range(cfg.nblk):
            self.xn_block(tb)
            self.h_block(tb)
            self.l0_block_proj(tb)
            b.F("vector", self.s_kv)
            b.F("scalar", self.s_kv)
            KVN = self.s_kv.n
            BAN = self.s_ba.n
            for c in range(4):
                self.l0_delta_chunk(tb, c, KVN, BAN)

    def wslot(self, src_ap):
        j = self.slot_i % 8
        self.slot_i += 1
        dst = self.wp[:, j * 1024:(j + 1) * 1024]
        tok = self.wq.load(dst, src_ap, pre_wait=self.slot_free[j])
        return j, tok

    def wpc(self, j, dc):
        return self.wp[:, j * 1024 + dc * 128:j * 1024 + dc * 128 + 128]

    def l0_block_proj(self, tb):
        b, cfg = self.b, self.cfg
        s_kv, s_wp, s_ba = self.s_kv, self.s_wp, self.s_ba
        HBv = self.s_v.n
        projs = [("w_q", DCH, 0, self.qb, 1), ("w_k", DCH, 8, self.kb, 2),
                 ("w_v", VCH, 16, self.vb, 0)]
        pk = 0
        for wname, noc, po0, store, do_l2 in projs:
            img = self.dr[wname]
            for oc in range(noc):
                po = po0 + oc
                j, tok = self.wslot(img[:, oc * DCH * 128:(oc + 1) * DCH * 128])
                bank = pk % 2
                pk += 1
                b.W("tensor", tok[0], tok[1])
                b.W("tensor", self.s_v, HBv)
                self.bwait(bank)
                for dc in range(DCH):
                    self.MMg(self.pb[bank][:, 0:512], self.wpc(j, dc),
                             self.hb[:, dc * 512:(dc + 1) * 512],
                             start=(dc == 0), stop=(dc == DCH - 1))
                if tb == 0:
                    self.bwait(5)
                    b.W("tensor", self.s_v, self.V_H3)
                    for dc in range(DCH):
                        self.MMg(self.pb[5][:, 0:4], self.wpc(j, dc),
                                 self.h3fm[:, 4 * dc:4 * (dc + 1)],
                                 start=(dc == 0), stop=(dc == DCH - 1))
                    b.W("vector", self.s_t, self.s_t.n)
                    b.I("vector", lambda v, po=po: v.tensor_copy(
                        self.ctails[:, 3 * po:3 * po + 3], self.pb[5][:, 0:3]),
                        self.bsem[5])
                b.F("tensor", s_wp)
                self.slot_free[j] = (s_wp, s_wp.n)
                tmm = self.s_t.n
                b.W("vector", self.s_t, tmm)
                b.D("vector")
                b.E("vector", lambda v, po=po: v.tensor_copy(
                    self.cext[:, 0:3], self.ctails[:, 3 * po:3 * po + 3]))
                b.D("vector")
                b.I("vector", lambda v, bank=bank: v.tensor_copy(
                    self.cext[:, 3:515], self.pb[bank][:, 0:512]), self.bsem[bank])
                b.D("vector")
                b.E("vector", lambda v, po=po: v.tensor_copy(
                    self.ctails[:, 3 * po:3 * po + 3], self.cext[:, 512:515]))
                b.D("vector")
                b.W("vector", self.s_sc, self.s_sc.n)
                b.E("vector", lambda v, po=po: v.tensor_scalar_mul(
                    self.sqtmp[:, :], self.cext[:, 0:512], self.convw[:, 4 * po:4 * po + 1]))
                for i in range(1, 4):
                    b.D("vector")
                    b.E("vector", lambda v, po=po, i=i: v.scalar_tensor_tensor(
                        self.sqtmp[:, :], self.cext[:, i:i + 512],
                        self.convw[:, 4 * po + i:4 * po + i + 1],
                        self.sqtmp[:, :], ALU.mult, ALU.add))
                b.F("vector", self.s_v)
                cvr = self.s_v.n
                dst = store[:, oc * 512:(oc + 1) * 512]
                b.W("scalar", self.s_v, cvr)
                b.W("scalar", s_kv, s_kv.n)
                if do_l2:
                    b.I("scalar", lambda s: s.activation(
                        self.cext[:, 0:512], self.sqtmp[:, :], AF.Silu), self.s_sc)
                    b.D("scalar")
                    b.I("scalar", lambda s: s.activation(
                        self.sqtmp[:, :], self.cext[:, 0:512], AF.Square), self.s_sc)
                    sq2 = self.s_sc.n
                    self.bwait(4)
                    self.MMg(self.pb[4][0:2, 0:512], self.ones2[:, :], self.sqtmp[:, :],
                             waits=((self.s_sc, sq2),))
                    tl2 = self.s_t.n
                    b.W("scalar", self.s_t, tl2)
                    b.I("scalar", lambda s: s.activation(
                        self.l2row[:, :], self.pb[4][0:2, 0:512], AF.Sqrt,
                        bias=self.cst(EPS_L2, 2)), self.s_sc)
                    b.I("scalar", lambda s: s.drain(), self.bsem[4])
                    b.W("vector", self.s_sc, self.s_sc.n)
                    b.E("vector", lambda v: v.reciprocal(self.l2row[:, :], self.l2row[:, :]))
                    b.F("vector", self.s_v)
                    rcn = self.s_v.n
                    o2t = self.ones2Tsc if do_l2 == 1 else self.ones2T
                    self.bwait(5)
                    self.MMg(self.pb[5][:, 0:512], o2t[:, :], self.l2row[:, :],
                             waits=((self.s_v, rcn),))
                    tbc = self.s_t.n
                    b.W("vector", self.s_t, tbc)
                    b.I("vector", lambda v, dst=dst: v.tensor_tensor(
                        dst, self.cext[:, 0:512], self.pb[5][:, 0:512], ALU.mult),
                        self.bsem[5])
                    b.I("vector", lambda v: v.drain(), s_kv)
                else:
                    b.I("scalar", lambda s, dst=dst: s.activation(
                        dst, self.sqtmp[:, :], AF.Silu), self.s_sc)
                    b.I("scalar", lambda s: s.drain(), s_kv)
        img = self.dr["w_ba"]
        j, tok = self.wslot(img[:, :])
        b.W("tensor", tok[0], tok[1])
        b.W("tensor", self.s_v, HBv)
        self.bwait(2)
        self.bwait(3)
        for dc in range(DCH):
            self.MMg(self.pb[2][0:16, 0:512], self.wpc(j, dc)[:, 0:16],
                     self.hb[:, dc * 512:(dc + 1) * 512],
                     start=(dc == 0), stop=(dc == DCH - 1))
        for dc in range(DCH):
            self.MMg(self.pb[3][0:16, 0:512], self.wpc(j, dc)[:, 16:32],
                     self.hb[:, dc * 512:(dc + 1) * 512],
                     start=(dc == 0), stop=(dc == DCH - 1))
        b.F("tensor", s_wp)
        self.slot_free[j] = (s_wp, s_wp.n)
        tba = self.s_t.n
        c0 = 512 * tb
        beta_ap = self.ar16[0:16, c0:c0 + 512]
        g_ap = self.ar16[32:48, c0:c0 + 512]
        b.W("scalar", self.s_t, tba)
        b.W("scalar", self.s_v, self.V_NEGAL)
        b.I("scalar", lambda s, d=beta_ap: s.activation(
            d, self.pb[2][0:16, 0:512], AF.Sigmoid), self.s_sc)
        b.I("scalar", lambda s: s.drain(), self.bsem[2])
        b.W("scalar", self.s_v, self.s_v.n)
        b.I("scalar", lambda s: s.activation(
            self.sqtmp[0:16, 0:512], self.pb[3][0:16, 0:512], AF.Abs,
            bias=self.dtb[0:16, 0:1]), self.s_sc)
        b.D("scalar")
        b.I("scalar", lambda s: s.activation(
            self.sqtmp[0:16, 0:512], self.sqtmp[0:16, 0:512], AF.Exp, scale=-1.0),
            self.s_sc)
        b.D("scalar")
        b.I("scalar", lambda s: s.activation(
            self.sqtmp[0:16, 0:512], self.sqtmp[0:16, 0:512], AF.Ln, bias=1.0),
            self.s_sc)
        b.D("scalar")
        b.I("scalar", lambda s: s.activation(
            self.cext[0:16, 0:512], self.pb[3][0:16, 0:512], AF.Relu,
            bias=self.dtb[0:16, 0:1]), self.s_sc)
        b.I("scalar", lambda s: s.drain(), self.bsem[3])
        spn = self.s_sc.n
        b.W("vector", self.s_sc, spn)
        b.E("vector", lambda v: v.tensor_add(
            self.sqtmp[0:16, 0:512], self.sqtmp[0:16, 0:512], self.cext[0:16, 0:512]))
        b.D("vector")
        b.I("vector", lambda v, d=g_ap: v.tensor_scalar_mul(
            d, self.sqtmp[0:16, 0:512], self.negal[0:16, 0:1]), self.s_ba)
        b.F("vector", self.s_v)

    def l0_delta_chunk(self, tb, c, KVN, BAN):
        b, cfg = self.b, self.cfg
        c0 = 512 * tb
        gch = tb * 4 + c
        col = 128 * c
        bs = self.bsem
        s_t, s_v, s_sc = self.s_t, self.s_v, self.s_sc
        s_kv, s_ba = self.s_kv, self.s_ba

        g_ap = self.ar16[32:48, c0 + col:c0 + col + 128]
        beta_ap = self.ar16[0:16, c0 + col:c0 + col + 128]
        b.W("tensor", s_ba, BAN)
        self.bwait(4)
        t1 = self.TRg(self.pb[4][0:128, 0:16], g_ap, self.identf[32:48, 32:48])
        b.W("vector", s_t, t1)
        b.I("vector", lambda v: v.tensor_copy(self.gtm[:, :], self.pb[4][0:128, 0:16]),
            bs[4])
        b.F("vector", s_v)
        self.bwait(4)
        t2 = self.MMg(self.pb[4][0:128, 16:32], self.mask_i1[:, :], self.gtm[:, :],
                      waits=((s_v, s_v.n),))
        b.W("vector", s_t, t2)
        b.E("vector", lambda v: v.tensor_copy(self.gtm[:, :], self.pb[4][0:128, 16:32]))
        b.D("vector")
        b.I("vector", lambda v: v.tensor_scalar_mul(self.ngtm[:, :], self.gtm[:, :], -1.0),
            bs[4])
        b.F("vector", s_v)
        vgtm = s_v.n
        self.bwait(5)
        t3 = self.TRg(self.pb[5][0:16, 0:128], self.gtm[:, :], self.identf[:, :],
                      waits=((s_v, vgtm),))
        b.W("vector", s_t, t3)
        b.I("vector", lambda v: v.tensor_copy(self.gfm[:, :], self.pb[5][0:16, 0:128]),
            bs[5])
        b.F("vector", s_v)
        vgfm = s_v.n
        self.bwait(5)
        t4 = self.TRg(self.pb[5][0:128, 128:144], beta_ap, self.identf[0:16, 0:16])
        b.W("vector", s_t, t4)
        b.I("vector", lambda v: v.tensor_copy(self.btm[:, :], self.pb[5][0:128, 128:144]),
            bs[5])
        b.F("vector", s_v)
        b.W("scalar", s_v, vgfm)
        b.I("scalar", lambda s: s.activation(self.egfm[:, :], self.gfm[:, :], AF.Exp),
            s_sc)
        b.D("scalar")
        b.I("scalar", lambda s: s.activation(self.edec[:, :], self.gfm[:, :], AF.Exp,
                                             scale=-1.0, bias=self.gfm[:, 127:128]), s_sc)
        scg = s_sc.n
        b.W("vector", s_sc, scg)
        b.E("vector", lambda v, ba=beta_ap: v.tensor_tensor(
            self.bexp[:, :], ba, self.egfm[:, :], ALU.mult))
        b.D("vector")
        b.E("vector", lambda v: v.tensor_copy(self.eg_b[:, :], self.egfm[:, :]))
        b.D("vector")
        b.E("vector", lambda v, ba=beta_ap: v.tensor_copy(self.bbf[:, :], ba))
        b.F("vector", s_v)
        vpre = s_v.n

        for j in range(8):
            K2 = self.kb[:, j * 512 + col:j * 512 + col + 128]
            Q2 = self.qb[:, j * 512 + col:j * 512 + col + 128]
            sel_j = self.sel16[:, 128 * j:128 * (j + 1)]
            self.bwait(0)
            self.MMg(self.pb[0][:, 0:128], sel_j, self.bexp[:, :],
                     waits=((s_v, vpre),))
            self.bwait(1)
            self.MMg(self.pb[1][:, 0:128], sel_j, self.eg_b[:, :])
            self.bwait(2)
            bb3 = self.MMg(self.pb[2][:, 0:128], sel_j, self.edec[:, :])
            self.bwait(3)
            bb4 = self.MMg(self.pb[3][:, 0:128], sel_j, self.bbf[:, :])
            b.W("vector", s_t, bb4)
            b.W("vector", s_kv, KVN)
            b.I("vector", lambda v, K2=K2: v.tensor_tensor(
                self.Wf2[:, :], K2, self.pb[0][:, 0:128], ALU.mult), bs[0])
            b.D("vector")
            b.E("vector", lambda v, Q2=Q2: v.tensor_tensor(
                self.Qt2[:, :], Q2, self.pb[1][:, 0:128], ALU.mult))
            b.D("vector")
            b.I("vector", lambda v: v.tensor_copy(
                self.egCc[:, :], self.pb[1][:, 127:128]), bs[1])
            b.D("vector")
            b.I("vector", lambda v, K2=K2: v.tensor_tensor(
                self.Kb2[:, :], K2, self.pb[2][:, 0:128], ALU.mult), bs[2])
            b.D("vector")
            b.I("vector", lambda v, K2=K2: v.tensor_tensor(
                self.Kbb[:, :], K2, self.pb[3][:, 0:128], ALU.mult), bs[3])
            b.F("vector", s_v)
            vW = s_v.n

            for hh in range(2):
                h = 2 * j + hh
                base = 64 * hh
                Kh = K2[base:base + 64, :]
                Qh = Q2[base:base + 64, :]
                Vh = self.vb[:, h * 512 + col:h * 512 + col + 128]
                idb = self.identf[base:base + 64, base:base + 64]
                idbb = self.ident[base:base + 64, base:base + 64]
                Sfh = self.S_f[base:base + 64, j * 128:(j + 1) * 128]
                Sbh = self.S_bf[base:base + 64, j * 128:(j + 1) * 128]
                Arfh = self.Ar_f[base:base + 64, j * 64:(j + 1) * 64]
                Arbh = self.Ar_bf[base:base + 64, j * 64:(j + 1) * 64]
                self.bwait(3)
                tPK = self.MMg(self.pb[3][:, 0:128], self.Kbb[base:base + 64, :], Kh,
                               waits=((s_kv, KVN), (s_v, vW)))
                self.bwait(4)
                tKQ = self.MMg(self.pb[4][:, 0:128], Kh, Qh)
                self.bwait(5)
                tex = self.MMg(self.pb[5][0:1, 0:128], self.identf[0:16, h:h + 1],
                               self.gfm[:, :], waits=((s_v, vgfm),))
                b.W("vector", s_t, tex)
                b.I("vector", lambda v: v.tensor_copy(self.onrow[:, :],
                                                      self.pb[5][0:1, 0:128]), bs[5])
                b.F("vector", s_v)
                vex = s_v.n
                self.bwait(5)
                tgb = self.MMg(self.pb[5][:, 0:128], self.onesall[0:1, 0:128],
                               self.onrow[:, :], waits=((s_v, vex),))
                b.W("vector", s_t, tgb)
                b.I("vector", lambda v, h=h: v.tensor_scalar(
                    self.tmpA[:, :], self.pb[5][:, 0:128], self.ngtm[:, h:h + 1], 0.0,
                    ALU.add, ALU.min), bs[5])
                b.F("vector", s_v)
                b.W("scalar", s_v, s_v.n)
                b.I("scalar", lambda s: s.activation(self.Dsb[:, :], self.tmpA[:, :],
                                                     AF.Exp), s_sc)
                scD = s_sc.n
                b.W("vector", s_sc, scD)
                b.W("vector", s_t, tKQ)
                b.E("vector", lambda v: v.tensor_tensor(
                    self.tmpA[:, :], self.Dsb[:, :], self.pb[4][:, 0:128], ALU.mult))
                b.I("vector", lambda v: v.drain(), bs[4])
                b.E("vector", lambda v: v.tensor_tensor(
                    self.At_bf[:, :], self.tmpA[:, :], self.mask_i1[:, :], ALU.mult))
                b.D("vector")
                b.E("vector", lambda v: v.tensor_tensor(
                    self.tmpB[:, :], self.Dsb[:, :], self.pb[3][:, 0:128], ALU.mult))
                b.I("vector", lambda v: v.drain(), bs[3])
                b.E("vector", lambda v: v.tensor_tensor(
                    self.Mt_bf[:, :], self.tmpB[:, :], self.mask_s[:, :], ALU.mult))
                b.F("vector", s_v)
                vM = s_v.n
                self.bwait(6)
                tmt = self.TRg(self.pb[6][:, 0:128], self.Mt_bf[:, :], self.ident[:, :],
                               waits=((s_v, vM),))
                b.W("vector", s_t, tmt)
                b.I("vector", lambda v: v.tensor_copy(self.Mti_bf[:, :],
                                                      self.pb[6][:, 0:128]), bs[6])
                b.F("vector", s_v)
                vmti = s_v.n
                self.bwait(4)
                tws = self.MMg(self.pb[4][:, 0:128], self.Wf2[base:base + 64, :], Sbh,
                               waits=((s_v, vW),))
                self.bwait(7)
                tvt = self.TRg(self.pb[7][:, 0:128], Vh, self.ident[:, :])
                b.W("vector", s_t, tvt)
                b.E("vector", lambda v, h=h: v.tensor_scalar_mul(
                    self.tmpA[:, :], self.pb[7][:, 0:128], self.btm[:, h:h + 1]))
                b.I("vector", lambda v: v.drain(), bs[7])
                b.E("vector", lambda v: v.tensor_sub(
                    self.RHS[:, 0:128], self.tmpA[:, :], self.pb[4][:, 0:128]))
                b.I("vector", lambda v: v.drain(), bs[4])
                self.bwait(7)
                twt = self.TRg(self.pb[7][:, 128:192], self.Wf2[base:base + 64, :], idbb)
                b.W("vector", s_t, twt)
                b.I("vector", lambda v: v.tensor_copy(
                    self.RHS[:, 128:192], self.pb[7][:, 128:192]), bs[7])
                b.F("vector", s_v)
                vrhs = s_v.n
                NSQ = 4
                curT, curti = self.Mt_bf[:, :], self.Mti_bf[:, :]
                for lvl in range(NSQ):
                    pt = self.PT[:, 128 * lvl:128 * (lvl + 1)]
                    pti = self.PT[:, 128 * (4 + lvl % 2):128 * (5 + lvl % 2)]
                    self.bwait(2)
                    tsq = self.MMg(self.pb[2][:, 0:128], curT, curti,
                                   waits=((s_v, s_v.n),))
                    self.bwait(3)
                    tsq2 = self.MMg(self.pb[3][:, 0:128], curti, curT)
                    b.W("vector", s_t, tsq2)
                    b.I("vector", lambda v, pti=pti: v.tensor_copy(
                        pti, self.pb[2][:, 0:128]), bs[2])
                    b.D("vector")
                    b.I("vector", lambda v, pt=pt: v.tensor_copy(pt, self.pb[3][:, 0:128]),
                        bs[3])
                    b.F("vector", s_v)
                    curT, curti = pt, pti
                vsq = s_v.n
                NAPP = NSQ + 1
                Y = self.RHS[:, :]
                b.W("tensor", s_v, vsq)
                for a in range(NAPP):
                    lhs = (self.Mt_bf[:, :] if a == NAPP - 1 else
                           self.PT[:, 128 * (NSQ - 1 - a):128 * (NSQ - a)])
                    Yn = self.Ybuf[:, 192 * (a % 2):192 * (a % 2) + 192]
                    self.bwait(5)
                    tap = self.MMg(self.pb[5][:, 0:192], lhs, Y,
                                   waits=((s_v, vrhs if a == 0 else s_v.n),))
                    b.W("vector", s_t, tap)
                    if a == NAPP - 1:
                        b.I("vector", lambda v, Yn=Yn, Y=Y: v.tensor_sub(
                            Yn, Y, self.pb[5][:, 0:192]), bs[5])
                    else:
                        b.I("vector", lambda v, Yn=Yn, Y=Y: v.tensor_add(
                            Yn, Y, self.pb[5][:, 0:192]), bs[5])
                    b.F("vector", s_v)
                    Y = Yn
                U = Y
                vU = s_v.n
                self.bwait(0)
                to1 = self.MMg(self.pb[0][:, 0:128], Sbh, self.Qt2[base:base + 64, :])
                self.bwait(1)
                to2 = self.MMg(self.pb[1][:, 0:128], U[:, 0:128], self.At_bf[:, :],
                               waits=((s_v, vU),))
                b.W("vector", s_t, to2)
                b.E("vector", lambda v: v.tensor_copy(self.o1sb[:, :], self.pb[0][:, 0:128]))
                b.I("vector", lambda v: v.drain(), bs[0])
                ob_ap = self.obc[:, h * 128:(h + 1) * 128]
                if gch >= 1 and h == 0:
                    so, sn = self.soc[gch - 1]
                    b.W("vector", so, sn)
                b.I("vector", lambda v, d=ob_ap: v.tensor_add(
                    d, self.o1sb[:, :], self.pb[1][:, 0:128]), bs[1])
                b.F("vector", s_v)
                self.bwait(7)
                tkb = self.TRg(self.pb[7][:, 256:320], self.Kb2[base:base + 64, :], idbb)
                b.W("vector", s_t, tkb)
                b.I("vector", lambda v: v.tensor_copy(self.Kbt[:, :],
                                                      self.pb[7][:, 256:320]), bs[7])
                b.F("vector", s_v)
                vkbt = s_v.n
                self.bwait(2)
                tsu = self.MMg(self.pb[2][base:base + 64, 0:192], self.Kbt[:, :],
                               U[:, 0:192], tp=(0, base), waits=((s_v, vkbt),))
                b.W("vector", s_t, tsu)
                eg_ap = self.egCc[base:base + 64, 0:1]
                ps_ap = self.pb[2][base:base + 64, 0:128]
                b.E("vector", lambda v, Sfh=Sfh, eg=eg_ap, ps=ps_ap:
                    v.scalar_tensor_tensor(Sfh, Sfh, eg, ps, ALU.mult, ALU.add))
                b.D("vector")
                b.E("vector", lambda v, Sfh=Sfh, Sbh=Sbh: v.tensor_copy(Sbh, Sfh))
                b.D("vector")
                tB_ap = self.tmpB[base:base + 64, 0:64]
                pa_ap = self.pb[2][base:base + 64, 128:192]
                b.I("vector", lambda v, idb=idb, tB=tB_ap, eg=eg_ap, pa=pa_ap:
                    v.scalar_tensor_tensor(tB, idb, eg, pa, ALU.mult, ALU.subtract),
                    bs[2])
                b.D("vector")
                z_ap = self.Z2[base:base + 64, 0:64]
                b.E("vector", lambda v, z=z_ap, tB=tB_ap: v.tensor_copy(z, tB))
                b.F("vector", s_v)
                vach = s_v.n
                self.bwait(3)
                tz = self.MMg(self.pb[3][base:base + 64, 0:128], U[:, 128:192],
                              self.At_bf[:, :], tp=(0, base))
                b.W("vector", s_t, tz)
                tA_ap = self.tmpA[base:base + 64, 0:128]
                b.I("vector", lambda v, tA=tA_ap, base=base: v.tensor_sub(
                    tA, self.Qt2[base:base + 64, :],
                    self.pb[3][base:base + 64, 0:128]), bs[3])
                b.D("vector")
                zc_ap = self.Mti_bf[base:base + 64, 0:128]
                b.E("vector", lambda v, z=zc_ap, tA=tA_ap: v.tensor_copy(z, tA))
                b.F("vector", s_v)
                vzb = s_v.n
                gslot = gch % 2
                gb_ap = self.gst[base:base + 64, gslot * 1024 + j * 128:
                                 gslot * 1024 + j * 128 + 128]
                self.bwait(4)
                tg = self.MMg(self.pb[4][base:base + 64, 0:128], Arbh, zc_ap,
                              tp=(base, base), waits=((s_v, vzb),))
                b.W("vector", s_t, tg)
                if gch >= 2 and h == 0:
                    so, sn = self.sgd[gch - 2]
                    b.W("vector", so, sn)
                pg_ap = self.pb[4][base:base + 64, 0:128]
                b.I("vector", lambda v, d=gb_ap, p=pg_ap: v.tensor_copy(d, p), bs[4])
                self.bwait(5)
                ta1 = self.MMg(self.pb[5][base:base + 64, 0:64], z_ap, idbb,
                               tp=(base, base), waits=((s_v, vach),))
                b.W("vector", s_t, ta1)
                ac_ap = self.AchT[base:base + 64, 0:64]
                b.I("vector", lambda v, d=ac_ap, base=base: v.tensor_copy(
                    d, self.pb[5][base:base + 64, 0:64]), bs[5])
                b.F("vector", s_v)
                vat = s_v.n
                self.bwait(5)
                ta2 = self.MMg(self.pb[5][base:base + 64, 64:128], ac_ap, Arbh,
                               tp=(base, base), waits=((s_v, vat),))
                b.W("vector", s_t, ta2)
                b.E("vector", lambda v, Arfh=Arfh, base=base: v.tensor_copy(
                    Arfh, self.pb[5][base:base + 64, 64:128]))
                b.D("vector")
                b.I("vector", lambda v, Arbh=Arbh, Arfh=Arfh: v.tensor_copy(Arbh, Arfh),
                    bs[5])
                b.F("vector", s_v)
        b.F("vector", s_v)
        VEND = s_v.n
        soc = self.socr[gch % 2]
        b.W("sync", s_v, VEND)
        b.I("sync", lambda e, gch=gch: e.dma_start(
            out=self.oloc_d.ap()[gch], in_=self.obc[:, :]), soc, k=16)
        self.soc[gch] = (soc, soc.n)
        sgd = self.sgdr[gch % 2]
        gslot = gch % 2
        b.I("sync", lambda e, gch=gch, gslot=gslot: e.dma_start(
            out=self.g_d.ap()[gch], in_=self.gst[:, gslot * 1024:(gslot + 1) * 1024]),
            sgd, k=16)
        self.sgd[gch] = (sgd, sgd.n)

    def ph4_ag_compose(self):
        b, cfg = self.b, self.cfg
        cc0_in, cc0_out = self.ccs[0], self.ccs[1]
        s_ag = b.sem("s_ag")
        s_agl = b.sem("s_agl")
        b.F("vector", self.s_v)
        vfin = self.s_v.n
        b.W("sync", self.s_v, vfin)
        for j in range(8):
            a_dst = cc0_in.ap()[j * 128:(j + 1) * 128, 0:64]
            a_src = self.Ar_f[:, j * 64:(j + 1) * 64]
            b.I("sync", lambda e, a=a_dst, s_=a_src: e.dma_start(out=a, in_=s_),
                s_ag, k=16)
            s_dst = cc0_in.ap()[j * 128:(j + 1) * 128, 64:192]
            s_src = self.S_f[:, j * 128:(j + 1) * 128]
            b.I("sync", lambda e, a=s_dst, s_=s_src: e.dma_start(out=a, in_=s_),
                s_ag, k=16)
        b.W("gpsimd", s_ag, s_ag.n)
        b.I("gpsimd", lambda g: g.collective_compute(
            "AllGather", ALU.bypass, replica_groups=[list(range(NCORE))],
            ins=[cc0_in.ap().opt()], outs=[cc0_out.ap().opt()]), self.s_cc)
        CC0 = self.s_cc.n
        b.W("gpsimd", s_ag, s_ag.n)
        self._ar_ident(self.AID[:, :])
        b.F("gpsimd", self.s_g)
        AIDN = self.s_g.n
        b.W("vector", self.s_g, AIDN)
        b.W("tensor", self.s_g, self.G_CONST)
        b.W("tensor", self.s_v, self.V_NEGAL)
        self.bwait(0)
        b.I("tensor", lambda t: t.matmul(self.pb[0][:, 0:NCORE], self.mksel[:, :],
                                         self.cmk[:, :], start=True, stop=True), self.s_t)
        self.bwait(1)
        b.I("tensor", lambda t: t.matmul(self.pb[1][:, 0:NCORE], self.nksel[:, :],
                                         self.cmk[:, :], start=True, stop=True), self.s_t)
        b.W("vector", self.s_t, self.s_t.n)
        b.I("vector", lambda v: v.tensor_copy(self.mubc[:, :], self.pb[0][:, 0:NCORE]),
            self.bsem[0])
        b.I("vector", lambda v: v.tensor_copy(self.nubc[:, :], self.pb[1][:, 0:NCORE]),
            self.bsem[1])
        b.W("vector", s_ag, s_ag.n)
        b.E("vector", lambda v: v.memset(self.S_f[:, :], 0.0))
        b.D("vector")
        b.E("vector", lambda v: v.memset(self.S_bf[:, :], 0.0))
        b.F("vector", self.s_v)
        b.W("sync", self.s_cc, CC0)
        for l in range(NCORE):
            for j in range(8):
                r0 = l * NH * DKH + j * 128
                b.I("sync", lambda e, r0=r0, j=j: e.dma_start(
                    out=self.Atmp[:, j * 64:(j + 1) * 64],
                    in_=cc0_out.ap()[r0:r0 + 128, 0:64]), s_agl, k=16)
                b.I("sync", lambda e, r0=r0, j=j: e.dma_start(
                    out=self.Btmp[:, j * 128:(j + 1) * 128],
                    in_=cc0_out.ap()[r0:r0 + 128, 64:192]), s_agl, k=16)
            AGL = s_agl.n
            b.W("tensor", s_agl, AGL)
            for jj in range(8):
                for sub in range(2):
                    bb = 64 * sub
                    src = self.Atmp[bb:bb + 64, jj * 64:(jj + 1) * 64]
                    dst = self.AtT[bb:bb + 64, jj * 64:(jj + 1) * 64]
                    pbk = 2 + (jj % 2)
                    self.bwait(pbk)
                    reg = self.pb[pbk][bb:bb + 64, 0:64]
                    b.I("tensor", lambda t, r=reg, s_=src, bb=bb: t.matmul(
                        r, s_, self.identf[bb:bb + 64, bb:bb + 64],
                        start=True, stop=True, tile_position=(bb, bb),
                        skip_group_check=True), self.s_t)
                    b.W("vector", self.s_t, self.s_t.n)
                    b.I("vector", lambda v, d=dst, r=reg: v.tensor_copy(d, r),
                        self.bsem[pbk])
            b.F("vector", self.s_v)
            mu = self.mubc[:, l:l + 1]
            b.E("vector", lambda v: v.tensor_sub(self.AtT[:, :], self.AtT[:, :],
                                                 self.AID[:, :]))
            b.D("vector")
            b.E("vector", lambda v, mu=mu: v.scalar_tensor_tensor(
                self.AtT[:, :], self.AtT[:, :], mu, self.AID[:, :], ALU.mult, ALU.add))
            b.D("vector")
            b.E("vector", lambda v: v.tensor_copy(self.AtTb[:, :], self.AtT[:, :]))
            b.F("vector", self.s_v)
            vbl = self.s_v.n
            b.W("tensor", self.s_v, vbl)
            for jj in range(8):
                for sub in range(2):
                    bb = 64 * sub
                    lhs = self.AtTb[bb:bb + 64, jj * 64:(jj + 1) * 64]
                    rhs = self.S_bf[bb:bb + 64, jj * 128:(jj + 1) * 128]
                    pbk = 4 + (jj % 2)
                    self.bwait(pbk)
                    reg = self.pb[pbk][bb:bb + 64, 0:128]
                    b.I("tensor", lambda t, r=reg, l_=lhs, rh=rhs, bb=bb: t.matmul(
                        r, l_, rh, start=True, stop=True, tile_position=(bb, bb)),
                        self.s_t)
                    b.W("vector", self.s_t, self.s_t.n)
                    sfh = self.S_f[bb:bb + 64, jj * 128:(jj + 1) * 128]
                    bth = self.Btmp[bb:bb + 64, jj * 128:(jj + 1) * 128]
                    muh = self.mubc[bb:bb + 64, l:l + 1]
                    b.I("vector", lambda v, s_=sfh, bt=bth, mu=muh, r=reg:
                        v.scalar_tensor_tensor(s_, bt, mu, r, ALU.mult, ALU.add),
                        self.bsem[pbk])
            b.F("vector", self.s_v)
            b.E("vector", lambda v: v.tensor_copy(self.S_bf[:, :], self.S_f[:, :]))
            b.F("vector", self.s_v)
            b.F("tensor", self.s_t)
            b.W("sync", self.s_t, self.s_t.n)
            b.W("sync", self.s_v, self.s_v.n)
        self.V_SIN = self.s_v.n

    def ph5_l0_blocks(self):
        b, cfg = self.b, self.cfg
        s_t, s_v, s_sc = self.s_t, self.s_v, self.s_sc
        self.s_fin = b.sem("s_fin")
        self.s_or = b.sem("s_or")
        self.s_gr = b.sem("s_gr")
        for s in (self.socr[0], self.socr[1], self.sgdr[0], self.sgdr[1]):
            b.W("sync", s, s.n)
        for tb in range(cfg.nblk):
            self.xn_block(tb, dst=self.arn(4096 * tb, 4096))
        self.V_XNF = self.s_v.n
        wgg = WS(self, [self.arn(32768 + 1024 * i, 1024) for i in range(4)])
        wgu = WS(self, [self.arn(20480 + 1024 * i, 1024) for i in range(8)])
        wdn = WS(self, [self.arn(42496 + 1408 * i, 1408) for i in range(2)])
        wwo = WS(self, [self.arn(36864 + 2048 * i, 2048) for i in range(2)])
        wwo.barrier((self.s_v, self.V_XNF))
        wgg.barrier((self.s_v, self.V_XNF))
        wdn.barrier((self.s_v, self.V_XNF))
        self.ofree = None
        self.gfence = {}
        for tb in range(cfg.nblk):
            self.l0_block_fin(tb, wgg, wgu, wdn, wwo)
        b.F("tensor", s_t)
        self.PH5B_T = (s_t, s_t.n)

    def h5_block(self, tb):
        b, cfg = self.b, self.cfg
        c0 = 512 * tb
        b.W("tensor", self.s_v, self.V_XNF)
        self.bwait(4)
        self.MMg(self.pb[4][:, 0:512], self.onesall[64:65, 0:128],
                 self.arsl[64:65, c0:c0 + 512])
        tn = self.s_t.n
        b.W("vector", self.s_t, tn)
        for dc in range(DCH):
            xn_ap = self.arn(4096 * tb + 512 * dc, 512)
            h_ap = self.arn(16384 + 512 * dc, 512)
            b.D("vector")
            b.W("vector", self.s_sc, self.s_sc.n)
            b.E("vector", lambda v, x=xn_ap, dc=dc: v.tensor_scalar_mul(
                self.sqtmp[:, :], x, self.nw[:, dc:dc + 1]))
            b.D("vector")
            last = dc == DCH - 1
            b.I("vector", lambda v, h=h_ap: v.tensor_tensor(
                h, self.sqtmp[:, :], self.pb[4][:, 0:512], ALU.mult),
                self.bsem[4] if last else self.s_v)
        b.F("vector", self.s_v)
        return self.s_v.n

    def l0_block_fin(self, tb, wgg, wgu, wdn, wwo):
        b, cfg = self.b, self.cfg
        tpc = cfg.tpc
        c0 = 512 * tb
        bs = self.bsem
        s_t, s_v, s_sc, s_fin = self.s_t, self.s_v, self.s_sc, self.s_fin
        hb5 = lambda dc: self.arn(16384 + 512 * dc, 512)
        gos = lambda h, c: self.arn(20480 + 512 * h + 128 * c, 128)
        obc5 = self.arn(28672, 2048)
        x1s = lambda dc: self.arn(32768 + 512 * dc, 512)
        HB = self.h5_block(tb)
        img = self.dr["w_g"]
        for oc in range(NH):
            j, buf, tok = wgg.get(img[:, oc * DCH * 128:(oc + 1) * DCH * 128])
            bank = oc % 2
            b.W("tensor", tok[0], tok[1])
            b.W("tensor", s_v, HB)
            self.bwait(bank)
            for dc in range(DCH):
                b.I("tensor", lambda t, buf=buf, dc=dc, bank=bank: t.matmul(
                    self.pb[bank][:, 0:512], buf[:, dc * 128:(dc + 1) * 128],
                    hb5(dc), start=(dc == 0), stop=(dc == DCH - 1),
                    skip_group_check=True), s_t)
            wgg.release(j)
            tn = s_t.n
            dst = self.arn(20480 + 512 * oc, 512)
            b.W("scalar", s_t, tn)
            b.I("scalar", lambda s, d=dst, bank=bank: s.activation(
                d, self.pb[bank][:, 0:512], AF.Silu), s_sc)
            b.I("scalar", lambda s, bank=bank: s.drain(), bs[bank])
        b.F("scalar", s_sc)
        GS = s_sc.n
        b.W("vector", s_sc, GS)
        for c in range(4):
            gch = tb * 4 + c
            gslot = gch % 2
            gbuf = self.arn(30720 + 1024 * gslot, 1024)
            if self.ofree is not None:
                b.W("sync", self.ofree[0], self.ofree[1])
            b.I("sync", lambda e, gch=gch, d=obc5: e.dma_start(
                out=d, in_=self.oloc_d.ap()[gch]), self.s_or, k=16)
            if gch >= 2:
                fo = self.gfence[gch - 2]
                b.W("sync", fo[0], fo[1])
            b.I("sync", lambda e, gch=gch, d=gbuf: e.dma_start(
                out=d, in_=self.g_d.ap()[gch]), self.s_gr, k=16)
            ORN, GRN = self.s_or.n, self.s_gr.n
            for h in range(NH):
                jj, sub = h // 2, h % 2
                bb = 64 * sub
                Sin = self.S_bf[bb:bb + 64, jj * 128:(jj + 1) * 128]
                Gh = gbuf[bb:bb + 64, jj * 128:(jj + 1) * 128]
                ob_ap = obc5[:, h * 128:(h + 1) * 128]
                og_ap = gos(h, c)
                self.bwait(0)
                b.W("tensor", s_v, self.V_SIN)
                b.W("tensor", self.s_gr, GRN)
                self.MMg(self.pb[0][:, 0:128], Sin, Gh, tp=(bb, 0))
                tcor = s_t.n
                if h == 0:
                    b.W("vector", self.s_or, ORN)
                b.W("vector", s_t, tcor)
                b.W("vector", s_sc, s_sc.n)
                b.I("vector", lambda v, o=ob_ap: v.tensor_add(
                    self.otile[:, :], o, self.pb[0][:, 0:128]), bs[0])
                if h == NH - 1:
                    b.F("vector", s_v)
                    self.ofree = (s_v, s_v.n)
                else:
                    b.F("vector", s_v)
                vcor = s_v.n
                b.W("scalar", s_v, vcor)
                b.I("scalar", lambda s: s.activation(
                    self.sqtmp[:, 0:128], self.otile[:, :], AF.Square), s_sc)
                b.W("tensor", s_sc, s_sc.n)
                self.bwait(1)
                self.MMg(self.pb[1][0:1, 0:128], self.onesc[:, :], self.sqtmp[:, 0:128])
                tsum = s_t.n
                b.W("scalar", s_t, tsum)
                b.I("scalar", lambda s: s.activation(
                    self.onrow[:, :], self.pb[1][0:1, 0:128], AF.Sqrt,
                    scale=self.cst(1.0 / DVH, 1), bias=self.cst(EPS_RMS, 1)), s_sc)
                b.I("scalar", lambda s: s.drain(), bs[1])
                b.W("vector", s_sc, s_sc.n)
                b.E("vector", lambda v: v.reciprocal(self.onrow[:, :], self.onrow[:, :]))
                b.F("vector", s_v)
                vrn = s_v.n
                b.W("tensor", s_v, vrn)
                self.bwait(2)
                self.MMg(self.pb[2][:, 0:128], self.onesr[:, :], self.onrow[:, :])
                tbc = s_t.n
                b.W("vector", s_t, tbc)
                b.E("vector", lambda v: v.tensor_tensor(
                    self.otile[:, :], self.otile[:, :], self.pb[2][:, 0:128], ALU.mult))
                b.I("vector", lambda v: v.drain(), bs[2])
                b.E("vector", lambda v: v.tensor_scalar_mul(
                    self.otile[:, :], self.otile[:, :], self.onw[:, 0:1]))
                b.D("vector")
                b.I("vector", lambda v, d=og_ap: v.tensor_tensor(
                    d, self.otile[:, :], d, ALU.mult), s_fin)
                b.F("vector", s_v)
            b.F("tensor", s_t)
            self.gfence[gch] = (s_t, s_t.n)
        b.F("vector", s_fin)
        OGN = s_fin.n
        img = self.dr["w_o"]
        b.W("tensor", s_v, self.V_SLIVERS)
        self.bwait(3)
        self.MMg(self.pb[3][:, 0:512], self.onesall[0:1, 0:128],
                 self.arsl[0:1, c0:c0 + 512])
        self.bwait(5)
        self.MMg(self.pb[5][:, 0:512], self.onesall[32:33, 0:128],
                 self.arsl[32:33, c0:c0 + 512])
        self.bwait(2)
        for dc in range(DCH):
            j, buf, tok = wwo.get(img[:, dc * NH * 128:(dc + 1) * NH * 128])
            bank = dc % 2
            b.W("tensor", tok[0], tok[1])
            b.W("tensor", s_fin, OGN)
            self.bwait(bank)
            for vc in range(NH):
                b.I("tensor", lambda t, buf=buf, vc=vc, bank=bank: t.matmul(
                    self.pb[bank][:, 0:512], buf[:, vc * 128:(vc + 1) * 128],
                    self.arn(20480 + 512 * vc, 512),
                    start=(vc == 0), stop=(vc == NH - 1),
                    skip_group_check=True), s_t)
            wwo.release(j)
            tmix = s_t.n
            xr_ap = self.xres[:, dc * tpc + c0:dc * tpc + c0 + 512]
            b.W("vector", s_t, tmix)
            b.W("vector", s_sc, s_sc.n)
            b.E("vector", lambda v, x=xr_ap: v.tensor_add(
                self.sqtmp[:, :], x, self.pb[3][:, 0:512]))
            b.D("vector")
            b.E("vector", lambda v: v.tensor_tensor(
                self.sqtmp[:, :], self.sqtmp[:, :], self.pb[5][:, 0:512], ALU.mult))
            b.D("vector")
            b.I("vector", lambda v, bank=bank: v.tensor_add(
                self.cext[:, 0:512], self.sqtmp[:, :], self.pb[bank][:, 0:512]),
                bs[bank])
            b.D("vector")
            b.E("vector", lambda v, x=xr_ap: v.tensor_add(x, x, self.cext[:, 0:512]))
            b.D("vector")
            b.I("vector", lambda v, dc=dc: v.tensor_copy(x1s(dc), self.cext[:, 0:512]),
                s_fin)
            b.F("vector", s_v)
            vx1 = s_v.n
            b.W("scalar", s_v, vx1)
            b.I("scalar", lambda s: s.activation(
                self.sqb[:, 0:512], self.cext[:, 0:512], AF.Square), s_sc)
            b.W("tensor", s_sc, s_sc.n)
            b.I("tensor", lambda t, dc=dc: t.matmul(
                self.pb[2][0:1, 0:512], self.onesc[:, :], self.sqb[:, 0:512],
                start=(dc == 0), stop=(dc == DCH - 1), skip_group_check=True), s_t)
            if dc < DCH - 1:
                b.W("scalar", s_t, s_t.n)
        b.I("vector", lambda v: v.drain(), bs[3])
        b.I("vector", lambda v: v.drain(), bs[5])
        b.F("vector", s_v)
        tss = s_t.n
        b.W("scalar", s_t, tss)
        b.I("scalar", lambda s: s.activation(
            self.l2row[0:1, 0:512], self.pb[2][0:1, 0:512], AF.Sqrt,
            scale=self.cst(1.0 / D, 1), bias=self.cst(EPS_RMS, 1)), s_sc)
        b.I("scalar", lambda s: s.drain(), bs[2])
        b.W("vector", s_sc, s_sc.n)
        b.E("vector", lambda v: v.reciprocal(self.l2row[0:1, 0:512],
                                             self.l2row[0:1, 0:512]))
        b.F("vector", s_v)
        vrm = s_v.n
        b.W("tensor", s_v, vrm)
        self.bwait(3)
        self.MMg(self.pb[3][:, 0:512], self.onesall[0:1, 0:128],
                 self.l2row[0:1, 0:512])
        trb = s_t.n
        b.W("vector", s_t, trb)
        for dc in range(DCH):
            b.D("vector")
            b.E("vector", lambda v, dc=dc: v.tensor_tensor(
                self.sqtmp[:, :], x1s(dc), self.pb[3][:, 0:512], ALU.mult))
            b.D("vector")
            b.I("vector", lambda v, dc=dc: v.tensor_scalar_mul(
                x1s(dc), self.sqtmp[:, :], self.pnw[:, dc:dc + 1]),
                bs[3] if dc == DCH - 1 else s_v)
        b.F("vector", s_v)
        VH2 = s_v.n
        gimg, uimg, dimg = self.dr["w_gate"], self.dr["w_up"], self.dr["w_down"]
        b.F("tensor", s_t)
        wgu.barrier((s_t, s_t.n))
        wdn.barrier((s_t, s_t.n))
        for half in range(2):
            for fi in range(11):
                ffc = half * 11 + fi
                w = 128 if ffc < FFCH - 1 else FF_LAST
                jg, bg, tokg = wgu.get(gimg[:, ffc * DCH * 128:(ffc + 1) * DCH * 128])
                ju, bu, toku = wgu.get(uimg[:, ffc * DCH * 128:(ffc + 1) * DCH * 128])
                b.W("tensor", tokg[0], tokg[1])
                b.W("tensor", toku[0], toku[1])
                b.W("tensor", s_v, VH2)
                self.bwait(0)
                self.bwait(1)
                for dc in range(DCH):
                    b.I("tensor", lambda t, bg=bg, dc=dc, w=w: t.matmul(
                        self.pb[0][0:w, 0:512], bg[:, dc * 128:dc * 128 + w],
                        x1s(dc), start=(dc == 0), stop=(dc == DCH - 1),
                        skip_group_check=True), s_t)
                for dc in range(DCH):
                    b.I("tensor", lambda t, bu=bu, dc=dc, w=w: t.matmul(
                        self.pb[1][0:w, 0:512], bu[:, dc * 128:dc * 128 + w],
                        x1s(dc), start=(dc == 0), stop=(dc == DCH - 1),
                        skip_group_check=True), s_t)
                wgu.release(jg)
                wgu.release(ju)
                tmlp = s_t.n
                hid_ap = self.arn(36864 + 512 * fi, 512)[0:w, :]
                b.W("scalar", s_t, tmlp)
                b.W("scalar", s_v, s_v.n)
                b.I("scalar", lambda s, w=w: s.activation(
                    self.sqtmp[0:w, 0:512], self.pb[0][0:w, 0:512], AF.Silu), s_sc)
                b.I("scalar", lambda s: s.drain(), bs[0])
                scs = s_sc.n
                b.W("vector", s_sc, scs)
                b.I("vector", lambda v, hid=hid_ap, w=w: v.tensor_tensor(
                    hid, self.sqtmp[0:w, 0:512], self.pb[1][0:w, 0:512], ALU.mult),
                    bs[1])
                b.F("vector", s_v)
            VHID = s_v.n
            for oc2 in range(DCH):
                j, buf, tok = wdn.get(
                    dimg[:, oc2 * FFCH * 128 + half * 11 * 128:
                         oc2 * FFCH * 128 + half * 11 * 128 + 1408])
                bank = 4 + oc2 % 2
                b.W("tensor", tok[0], tok[1])
                b.W("tensor", s_v, VHID)
                self.bwait(bank)
                for fi in range(11):
                    ffc = half * 11 + fi
                    w = 128 if ffc < FFCH - 1 else FF_LAST
                    b.I("tensor", lambda t, buf=buf, fi=fi, w=w, bank=bank: t.matmul(
                        self.pb[bank][:, 0:512], buf[0:w, fi * 128:fi * 128 + 128],
                        self.arn(36864 + 512 * fi, 512)[0:w, :],
                        start=(fi == 0), stop=(fi == 10), skip_group_check=True), s_t)
                wdn.release(j)
                tdn = s_t.n
                xr_ap = self.xres[:, oc2 * tpc + c0:oc2 * tpc + c0 + 512]
                b.W("vector", s_t, tdn)
                b.D("vector")
                b.I("vector", lambda v, x=xr_ap, bank=bank: v.tensor_tensor(
                    x, x, self.pb[bank][:, 0:512], ALU.add), bs[bank])
            b.F("vector", s_v)
        b.F("tensor", s_t)
        wgg.barrier((s_t, s_t.n))
        wwo.barrier((s_t, s_t.n))

    def route_tiles(self, src_row, nexp, dst_row, use_bias, probs):
        b, cfg = self.b, self.cfg
        s_t, s_v, s_sc = self.s_t, self.s_v, self.s_sc
        bs = self.bsem
        idsl = self.identf[src_row:src_row + nexp, src_row:src_row + nexp]
        VW = self.s_v.n
        for ti in range(cfg.ntile):
            cols = 128 * ti
            self.bwait(2)
            t1 = self.TRg(self.pb[2][0:128, 0:nexp],
                          self.ar16[src_row:src_row + nexp, cols:cols + 128],
                          idsl, tp=(src_row, 0),
                          waits=((s_v, VW),) if ti == 0 else ())
            b.W("vector", s_t, t1)
            b.I("vector", lambda v, n=nexp: v.tensor_copy(
                self.rtm[:, 0:n], self.pb[2][0:128, 0:n]), bs[2])
            b.D("vector")
            if use_bias:
                b.E("vector", lambda v, n=nexp: v.tensor_add(
                    self.rw1[:, 0:n], self.rtm[:, 0:n], self.ebb[:, 0:n]))
                sel = self.rw1
            else:
                sel = self.rtm
            b.D("vector")
            b.E("vector", lambda v, sel=sel, n=nexp: v.max(
                self.mx8[:, :], sel[:, 0:n]))
            b.D("vector")
            b.E("vector", lambda v: v.tensor_sub(
                self.sqtmp[:, 0:1], self.mx8[:, 0:1], self.mx8[:, 1:2]))
            b.F("vector", s_v)
            vmx = s_v.n
            if probs:
                b.W("scalar", s_v, vmx)
                b.I("scalar", lambda s: s.activation(
                    self.sqtmp[:, 1:2], self.sqtmp[:, 0:1], AF.Sigmoid), s_sc)
                b.D("scalar")
                b.I("scalar", lambda s: s.activation(
                    self.sqtmp[:, 2:3], self.sqtmp[:, 0:1], AF.Sigmoid, scale=-1.0),
                    s_sc)
                b.W("vector", s_sc, s_sc.n)
                b.E("vector", lambda v, sel=sel, n=nexp: v.tensor_scalar(
                    self.rtm[:, 16:16 + n], sel[:, 0:n], self.mx8[:, 0:1],
                    self.sqtmp[:, 1:2], ALU.is_equal, ALU.mult))
                b.D("vector")
                b.E("vector", lambda v, sel=sel, n=nexp: v.tensor_scalar(
                    self.sqtmp[:, 4:4 + n], sel[:, 0:n], self.mx8[:, 1:2],
                    self.sqtmp[:, 2:3], ALU.is_equal, ALU.mult))
                b.D("vector")
                b.E("vector", lambda v, n=nexp: v.tensor_add(
                    self.rtm[:, 16:16 + n], self.rtm[:, 16:16 + n],
                    self.sqtmp[:, 4:4 + n]))
            else:
                b.E("vector", lambda v, sel=sel, n=nexp: v.tensor_scalar(
                    self.rtm[:, 16:16 + n], sel[:, 0:n], self.mx8[:, 0:1], 1.0,
                    ALU.is_equal, ALU.mult))
                b.D("vector")
                b.E("vector", lambda v, sel=sel, n=nexp: v.tensor_scalar(
                    self.sqtmp[:, 4:4 + n], sel[:, 0:n], self.mx8[:, 1:2], 1.0,
                    ALU.is_equal, ALU.mult))
                b.D("vector")
                b.E("vector", lambda v, n=nexp: v.tensor_add(
                    self.rtm[:, 16:16 + n], self.rtm[:, 16:16 + n],
                    self.sqtmp[:, 4:4 + n]))
                b.D("vector")
                b.E("vector", lambda v, n=nexp: v.tensor_mul(
                    self.rtm[:, 16:16 + n], self.rtm[:, 16:16 + n], self.rtm[:, 0:n]))
            b.F("vector", s_v)
            vw2 = s_v.n
            self.bwait(3)
            t2 = self.TRg(self.pb[3][0:nexp, 0:128], self.rtm[:, 16:16 + nexp],
                          self.identf[:, :], waits=((s_v, vw2),))
            b.W("vector", s_t, t2)
            b.I("vector", lambda v, cols=cols, n=nexp: v.tensor_copy(
                self.ar16[dst_row:dst_row + n, cols:cols + 128],
                self.pb[3][0:n, 0:128]), bs[3])
            b.F("vector", s_v)

    def webc_build(self, e, nexp, wrow):
        b, cfg = self.b, self.cfg
        s_t, s_v = self.s_t, self.s_v
        bs = self.bsem
        for tb in range(cfg.nblk):
            c0 = 512 * tb
            self.bwait(0)
            self.MMg(self.pb[0][0:1, 0:512], self.identf[0:nexp, e:e + 1],
                     self.ar16[wrow:wrow + nexp, c0:c0 + 512],
                     waits=((s_v, s_v.n),))
            b.W("vector", s_t, s_t.n)
            b.I("vector", lambda v: v.tensor_copy(self.l2row[0:1, 0:512],
                                                  self.pb[0][0:1, 0:512]), bs[0])
            b.F("vector", s_v)
            vv = s_v.n
            self.bwait(0)
            self.MMg(self.pb[0][:, 0:512], self.onesall[0:1, 0:128],
                     self.l2row[0:1, 0:512], waits=((s_v, vv),))
            b.W("vector", s_t, s_t.n)
            b.I("vector", lambda v, tb=tb: v.tensor_copy(
                self.webc_v(tb), self.pb[0][:, 0:512]), bs[0])
            b.F("vector", s_v)

    def webc_v(self, tb):
        return self.xtm[tb // 2][:, 512 * (tb % 2):512 * (tb % 2) + 512]

    def ph5c_moe0(self):
        b, cfg = self.b, self.cfg
        tpc = cfg.tpc
        s_t, s_v, s_sc = self.s_t, self.s_v, self.s_sc
        bs = self.bsem
        b.W("tensor", s_v, self.s_v.n)
        self.route_tiles(64, E0, 0, use_bias=False, probs=True)
        wsl = WS(self, [self.arn(32768 + 1024 * i, 1024) for i in range(12)])
        wsl.barrier(self.PH5B_T)
        m1i, m2i, m3i = self.dr["w_m1"], self.dr["w_m2"], self.dr["w_m3"]
        hid = lambda tb, oc8: self.arn(16384 + (8 * tb + oc8) * 512, 512)
        for e in range(E0):
            self.webc_build(e, E0, 0)
            VWE = s_v.n
            for half in range(2):
                for oc8 in range(8):
                    oc = 8 * half + oc8
                    bk1 = 2 * (oc8 % 2)
                    bk2 = 1 + 2 * (oc8 % 2)
                    j1, b1, tk1 = wsl.get(m1i.ap()[e][:, oc * 1024:(oc + 1) * 1024])
                    j2, b2, tk2 = wsl.get(m2i.ap()[e][:, oc * 1024:(oc + 1) * 1024])
                    b.W("tensor", tk1[0], tk1[1])
                    b.W("tensor", tk2[0], tk2[1])
                    for tb in range(cfg.nblk):
                        self.bwait(bk1)
                        self.bwait(bk2)
                        for dc in range(DCH):
                            b.I("tensor", lambda t, b1=b1, dc=dc, tb=tb, bk1=bk1:
                                t.matmul(self.pb[bk1][:, 0:512],
                                         b1[:, dc * 128:(dc + 1) * 128],
                                         self.arn(4096 * tb + 512 * dc, 512),
                                         start=(dc == 0), stop=(dc == DCH - 1),
                                         skip_group_check=True), s_t)
                        for dc in range(DCH):
                            b.I("tensor", lambda t, b2=b2, dc=dc, tb=tb, bk2=bk2:
                                t.matmul(self.pb[bk2][:, 0:512],
                                         b2[:, dc * 128:(dc + 1) * 128],
                                         self.arn(4096 * tb + 512 * dc, 512),
                                         start=(dc == 0), stop=(dc == DCH - 1),
                                         skip_group_check=True), s_t)
                        th = s_t.n
                        b.W("scalar", s_t, th)
                        b.W("scalar", s_v, s_v.n)
                        b.I("scalar", lambda s, bk1=bk1: s.activation(
                            self.sqtmp[:, :], self.pb[bk1][:, 0:512], AF.Silu), s_sc)
                        b.I("scalar", lambda s: s.drain(), bs[bk1])
                        b.W("vector", s_sc, s_sc.n)
                        b.I("vector", lambda v, d=hid(tb, oc8), bk2=bk2: v.tensor_tensor(
                            d, self.sqtmp[:, :], self.pb[bk2][:, 0:512], ALU.mult),
                            bs[bk2])
                        b.F("vector", s_v)
                    wsl.release(j1)
                    wsl.release(j2)
                VHID = s_v.n
                for oc2 in range(DCH):
                    j3, b3, tk3 = wsl.get(
                        m3i.ap()[e][:, oc2 * 2048 + half * 1024:
                                    oc2 * 2048 + half * 1024 + 1024])
                    b.W("tensor", tk3[0], tk3[1])
                    b.W("tensor", s_v, VHID)
                    for tb in range(cfg.nblk):
                        bank = 4 + (oc2 + tb) % 2
                        self.bwait(bank)
                        for oc8 in range(8):
                            b.I("tensor", lambda t, b3=b3, oc8=oc8, tb=tb, bank=bank:
                                t.matmul(self.pb[bank][:, 0:512],
                                         b3[:, oc8 * 128:(oc8 + 1) * 128],
                                         self.arn(16384 + (8 * tb + oc8) * 512, 512),
                                         start=(oc8 == 0), stop=(oc8 == 7),
                                         skip_group_check=True), s_t)
                        tdn = s_t.n
                        c0 = 512 * tb
                        xr_ap = self.xres[:, oc2 * tpc + c0:oc2 * tpc + c0 + 512]
                        b.W("vector", s_t, tdn)
                        b.W("vector", s_sc, s_sc.n)
                        b.E("vector", lambda v, tb=tb, bank=bank: v.tensor_tensor(
                            self.sqb[:, 0:512], self.pb[bank][:, 0:512],
                            self.webc_v(tb), ALU.mult))
                        b.I("vector", lambda v, bank=bank: v.drain(), bs[bank])
                        b.I("vector", lambda v, x=xr_ap: v.tensor_add(
                            x, x, self.sqb[:, 0:512]), self.s_fin)
                        b.F("vector", s_v)
                    wsl.release(j3)
        b.F("vector", s_v)
        self.V_MOE = s_v.n
        b.F("tensor", s_t)
        self.T_MOE = (s_t, s_t.n)

    def ph7_l1_pre(self):
        b, cfg = self.b, self.cfg
        tpc = cfg.tpc
        s_t, s_v, s_sc = self.s_t, self.s_v, self.s_sc
        bs = self.bsem
        s_r1 = b.sem("s_r1")
        b.W("sync", s_t, self.T_MOE[1])
        b.I("sync", lambda e: e.dma_start(out=self.rt_sb[:, :],
                                          in_=self.dr["w_rt1"][:, :]), s_r1, k=16)
        self.R1N = s_r1.n
        self.s_r1 = s_r1
        tokk = self.wq.load(self.wkv[:, 0:512], self.dr["w_wk"].ap(),
                            pre_wait=self.T_MOE)
        tokv = self.wq.load(self.wkv[:, 512:1024], self.dr["w_wv"].ap())
        self.WKV = (tokk, tokv)
        b.W("gpsimd", s_t, self.T_MOE[1])
        b.E("gpsimd", lambda g: g.memset(self.onescb[:, :], 1.0))
        b.F("gpsimd", self.s_g)
        self.G_OCB = self.s_g.n
        for tb in range(cfg.nblk):
            c0 = 512 * tb
            self.bwait(2)
            self.bwait(3)
            for dc in range(DCH):
                xr = self.xres[:, dc * tpc + c0:dc * tpc + c0 + 512]
                b.W("scalar", s_v, self.V_MOE)
                if dc > 0:
                    b.W("scalar", s_t, s_t.n)
                b.I("scalar", lambda s, x=xr: s.activation(
                    self.sqb[:, 0:512], x, AF.Square), s_sc)
                self.MMg(self.pb[2][0:1, 0:512], self.onesc[:, :], xr,
                         start=(dc == 0), stop=(dc == DCH - 1),
                         waits=(((s_v, self.V_MOE),) if dc == 0 and tb == 0 else ()))
                b.W("tensor", s_sc, s_sc.n)
                self.MMg(self.pb[3][0:1, 0:512], self.onesc[:, :],
                         self.sqb[:, 0:512], start=(dc == 0), stop=(dc == DCH - 1))
            tss = s_t.n
            m_ap = self.arsl[0:1, c0:c0 + 512]
            b.W("vector", s_t, tss)
            b.I("vector", lambda v, m=m_ap: v.tensor_scalar_mul(
                m, self.pb[2][0:1, 0:512], -1.0 / D), bs[2])
            b.D("vector")
            b.I("vector", lambda v: v.tensor_scalar_mul(
                self.l2row[0:1, 0:512], self.pb[3][0:1, 0:512], 1.0 / D), bs[3])
            b.D("vector")
            b.E("vector", lambda v, m=m_ap: v.tensor_mul(
                self.sqtmp[0:1, 0:512], m, m))
            b.D("vector")
            b.E("vector", lambda v: v.tensor_sub(
                self.l2row[0:1, 0:512], self.l2row[0:1, 0:512],
                self.sqtmp[0:1, 0:512]))
            b.F("vector", s_v)
            b.W("scalar", s_v, s_v.n)
            b.I("scalar", lambda s: s.activation(
                self.l2row[0:1, 0:512], self.l2row[0:1, 0:512], AF.Sqrt,
                bias=self.cst(EPS_LN, 1)), s_sc)
            b.W("vector", s_sc, s_sc.n)
            b.E("vector", lambda v: v.reciprocal(self.l2row[0:1, 0:512],
                                                 self.l2row[0:1, 0:512]))
            b.F("vector", s_v)
            vrs = s_v.n
            self.bwait(4)
            self.MMg(self.pb[4][32:33, 0:512], self.onesall[0:1, 0:1],
                     self.l2row[0:1, 0:512], tp=(0, 32), waits=((s_v, vrs),))
            b.W("vector", s_t, s_t.n)
            b.I("vector", lambda v, c0=c0: v.tensor_copy(
                self.arsl[32:33, c0:c0 + 512], self.pb[4][32:33, 0:512]), bs[4])
            b.F("vector", s_v)
        self.V_SLIVERS = self.s_v.n
        wrt = self.rt_sb[:, 0:DCH * 128]
        self.bwait(2)
        b.W("tensor", self.s_r1, self.R1N)
        for dc in range(DCH):
            self.MMg(self.pb[2][0:16, 0:1], wrt[:, 128 * dc:128 * dc + 16],
                     self.onesc[:, :], start=(dc == 0), stop=(dc == DCH - 1))
        b.W("vector", s_t, s_t.n)
        b.I("vector", lambda v: v.tensor_copy(self.cs16[:, 0:1],
                                              self.pb[2][0:16, 0:1]), bs[2])
        b.F("vector", s_v)
        vcs = s_v.n
        for tb in range(cfg.nblk):
            c0 = 512 * tb
            self.bwait(2)
            for dc in range(DCH):
                self.MMg(self.pb[2][0:16, 0:512], wrt[:, 128 * dc:128 * dc + 16],
                         self.xres[:, dc * tpc + c0: dc * tpc + c0 + 512],
                         start=(dc == 0), stop=(dc == DCH - 1))
            self.bwait(3)
            self.MMg(self.pb[3][0:16, 0:512], self.onesall[0:1, 0:16],
                     self.arsl[0:1, c0:c0 + 512],
                     waits=((s_v, self.V_SLIVERS),))
            self.bwait(4)
            self.MMg(self.pb[4][0:16, 0:512], self.onesall[32:33, 0:16],
                     self.arsl[32:33, c0:c0 + 512])
            tn = s_t.n
            b.W("vector", s_t, tn)
            b.W("vector", s_v, vcs)
            b.E("vector", lambda v: v.tensor_copy(self.sqtmp[0:16, 0:512],
                                                  self.pb[2][0:16, 0:512]))
            b.I("vector", lambda v: v.drain(), bs[2])
            b.E("vector", lambda v: v.scalar_tensor_tensor(
                self.sqtmp[0:16, 0:512], self.pb[3][0:16, 0:512], self.cs16[0:16, 0:1],
                self.sqtmp[0:16, 0:512], ALU.mult, ALU.add))
            b.I("vector", lambda v: v.drain(), bs[3])
            b.I("vector", lambda v: v.tensor_tensor(
                self.sqb[0:16, 0:512], self.sqtmp[0:16, 0:512],
                self.pb[4][0:16, 0:512], ALU.mult), bs[4])
            b.F("vector", s_v)
            g_ap = self.ar16[96:112, c0:c0 + 512]
            b.W("scalar", s_v, s_v.n)
            b.I("scalar", lambda s, g=g_ap: s.activation(
                g, self.sqb[0:16, 0:512], AF.Sigmoid), s_sc)
        b.F("scalar", s_sc)
        self.SC1 = s_sc.n
        for tb in range(cfg.nblk):
            self.xn_block(tb, dst=self.arn(4096 * tb, 4096))
        self.V_XN2 = self.s_v.n
        kfb = self.arn(16384, 512)
        vtb = self.arn(16896, 512)
        b.W("tensor", self.s_g, self.G_OCB)
        b.W("vector", s_sc, s_sc.n)
        b.E("vector", lambda v: v.memset(self.kvz[:, :], 0.0))
        b.F("vector", s_v)
        for tb in range(cfg.nblk):
            for j in range(8):
                xp_ap = self.arn(4096 * tb + 512 * j, 512)
                self.bwait(0)
                b.W("tensor", self.WKV[0][0], self.WKV[0][1])
                b.W("tensor", self.WKV[1][0], self.WKV[1][1])
                b.W("tensor", s_v, self.V_XN2)
                b.I("tensor", lambda t, j=j, x=xp_ap: t.matmul(
                    self.pb[0][0:64, 0:512], self.wkv[0:64, 64 * j:64 * (j + 1)],
                    x[0:64, :], start=True, stop=True, tile_position=(0, 0),
                    skip_group_check=True), s_t)
                b.I("tensor", lambda t, j=j, x=xp_ap: t.matmul(
                    self.pb[0][64:128, 0:512], self.wkv[64:128, 64 * j:64 * (j + 1)],
                    x[64:128, :], start=True, stop=True, tile_position=(64, 64),
                    skip_group_check=True), s_t)
                self.bwait(1)
                b.I("tensor", lambda t, j=j, x=xp_ap: t.matmul(
                    self.pb[1][0:64, 0:512], self.wkv[0:64, 512 + 64 * j:512 + 64 * (j + 1)],
                    x[0:64, :], start=True, stop=True, tile_position=(0, 0),
                    skip_group_check=True), s_t)
                b.I("tensor", lambda t, j=j, x=xp_ap: t.matmul(
                    self.pb[1][64:128, 0:512], self.wkv[64:128, 512 + 64 * j:512 + 64 * (j + 1)],
                    x[64:128, :], start=True, stop=True, tile_position=(64, 64),
                    skip_group_check=True), s_t)
                tkv = s_t.n
                b.W("vector", s_t, tkv)
                b.W("vector", s_sc, s_sc.n)
                b.E("vector", lambda v: v.tensor_scalar(
                    self.sqtmp[:, :], self.pb[0][:, 0:512], 0.0, 0.0,
                    ALU.min, ALU.add))
                b.D("vector")
                b.I("vector", lambda v, d=vtb: v.tensor_copy(d, self.pb[1][:, 0:512]),
                    bs[1])
                b.F("vector", s_v)
                vmin = s_v.n
                b.W("scalar", s_v, vmin)
                b.I("scalar", lambda s: s.activation(
                    self.sqtmp[:, :], self.sqtmp[:, :], AF.Exp), s_sc)
                b.D("scalar")
                b.I("scalar", lambda s: s.activation(
                    self.cext[:, 0:512], self.pb[0][:, 0:512], AF.Relu), s_sc)
                b.I("scalar", lambda s: s.drain(), bs[0])
                b.W("vector", s_sc, s_sc.n)
                b.I("vector", lambda v, d=kfb: v.tensor_tensor(
                    d, self.sqtmp[:, :], self.cext[:, 0:512], ALU.add), s_v)
                b.F("vector", s_v)
                vkf = s_v.n
                for st in range(4):
                    ktm = self.arn(17408 + 128 * (st % 2), 128)
                    vtm = self.arn(17664 + 128 * (st % 2), 128)
                    self.bwait(6)
                    t6 = self.TRg(self.pb[6][:, 0:128], kfb[:, 128 * st:128 * (st + 1)],
                                  self.ident[:, :], waits=((s_v, vkf),))
                    self.bwait(7)
                    t7 = self.TRg(self.pb[7][:, 0:128], vtb[:, 128 * st:128 * (st + 1)],
                                  self.ident[:, :])
                    b.W("vector", s_t, t7)
                    b.I("vector", lambda v, d=ktm: v.tensor_copy(d, self.pb[6][:, 0:128]),
                        bs[6])
                    b.D("vector")
                    b.I("vector", lambda v, d=vtm: v.tensor_copy(d, self.pb[7][:, 0:128]),
                        bs[7])
                    b.F("vector", s_v)
                    vst = s_v.n
                    if st == 0:
                        self.bwait(4)
                        self.bwait(5)
                    b.W("tensor", s_v, vst)
                    b.I("tensor", lambda t, j=j, ktm=ktm, vtm=vtm, st=st:
                        t.matmul(self.pb[5][0:64, 64 * j:64 * (j + 1)], ktm[:, 0:64],
                                 vtm[:, 0:64], start=(st == 0), stop=(st == 3),
                                 tile_position=(0, 0), skip_group_check=True), s_t)
                    b.I("tensor", lambda t, j=j, ktm=ktm, vtm=vtm, st=st:
                        t.matmul(self.pb[5][64:128, 64 * j:64 * (j + 1)], ktm[:, 64:128],
                                 vtm[:, 64:128], start=(st == 0), stop=(st == 3),
                                 tile_position=(0, 64), skip_group_check=True), s_t)
                    b.I("tensor", lambda t, j=j, ktm=ktm, st=st:
                        t.matmul(self.pb[4][0:64, j:j + 1], ktm[:, 0:64],
                                 self.onescb[:, :], start=(st == 0), stop=(st == 3),
                                 tile_position=(0, 0), skip_group_check=True), s_t)
                    b.I("tensor", lambda t, j=j, ktm=ktm, st=st:
                        t.matmul(self.pb[4][64:128, j:j + 1], ktm[:, 64:128],
                                 self.onescb[:, :], start=(st == 0), stop=(st == 3),
                                 tile_position=(0, 64), skip_group_check=True), s_t)
                tacc = s_t.n
                b.W("vector", s_t, tacc)
                b.E("vector", lambda v, j=j: v.tensor_tensor(
                    self.kvz[:, 64 * j:64 * (j + 1)], self.kvz[:, 64 * j:64 * (j + 1)],
                    self.pb[5][:, 64 * j:64 * (j + 1)], ALU.add))
                b.I("vector", lambda v: v.drain(), bs[5])
                b.E("vector", lambda v, j=j: v.tensor_tensor(
                    self.kvz[:, 512 + j:513 + j], self.kvz[:, 512 + j:513 + j],
                    self.pb[4][:, j:j + 1], ALU.add))
                b.I("vector", lambda v: v.drain(), bs[4])
                b.F("vector", s_v)
        self.V_KVL = self.s_v.n

    def ph8_ag2(self):
        b, cfg = self.b, self.cfg
        cc1_in, cc1_out = self.ccs[2], self.ccs[3]
        s_t, s_v, s_sc = self.s_t, self.s_v, self.s_sc
        bs = self.bsem
        s_ag = b.sem("s_ag2")
        s_agl = b.sem("s_agl2")
        b.W("sync", s_v, self.V_KVL)
        b.I("sync", lambda e: e.dma_start(out=cc1_in.ap()[:, :], in_=self.kvz[:, :]),
            s_ag, k=16)
        b.W("gpsimd", s_ag, s_ag.n)
        b.I("gpsimd", lambda g: g.collective_compute(
            "AllGather", ALU.bypass, replica_groups=[list(range(NCORE))],
            ins=[cc1_in.ap().opt()], outs=[cc1_out.ap().opt()]), self.s_cc)
        CC1 = self.s_cc.n
        b.W("vector", s_ag, s_ag.n)
        b.E("vector", lambda v: v.memset(self.kvz[:, :], 0.0))
        b.F("vector", s_v)
        b.W("sync", self.s_cc, CC1)
        for l in range(NCORE):
            b.W("sync", s_v, s_v.n)
            b.I("sync", lambda e, l=l: e.dma_start(
                out=self.kvtmp, in_=cc1_out.ap()[128 * l:128 * (l + 1), :]),
                s_agl, k=16)
            b.W("vector", s_agl, s_agl.n)
            b.E("vector", lambda v, l=l: v.scalar_tensor_tensor(
                self.kvz[:, :], self.kvtmp, self.nubc[:, l:l + 1], self.kvz[:, :],
                ALU.mult, ALU.add))
            b.F("vector", s_v)
        b.E("vector", lambda v: v.tensor_copy(self.kvzb[:, :], self.kvz[:, :]))
        b.D("vector")
        b.E("vector", lambda v: v.memset(self.zbd[:, :], 0.0))
        b.D("vector")
        for j in range(8):
            b.E("vector", lambda v, j=j: v.tensor_copy(
                self.zbd[0:64, 2 * j:2 * j + 1], self.kvzb[0:64, 512 + j:513 + j]))
            b.D("vector")
            b.E("vector", lambda v, j=j: v.tensor_copy(
                self.zbd[64:128, 2 * j + 1:2 * j + 2], self.kvzb[64:128, 512 + j:513 + j]))
            b.D("vector")
        b.F("vector", s_v)
        self.bwait(0)
        self.MMg(self.pb[0][:, 0:16], self.onesall[0:1, 0:128], self.ebrow[0:1, 0:16],
                 waits=((s_v, self.V_NEGAL),))
        b.W("vector", s_t, s_t.n)
        b.I("vector", lambda v: v.tensor_copy(self.ebb[:, 0:16], self.pb[0][:, 0:16]),
            bs[0])
        b.F("vector", s_v)
        self.V_KVZ = s_v.n

    def ph9_l1_blocks(self):
        b, cfg = self.b, self.cfg
        tpc = cfg.tpc
        s_t, s_v, s_sc = self.s_t, self.s_v, self.s_sc
        bs = self.bsem
        b.W("tensor", s_sc, self.SC1)
        b.W("vector", s_v, self.V_KVZ)
        self.route_tiles(96, E1, 0, use_bias=True, probs=False)
        b.F("tensor", s_t)
        wsl = WS(self, [self.arn(36864 + 1024 * i, 1024) for i in range(6)])
        wsl.barrier((s_t, s_t.n))
        tokq = self.wq.load(self.wq_sb[:, :], self.dr["w_wq"].ap(),
                            pre_wait=(s_t, s_t.n))
        self.WQTOK = tokq
        qfb = self.arn(43520, 512)
        ctx = lambda j: self.arn(16384 + 512 * j, 512)
        for tb in range(cfg.nblk):
            c0 = 512 * tb
            for j in range(8):
                xp_ap = self.arn(4096 * tb + 512 * j, 512)
                self.bwait(0)
                b.W("tensor", self.WQTOK[0], self.WQTOK[1])
                b.I("tensor", lambda t, j=j, x=xp_ap: t.matmul(
                    self.pb[0][0:64, 0:512], self.wq_sb[0:64, 64 * j:64 * (j + 1)],
                    x[0:64, :], start=True, stop=True, tile_position=(0, 0),
                    skip_group_check=True), s_t)
                b.I("tensor", lambda t, j=j, x=xp_ap: t.matmul(
                    self.pb[0][64:128, 0:512], self.wq_sb[64:128, 64 * j:64 * (j + 1)],
                    x[64:128, :], start=True, stop=True, tile_position=(64, 64),
                    skip_group_check=True), s_t)
                tq = s_t.n
                b.W("vector", s_t, tq)
                b.W("vector", s_sc, s_sc.n)
                b.E("vector", lambda v: v.tensor_scalar(
                    self.sqtmp[:, :], self.pb[0][:, 0:512], 0.0, 0.0,
                    ALU.min, ALU.add))
                b.F("vector", s_v)
                vmin = s_v.n
                b.W("scalar", s_v, vmin)
                b.I("scalar", lambda s: s.activation(
                    self.sqtmp[:, :], self.sqtmp[:, :], AF.Exp), s_sc)
                b.D("scalar")
                b.I("scalar", lambda s: s.activation(
                    self.cext[:, 0:512], self.pb[0][:, 0:512], AF.Relu), s_sc)
                b.I("scalar", lambda s: s.drain(), bs[0])
                b.W("vector", s_sc, s_sc.n)
                b.I("vector", lambda v, d=qfb: v.tensor_tensor(
                    d, self.sqtmp[:, :], self.cext[:, 0:512], ALU.add), s_v)
                b.F("vector", s_v)
                vqf = s_v.n
                self.bwait(1)
                b.W("tensor", s_v, vqf)
                b.I("tensor", lambda t, j=j: t.matmul(
                    self.pb[1][0:64, 0:512], self.kvzb[0:64, 64 * j:64 * (j + 1)],
                    qfb[0:64, :], start=True, stop=True, tile_position=(0, 0),
                    skip_group_check=True), s_t)
                b.I("tensor", lambda t, j=j: t.matmul(
                    self.pb[1][64:128, 0:512], self.kvzb[64:128, 64 * j:64 * (j + 1)],
                    qfb[64:128, :], start=True, stop=True, tile_position=(64, 64),
                    skip_group_check=True), s_t)
                self.bwait(2)
                b.I("tensor", lambda t, j=j: t.matmul(
                    self.pb[2][0:2, 0:512], self.zbd[:, 2 * j:2 * j + 2],
                    qfb[:, :], start=True, stop=True, skip_group_check=True), s_t)
                tden = s_t.n
                b.W("scalar", s_t, tden)
                b.W("scalar", s_v, s_v.n)
                b.I("scalar", lambda s: s.activation(
                    self.l2row[0:2, 0:512], self.pb[2][0:2, 0:512], AF.Abs,
                    bias=self.cst(EPS_DEN, 2)), s_sc)
                b.I("scalar", lambda s: s.drain(), bs[2])
                b.W("vector", s_sc, s_sc.n)
                b.E("vector", lambda v: v.reciprocal(self.l2row[0:2, 0:512],
                                                     self.l2row[0:2, 0:512]))
                b.F("vector", s_v)
                vden = s_v.n
                self.bwait(3)
                self.MMg(self.pb[3][:, 0:512], self.ones2T[:, :],
                         self.l2row[0:2, 0:512], waits=((s_v, vden),))
                tbc = s_t.n
                b.W("vector", s_t, tbc)
                b.E("vector", lambda v: v.tensor_copy(
                    self.sqtmp[:, :], self.pb[1][:, 0:512]))
                b.I("vector", lambda v: v.drain(), bs[1])
                b.I("vector", lambda v, d=ctx(j): v.tensor_tensor(
                    d, self.sqtmp[:, :], self.pb[3][:, 0:512], ALU.mult), bs[3])
                b.F("vector", s_v)
            VCTX = s_v.n
            img = self.dr["w_wo"]
            for oc in range(DCH):
                j2, buf, tok = wsl.get(img[:, oc * DCH * 128:(oc + 1) * DCH * 128])
                bank = 4 + oc % 2
                b.W("tensor", tok[0], tok[1])
                b.W("tensor", s_v, VCTX)
                self.bwait(bank)
                for dcp in range(DCH):
                    b.I("tensor", lambda t, buf=buf, dcp=dcp, bank=bank: t.matmul(
                        self.pb[bank][:, 0:512], buf[:, dcp * 128:(dcp + 1) * 128],
                        self.arn(16384 + 512 * dcp, 512),
                        start=(dcp == 0), stop=(dcp == DCH - 1),
                        skip_group_check=True), s_t)
                wsl.release(j2)
                two = s_t.n
                xr_ap = self.xres[:, oc * tpc + c0:oc * tpc + c0 + 512]
                b.W("vector", s_t, two)
                b.D("vector")
                b.I("vector", lambda v, x=xr_ap, bank=bank: v.tensor_tensor(
                    x, x, self.pb[bank][:, 0:512], ALU.add), bs[bank])
            b.F("vector", s_v)
        s1i, s2i, s3i = self.dr["w_s1"], self.dr["w_s2"], self.dr["w_s3"]
        for q4 in range(4):
            self.moe_pass(wsl, lambda oc8, q4=q4: s1i[:, (8 * q4 + oc8) * 1024:
                                                      (8 * q4 + oc8 + 1) * 1024],
                          lambda oc8, q4=q4: s2i[:, (8 * q4 + oc8) * 1024:
                                                  (8 * q4 + oc8 + 1) * 1024],
                          lambda oc2, q4=q4: s3i[:, oc2 * 4096 + q4 * 1024:
                                                  oc2 * 4096 + q4 * 1024 + 1024],
                          webc=False)
        r1i, r2i, r3i = self.dr["w_r1"], self.dr["w_r2"], self.dr["w_r3"]
        for e in range(E1):
            self.webc_build(e, E1, 0)
            self.moe_pass(wsl, lambda oc8, e=e: r1i.ap()[e][:, oc8 * 1024:(oc8 + 1) * 1024],
                          lambda oc8, e=e: r2i.ap()[e][:, oc8 * 1024:(oc8 + 1) * 1024],
                          lambda oc2, e=e: r3i.ap()[e][:, oc2 * 1024:(oc2 + 1) * 1024],
                          webc=True)
        b.F("vector", s_v)
        self.V_L1 = s_v.n

    def moe_pass(self, wsl, f1, f2, f3, webc):
        b, cfg = self.b, self.cfg
        tpc = cfg.tpc
        s_t, s_v, s_sc = self.s_t, self.s_v, self.s_sc
        bs = self.bsem
        for oc8 in range(8):
            bk1 = 2 * (oc8 % 2)
            bk2 = 1 + 2 * (oc8 % 2)
            j1, b1, tk1 = wsl.get(f1(oc8))
            j2, b2, tk2 = wsl.get(f2(oc8))
            b.W("tensor", tk1[0], tk1[1])
            b.W("tensor", tk2[0], tk2[1])
            for tb in range(cfg.nblk):
                self.bwait(bk1)
                self.bwait(bk2)
                for dc in range(DCH):
                    b.I("tensor", lambda t, b1=b1, dc=dc, tb=tb, bk1=bk1: t.matmul(
                        self.pb[bk1][:, 0:512], b1[:, dc * 128:(dc + 1) * 128],
                        self.arn(4096 * tb + 512 * dc, 512),
                        start=(dc == 0), stop=(dc == DCH - 1),
                        skip_group_check=True), s_t)
                for dc in range(DCH):
                    b.I("tensor", lambda t, b2=b2, dc=dc, tb=tb, bk2=bk2: t.matmul(
                        self.pb[bk2][:, 0:512], b2[:, dc * 128:(dc + 1) * 128],
                        self.arn(4096 * tb + 512 * dc, 512),
                        start=(dc == 0), stop=(dc == DCH - 1),
                        skip_group_check=True), s_t)
                th = s_t.n
                b.W("scalar", s_t, th)
                b.W("scalar", s_v, s_v.n)
                b.I("scalar", lambda s, bk1=bk1: s.activation(
                    self.sqtmp[:, :], self.pb[bk1][:, 0:512], AF.Silu), s_sc)
                b.I("scalar", lambda s: s.drain(), bs[bk1])
                b.W("vector", s_sc, s_sc.n)
                b.I("vector", lambda v, tb=tb, oc8=oc8, bk2=bk2: v.tensor_tensor(
                    self.arn(20480 + (8 * tb + oc8) * 512, 512), self.sqtmp[:, :],
                    self.pb[bk2][:, 0:512], ALU.mult), bs[bk2])
                b.F("vector", s_v)
            wsl.release(j1)
            wsl.release(j2)
        VHID = s_v.n
        for oc2 in range(DCH):
            j3, b3, tk3 = wsl.get(f3(oc2))
            b.W("tensor", tk3[0], tk3[1])
            b.W("tensor", s_v, VHID)
            for tb in range(cfg.nblk):
                bank = 4 + (oc2 + tb) % 2
                self.bwait(bank)
                for oc8 in range(8):
                    b.I("tensor", lambda t, b3=b3, oc8=oc8, tb=tb, bank=bank: t.matmul(
                        self.pb[bank][:, 0:512], b3[:, oc8 * 128:(oc8 + 1) * 128],
                        self.arn(20480 + (8 * tb + oc8) * 512, 512),
                        start=(oc8 == 0), stop=(oc8 == 7),
                        skip_group_check=True), s_t)
                tdn = s_t.n
                c0 = 512 * tb
                xr_ap = self.xres[:, oc2 * tpc + c0:oc2 * tpc + c0 + 512]
                b.W("vector", s_t, tdn)
                if webc:
                    b.W("vector", s_sc, s_sc.n)
                    b.E("vector", lambda v, tb=tb, bank=bank: v.tensor_tensor(
                        self.sqb[:, 0:512], self.pb[bank][:, 0:512],
                        self.webc_v(tb), ALU.mult))
                    b.I("vector", lambda v, bank=bank: v.drain(), bs[bank])
                    b.I("vector", lambda v, x=xr_ap: v.tensor_add(
                        x, x, self.sqb[:, 0:512]), self.s_fin)
                else:
                    b.D("vector")
                    b.I("vector", lambda v, x=xr_ap, bank=bank: v.tensor_tensor(
                        x, x, self.pb[bank][:, 0:512], ALU.add), bs[bank])
                b.F("vector", s_v)
            wsl.release(j3)

    def dump(self, name, ap, width, dtype, parts=128):
        if name not in self.dbg:
            return
        b = self.b
        if not hasattr(self, "s_dbg"):
            self.s_dbg = b.sem("s_dbg")
        d = self.nc.dram_tensor("dbg_" + name, [parts, width], dtype,
                                kind="ExternalOutput")
        b.F("vector", self.s_v)
        b.F("scalar", self.s_sc)
        b.W("sync", self.s_v, self.s_v.n)
        b.W("sync", self.s_sc, self.s_sc.n)
        b.I("sync", lambda e, d=d, ap=ap: e.dma_start(out=d.ap(), in_=ap),
            self.s_dbg, k=16)
        b.W("vector", self.s_dbg, self.s_dbg.n)
        b.W("scalar", self.s_dbg, self.s_dbg.n)
        b.W("gpsimd", self.s_dbg, self.s_dbg.n)

    def ph_out(self):
        b, cfg = self.b, self.cfg
        tpc, ntile = cfg.tpc, cfg.ntile
        s_t, s_v = self.s_t, self.s_v
        s_out = [b.sem("out0"), b.sem("out1")]
        s_oq = b.sem("oq")
        b.F("vector", s_v)
        XF = s_v.n
        b.W("tensor", s_v, XF)
        buffree = [None, None]
        for i in range(ntile):
            jb = i % 2
            buf = self.xtm[jb]
            for dc in range(DCH):
                bank = dc % 2
                self.bwait(bank)
                b.I("tensor", lambda t, i=i, dc=dc, bank=bank: t.transpose(
                    self.pb[bank][:, 0:128],
                    self.xres[:, dc * tpc + 128 * i:dc * tpc + 128 * (i + 1)],
                    self.identf[:, :]), s_t)
                tn = s_t.n
                b.W("vector", s_t, tn)
                if dc < 2 and buffree[jb] is not None:
                    b.W("vector", buffree[jb][0], buffree[jb][1])
                b.I("vector", lambda v, buf=buf, dc=dc, bank=bank: v.tensor_copy(
                    buf[:, 128 * dc:128 * (dc + 1)], self.pb[bank][:, 0:128]),
                    self.bsem[bank])
            b.F("vector", s_v)
            vst = s_v.n
            b.W("sync", s_v, vst)
            b.I("sync", lambda e, i=i, buf=buf: e.dma_start(
                out=self.y.ap()[128 * i:128 * (i + 1), :], in_=buf[:, :]),
                s_out[jb], k=16)
            buffree[jb] = (s_out[jb], s_out[jb].n)
        b.W("vector", s_out[0], s_out[0].n)
        b.W("vector", s_out[1], s_out[1].n)
        if hasattr(self, "s_dbg"):
            b.W("vector", self.s_dbg, self.s_dbg.n)
        b.E("vector", lambda v: v.memset(self.onrow[:, :], 0.0))


def _img_lhsT(w):
    din, dout = w.shape
    ndc, noc = (din + 127) // 128, (dout + 127) // 128
    padr = ndc * 128 - din
    padc = noc * 128 - dout
    if padr or padc:
        w2 = np.zeros((ndc * 128, noc * 128), w.dtype)
        w2[:din, :dout] = w
        w = w2
    img = w.reshape(ndc, 128, noc, 128).transpose(1, 2, 0, 3).reshape(128, noc * ndc * 128)
    return np.ascontiguousarray(img)


def prep_weights(I):
    W = {}
    W["w_norm"] = np.ascontiguousarray(np.asarray(I["L0_attn_norm_w"]).reshape(DCH, 128).T)
    W["w_pnw"] = np.ascontiguousarray(np.asarray(I["L0_attn_post_norm_w"]).reshape(DCH, 128).T)
    W["w_q"] = _img_lhsT(I["L0_attn_q_w"])
    W["w_k"] = _img_lhsT(I["L0_attn_k_w"])
    W["w_v"] = _img_lhsT(I["L0_attn_v_w"])
    ba = np.concatenate([I["L0_attn_b_w"], I["L0_attn_a_w"]], 1)
    W["w_ba"] = _img_lhsT(ba)
    conv = np.concatenate([I["L0_attn_conv_q_w"], I["L0_attn_conv_k_w"],
                           I["L0_attn_conv_v_w"]], 0)
    W["w_conv"] = np.ascontiguousarray(
        conv.reshape(-1, 128, 4).transpose(1, 0, 2).reshape(128, -1))
    W["w_Alog"] = np.ascontiguousarray(np.asarray(I["L0_attn_A_log"])[:, None])
    W["w_dtb"] = np.ascontiguousarray(np.asarray(I["L0_attn_dt_bias"])[:, None])
    W["w_g"] = _img_lhsT(I["L0_attn_g_w"])
    W["w_onw"] = np.ascontiguousarray(np.asarray(I["L0_attn_o_norm_w"])[:, None])
    W["w_o"] = _img_lhsT(I["L0_attn_o_w"])
    W["w_gate"] = _img_lhsT(I["L0_attn_gate_w"])
    W["w_up"] = _img_lhsT(I["L0_attn_up_w"])
    W["w_down"] = _img_lhsT(I["L0_attn_down_w"])
    W["w_rt0"] = _img_lhsT(np.asarray(I["L0_ffn_Wg"]).T)
    W["w_m1"] = np.stack([_img_lhsT(np.asarray(I["L0_ffn_W1"])[e].T) for e in range(E0)])
    W["w_m2"] = np.stack([_img_lhsT(np.asarray(I["L0_ffn_W2"])[e].T) for e in range(E0)])
    W["w_m3"] = np.stack([_img_lhsT(np.asarray(I["L0_ffn_W3"])[e].T) for e in range(E0)])
    for nm, key in (("w_wq", "L1_attn_Wq"), ("w_wk", "L1_attn_Wk"), ("w_wv", "L1_attn_Wv")):
        w = np.asarray(I[key])
        img = np.zeros((128, (NH // 2) * DKH), np.float32)
        for j in range(NH // 2):
            img[0:64, j * 64:(j + 1) * 64] = w[2 * j]
            img[64:128, j * 64:(j + 1) * 64] = w[2 * j + 1]
        W[nm] = img
    W["w_wo"] = _img_lhsT(I["L1_attn_Wo"])
    W["w_rt1"] = _img_lhsT(I["L1_ffn_router_w"])
    W["w_eb"] = np.ascontiguousarray(np.asarray(I["L1_ffn_expert_bias"])[None, :])
    W["w_s1"] = _img_lhsT(np.asarray(I["L1_ffn_shared_w1"])[0])
    W["w_s2"] = _img_lhsT(np.asarray(I["L1_ffn_shared_w2"])[0])
    W["w_s3"] = _img_lhsT(np.asarray(I["L1_ffn_shared_w3"])[0])
    W["w_r1"] = np.stack([_img_lhsT(np.asarray(I["L1_ffn_routed_w1"])[e]) for e in range(E1)])
    W["w_r2"] = np.stack([_img_lhsT(np.asarray(I["L1_ffn_routed_w2"])[e]) for e in range(E1)])
    W["w_r3"] = np.stack([_img_lhsT(np.asarray(I["L1_ffn_routed_w3"])[e]) for e in range(E1)])
    return {k: np.ascontiguousarray(np.asarray(v, np.float32)) for k, v in W.items()}


def make_inmaps(inputs, cfg):
    x = np.asarray(inputs["L0_attn_hidden_states"])
    Bb, T, _ = x.shape
    nsc = NCORE // Bb
    W = prep_weights(inputs)
    in_maps = []
    for c in range(NCORE):
        seq, pos = c // nsc, c % nsc
        s = pos * cfg.tpc
        xs = np.ascontiguousarray(x[seq, s:s + cfg.tpc])
        xp = np.zeros((4, D), np.float32)
        if pos > 0:
            xp[0:3] = x[seq, s - 3:s]
        mu = np.zeros(NCORE, np.float32)
        nu = np.zeros(NCORE, np.float32)
        for l in range(NCORE):
            if l // nsc == seq:
                nu[l] = 1.0
                if l % nsc < pos:
                    mu[l] = 1.0
        cm = np.stack([mu, nu]).astype(np.float32)
        m = {"x": xs, "xp": xp, "cmask": cm}
        m.update(W)
        in_maps.append(m)
    return in_maps


_CFG = Cfg(2048)
_NC_CACHE = {}


def kernel(**inputs):
    cfg = _CFG
    in_maps = make_inmaps(inputs, cfg)
    key = cfg.tpc
    if key not in _NC_CACHE:
        _NC_CACHE[key] = build(cfg)
    nc = _NC_CACHE[key]
    res = run_bass_kernel_spmd(nc, in_maps, core_ids=list(range(NCORE)))
    x = np.asarray(inputs["L0_attn_hidden_states"])
    Bb, T, _ = x.shape
    nsc = NCORE // Bb
    out = np.empty((Bb, T, D), np.float32)
    for c in range(NCORE):
        seq, pos = c // nsc, c % nsc
        out[seq, pos * cfg.tpc:(pos + 1) * cfg.tpc] = res.results[c]["y"]
    return out
